# version 07
# baseline (speedup 1.0000x reference)
import os
import time
import numpy as np
import ml_dtypes
import concourse.bass as bass
import concourse.mybir as mybir
from concourse.ap import AP
from concourse.bass_utils import run_bass_kernel_spmd
from concourse.bass_core_rust import IndirectOffsetOnAxis

F32 = mybir.dt.float32
BF16 = mybir.dt.bfloat16
I32 = mybir.dt.int32
U32 = mybir.dt.uint32
AF = mybir.ActivationFunctionType
ALU = mybir.AluOpType
AX = mybir.AxisListType

BF = ml_dtypes.bfloat16

N_CORES = 8
P = 2048
C = 1024
NCH = C // 128
HT = 1024
LNEPS = 1e-5
APH = int(os.environ.get("APH", "99"))
SUB6 = int(os.environ.get("SUB6", "9"))


def rep_mid(ap2d, n):
    b = ap2d.ap
    return AP(ap2d.tensor, ap2d.offset, [list(b[0]), [0, n], list(b[1])])


def bcast_inner(ap2d, n):
    b = ap2d.ap
    return AP(ap2d.tensor, ap2d.offset, [list(b[0]), list(b[1]), [0, n]])


class Ctr:
    def __init__(self, h):
        self.h = h
        self.n = 0

    def inc(self, inst, k=1):
        inst.then_inc(self.h, k)
        self.n += k
        return inst

    def wait(self, eng, n=None):
        eng.wait_ge(self.h, self.n if n is None else n)


class Alloc:
    def __init__(self, nc):
        self.nc = nc
        self.ctx = []

    def sb(self, name, shape, dt):
        t = self.nc.sbuf_tensor(name, shape, dt)
        self.ctx.append(t)
        return t.__enter__()

    def ps(self, name, shape, dt):
        t = self.nc.psum_tensor(name, shape, dt)
        self.ctx.append(t)
        return t.__enter__()

    def sem(self, name):
        t = self.nc.semaphore(name)
        self.ctx.append(t)
        return Ctr(t.__enter__())

    def close(self):
        for t in reversed(self.ctx):
            t.__exit__(None, None, None)



def build_launch_A():
    nc = bass.Bass("TRN2", target_bir_lowering=False)

    x_in = nc.dram_tensor("x_in", [P, C], F32, kind="ExternalInput").ap()
    wqkv = nc.dram_tensor("wqkv", [C, 3 * C], F32, kind="ExternalInput").ap()
    wproj = nc.dram_tensor("wproj", [C, C], F32, kind="ExternalInput").ap()
    bproj = nc.dram_tensor("bproj", [1, C], F32, kind="ExternalInput").ap()
    r0w = nc.dram_tensor("r0w", [C, 16], F32, kind="ExternalInput").ap()
    wgs = nc.dram_tensor("wgs", [C, 1024], F32, kind="ExternalInput").ap()
    wus = nc.dram_tensor("wus", [C, 1024], F32, kind="ExternalInput").ap()
    wds = nc.dram_tensor("wds", [1024, C], F32, kind="ExternalInput").ap()
    wgr = nc.dram_tensor("wgr", [C, 4096], F32, kind="ExternalInput").ap()
    wur = nc.dram_tensor("wur", [C, 4096], F32, kind="ExternalInput").ap()
    wdr = nc.dram_tensor("wdr", [4096, C], F32, kind="ExternalInput").ap()
    l1rw = nc.dram_tensor("l1rw", [C, 8], F32, kind="ExternalInput").ap()
    l1rb = nc.dram_tensor("l1rb", [1, 8], F32, kind="ExternalInput").ap()
    trilm = nc.dram_tensor("trilm", [128, 128], F32, kind="ExternalInput").ap()

    x1_out = nc.dram_tensor("x1_out", [NCH, 128, P], F32, kind="ExternalOutput").ap()
    xn1_out = nc.dram_tensor("xn1_out", [NCH, 128, P], BF16,
                             kind="ExternalOutput").ap()
    sc_out = nc.dram_tensor("sc_out", [P, 8], F32, kind="ExternalOutput").ap()

    comb_dram = nc.dram_tensor("comb_dram", [16, HT], F32).ap()

    al = Alloc(nc)
    sb, ps, sem = al.sb, al.ps, al.sem

    XF = sb("XF", [128, NCH * HT], F32)
    XN = sb("XN", [128, NCH * HT], BF16)
    S0 = sb("S0", [128, NCH * HT], F32)
    G0 = sb("G0", [128, NCH * HT], F32)
    QKT = sb("QKT", [128, 16 * 512], BF16)
    AFM = sb("AFM", [128, NCH * 512], BF16)
    VT = sb("VT", [128, 4 * 1024], BF16)
    WS = sb("WS", [128, 8192], BF16)
    SC1 = sb("SC1", [128, 2048], F32)
    SC2 = sb("SC2", [128, 1024], F32)
    SCB = sb("SCB", [128, 1024], BF16)
    SCB2 = sb("SCB2", [128, 512], BF16)
    SCM = sb("SCM", [128, 1024], F32)
    ROWS = sb("ROWS", [1, 3 * 1024], F32)
    ROW0 = sb("ROW0", [1, 1024], F32)
    ONES1 = sb("ONES1", [1, 128], F32)
    ONESC = sb("ONESC", [128, 1], F32)
    IDF = sb("IDF", [128, 128], F32)
    TRIL = sb("TRIL", [128, 128], F32)
    BPROJ = sb("BPROJ", [128, NCH], F32)
    SMAX = sb("SMAX", [128, 16], F32)
    CT = sb("CT", [16, 1024], F32)
    CMB = sb("CMB", [128, 1024], BF16)
    RT = sb("RT", [128, 8 * 16], F32)
    RS = sb("RS", [128, 64], F32)
    ZR16 = sb("ZR16", [128, 16], F32)
    CONSTS = sb("CONSTS", [1, 4], F32)

    PS_ = [ps(f"PSU{i}", [128, 512], F32) for i in range(8)]

    LW = sem("LW")
    LX = sem("LX")
    MM = sem("MM")
    EV = sem("EV")
    EA = sem("EA")
    ST = sem("ST")
    MS = sem("MS"); VA = sem("VA"); VB = sem("VB")
    MT = sem("MT"); VC = sem("VC"); MO = sem("MO"); VD = sem("VD")
    EB = sem("EB"); VM = sem("VM"); EX = sem("EX")
    VP = sem("VP"); EC = sem("EC")
    VE = sem("VE")
    LP = [sem(f"LP{i}") for i in range(7)]

    SUMS = MEAN = MR = ROWS[0:1, 0 * HT:1 * HT]
    SSQ = VAR = ROWS[0:1, 1 * HT:2 * HT]
    RSCR = RSTD = ROWS[0:1, 2 * HT:3 * HT]

    def slab_ap(dram, r0, nr, c0, ncols, row_len):
        assert nr % 128 == 0
        t = dram.tensor
        off = dram.offset + r0 * row_len + c0
        return AP(t, off, [[row_len, 128], [128 * row_len, nr // 128], [1, ncols]])

    def ws_dst(col0, nkc, ncols):
        return AP(WS[:, :].tensor, WS[:, :].offset + col0,
                  [[8192, 128], [ncols, nkc], [1, ncols]])

    with nc.Block() as block:

        @block.sync
        def _(sy):
            LX.inc(sy.dma_start(out=TRIL[:, :], in_=trilm[:, :]), 16)
            LX.inc(sy.dma_start(out=ROW0[0:1, :], in_=bproj[0:1, :]), 16)

        nc.const_aps.register(F32, LNEPS, CONSTS[0:1, 0:1])

        @block.gpsimd
        def _(g):
            g.memset(ONES1[:, :], 1.0)
            g.memset(ONESC[:, :], 1.0)
            g.memset(ZR16[:, :], 0.0)
            g.memset(CONSTS[0:1, 0:1], LNEPS)
            LW.inc(g.memset(IDF[:, :], 1.0))
            g.wait_ge(LW.h, 1)
            LW.inc(g.affine_select(IDF[:, :], IDF[:, :], pattern=[[1, 128]],
                                   compare_op=ALU.is_equal, fill=0.0,
                                   channel_multiplier=-1))

        @block.tensor
        def _(t):
            t.wait_ge(LW.h, 2)
            t.wait_ge(LX.h, 32)
            for k in range(NCH):
                ins = t.transpose(PS_[6][:, k:k + 1],
                                  ROW0[0:1, 128 * k:128 * (k + 1)], IDF[0:1, 0:1])
            MM.inc(ins)

        @block.vector
        def _(v):
            MM.wait(v)
            EV.inc(v.tensor_copy(BPROJ[:, :], PS_[6][:, 0:NCH]))

        def edge(eng, inst):
            VE.inc(inst)
            VE.wait(eng)

        def tensor_barrier(t):
            EV.wait(t)
            EA.wait(t)
            VA.wait(t)
            EB.wait(t)
            VM.wait(t)
            EX.wait(t)
            VP.wait(t)
            EC.wait(t)
            VD.wait(t)

        def emit_half(h):
            t0 = h * HT

            ev0, mm0, lx0 = EV.n, MM.n, LX.n
            for j in range(8):
                slot = j % 2

                @block.sync
                def _(sy, j=j, slot=slot):
                    if j < 2:
                        EA.wait(sy)
                        EV.wait(sy)
                        VP.wait(sy)
                        MT.wait(sy)
                        EX.wait(sy)
                    else:
                        sy.wait_ge(EV.h, ev0 + j - 1)
                    LP[j % 2].inc(sy.dma_start(
                        out=SC1[:, 1024 * slot:1024 * (slot + 1)],
                        in_=x_in[t0 + 128 * j: t0 + 128 * (j + 1), :]), 16)

                @block.tensor
                def _(t, j=j, slot=slot):
                    LP[j % 2].wait(t)
                    t.wait_ge(LW.h, 2)
                    if j == 0:
                        tensor_barrier(t)
                    if j >= 2:
                        t.wait_ge(EV.h, ev0 + j - 1)
                    pb = (j % 2) * 2
                    for k in range(NCH):
                        ins = t.transpose(
                            PS_[pb + k // 4][:, 128 * (k % 4):128 * (k % 4 + 1)],
                            SC1[:, 1024 * slot + 128 * k:1024 * slot + 128 * (k + 1)],
                            IDF[:, :])
                    MM.inc(ins)

                @block.vector
                def _(v, j=j):
                    v.wait_ge(MM.h, mm0 + j + 1)
                    pb = (j % 2) * 2
                    for k in range(NCH):
                        ins = v.tensor_copy(
                            XF[:, HT * k + 128 * j: HT * k + 128 * (j + 1)],
                            PS_[pb + k // 4][:, 128 * (k % 4):128 * (k % 4 + 1)])
                    EV.inc(ins)

            def emit_ln_stats(src, dst_xn, out_dram=None, scores=False):
                for tt in range(2):
                    for grp in range(2):
                        @block.scalar
                        def _(s, tt=tt, grp=grp):
                            EV.wait(s)
                            if grp == 1 or tt == 1:
                                MM.wait(s)
                            for kk in range(4):
                                k = grp * 4 + kk
                                ins = s.activation(
                                    SC1[:, 512 * kk:512 * (kk + 1)],
                                    src[:, HT * k + 512 * tt:HT * k + 512 * tt + 512],
                                    AF.Square)
                            EA.inc(ins)

                        @block.tensor
                        def _(t, tt=tt, grp=grp):
                            EA.wait(t)
                            if tt + grp == 0:
                                EV.wait(t)
                            for kk in range(4):
                                k = grp * 4 + kk
                                t.matmul(PS_[6][0:1, 0:512], ONESC[:, 0:1],
                                         src[:, HT * k + 512 * tt:
                                             HT * k + 512 * tt + 512],
                                         start=(k == 0), stop=(k == 7))
                                ins = t.matmul(PS_[7][0:1, 0:512], ONESC[:, 0:1],
                                               SC1[:, 512 * kk:512 * (kk + 1)],
                                               start=(k == 0), stop=(k == 7))
                            MM.inc(ins)

                    @block.vector
                    def _(v, tt=tt):
                        MM.wait(v)
                        v.tensor_copy(SUMS[0:1, 512 * tt:512 * (tt + 1)],
                                      PS_[6][0:1, 0:512])
                        EV.inc(v.tensor_copy(SSQ[0:1, 512 * tt:512 * (tt + 1)],
                                             PS_[7][0:1, 0:512]))

                @block.vector
                def _(v):
                    EV.wait(v)
                    edge(v, v.tensor_scalar_mul(MEAN, SUMS, 1.0 / C))
                    edge(v, v.tensor_scalar_mul(VAR, SSQ, 1.0 / C))
                    edge(v, v.tensor_mul(RSCR, MEAN, MEAN))
                    EV.inc(v.tensor_sub(VAR, VAR, RSCR))

                @block.scalar
                def _(s):
                    EV.wait(s)
                    s.wait_ge(LW.h, 2)
                    edge(s, s.activation(RSCR, VAR, AF.Ln, bias=LNEPS))
                    EA.inc(s.activation(RSTD, RSCR, AF.Exp, scale=-0.5))

                @block.vector
                def _(v):
                    EA.wait(v)
                    EV.inc(v.tensor_mul(MR, MEAN, RSTD))

                for tt in range(2):
                    @block.tensor
                    def _(t, tt=tt):
                        EV.wait(t)
                        t.matmul(PS_[6][:, 0:512], ONES1[0:1, :],
                                 RSTD[0:1, 512 * tt:512 * (tt + 1)],
                                 start=True, stop=True)
                        MM.inc(t.matmul(PS_[7][:, 0:512], ONES1[0:1, :],
                                        MR[0:1, 512 * tt:512 * (tt + 1)],
                                        start=True, stop=True))

                    @block.vector
                    def _(v, tt=tt):
                        MM.wait(v)
                        for k in range(NCH):
                            sl = slice(HT * k + 512 * tt, HT * k + 512 * tt + 512)
                            v.tensor_mul(SC2[:, 0:512], src[:, sl], PS_[6][:, 0:512])
                            ins = v.tensor_sub(dst_xn[:, sl], SC2[:, 0:512],
                                               PS_[7][:, 0:512])
                        EV.inc(ins)

                if out_dram is not None:
                    @block.sync
                    def _(sy):
                        EV.wait(sy)
                        for k in range(NCH):
                            ST.inc(sy.dma_start(
                                out=out_dram[k, :, t0:t0 + HT],
                                in_=dst_xn[:, HT * k:HT * (k + 1)]), 16)

            def emit_debug_dump():
                @block.sync
                def _(sy):
                    EV.wait(sy)
                    EA.wait(sy)
                    VD.wait(sy)
                    for k in range(NCH):
                        ST.inc(sy.dma_start(out=x1_out[k, :, t0:t0 + HT],
                                            in_=XF[:, HT * k:HT * (k + 1)]), 16)
                    for k in range(NCH):
                        ST.inc(sy.dma_start(out=xn1_out[k, :, t0:t0 + HT],
                                            in_=XN[:, HT * k:HT * (k + 1)]), 16)

            if APH <= 1:
                emit_debug_dump()
                return
            emit_ln_stats(XF, XN)
            if APH <= 2:
                emit_debug_dump()
                return

            def wload(g, ctr, dram_block, ws_col0, nkc, ncols):
                ins = g.dma_start(out=ws_dst(ws_col0, nkc, ncols),
                                  in_=dram_block)
                ctr.inc(ins, 16)
                return ins

            attn_i = {"i": 0}
            for tt in range(2 if APH >= 4 else 0):
                for nh in range(2):
                    @block.gpsimd
                    def _(g, tt=tt, nh=nh):
                        MM.wait(g)
                        wload(g, LP[4],
                              slab_ap(wqkv, 0, C, 2048 + 512 * nh, 512, 3 * C),
                              2048, NCH, 512)

                    for jb in range(4):
                        j = 4 * tt + jb

                        @block.tensor
                        def _(t, nh=nh, j=j, jb=jb):
                            LP[4].wait(t)
                            if nh == 0 and jb == 0:
                                tensor_barrier(t)
                            if nh * 4 + jb >= 2:
                                EV.wait(t, EV.n - 1)
                            bank = jb % 2
                            for kc in range(NCH):
                                ins = t.matmul(
                                    PS_[bank][:, 0:512],
                                    XN[:, HT * kc + 128 * j:HT * kc + 128 * (j + 1)],
                                    WS[:, 2048 + 512 * kc:2048 + 512 * (kc + 1)],
                                    start=(kc == 0), stop=(kc == NCH - 1))
                            MM.inc(ins)

                        @block.vector
                        def _(v, nh=nh, jb=jb):
                            MM.wait(v)
                            bank = jb % 2
                            EV.inc(v.tensor_copy(
                                VT[:, 1024 * jb + 512 * nh:1024 * jb + 512 * (nh + 1)],
                                PS_[bank][:, 0:512]))

                for s16 in range(16 if APH >= 5 else 0):
                    wb = 1024 * (s16 % 2)

                    @block.gpsimd
                    def _(g, s16=s16, wb=wb, tt=tt):
                        MM.wait(g, MM.n - 1)
                        wload(g, LP[2 + s16 % 2],
                              slab_ap(wqkv, 0, C, 128 * s16, 128, 3 * C),
                              wb, NCH, 128)

                    @block.tensor
                    def _(t, s16=s16, wb=wb, tt=tt):
                        LP[2 + s16 % 2].wait(t)
                        if s16 == 0:
                            tensor_barrier(t)
                        if s16 >= 2:
                            EV.wait(t, EV.n - 1)
                        bank = s16 % 2
                        for kc in range(NCH):
                            ins = t.matmul(
                                PS_[bank][:, 0:512],
                                WS[:, wb + 128 * kc:wb + 128 * (kc + 1)],
                                XN[:, HT * kc + 512 * tt:HT * kc + 512 * tt + 512],
                                start=(kc == 0), stop=(kc == NCH - 1))
                        MM.inc(ins)

                    @block.vector
                    def _(v, s16=s16):
                        MM.wait(v)
                        bank = s16 % 2
                        EV.inc(v.tensor_copy(QKT[:, 512 * s16:512 * (s16 + 1)],
                                             PS_[bank][:, 0:512]))

                if APH in (4, 5) and tt == 1:
                    @block.sync
                    def _(sy):
                        EV.wait(sy)
                        if APH == 4:
                            for k in range(NCH):
                                ST.inc(sy.dma_start(
                                    out=xn1_out[k, :, t0:t0 + 512],
                                    in_=VT[:, 512 * k:512 * (k + 1)]), 16)
                        else:
                            for k in range(NCH):
                                ST.inc(sy.dma_start(
                                    out=xn1_out[k, :, t0:t0 + 1024],
                                    in_=QKT[:, 1024 * k:1024 * (k + 1)]), 16)

                @block.gpsimd
                def _(g, tt=tt):
                    MM.wait(g)
                    wload(g, LP[5], slab_ap(wproj, 0, C, 0, C, C), 0, NCH, C)
                pw0 = LP[5].n

                if tt == 0:
                    attn_i["base"] = dict(
                        ms=MS.n, va=VA.n, eb=EB.n, vm=VM.n, ex=EX.n,
                        vp=VP.n, mt=MT.n, ec=EC.n, mo=MO.n, vd=VD.n)

                for bj in range(4 if APH >= 6 else 0):
                    for hg in range(4):
                        i = attn_i["i"]
                        attn_i["i"] += 1
                        b0 = attn_i["base"]
                        ptb = PS_[4 + i % 2]
                        so = 512 * (i % 2)
                        sm = 8 * (i % 2)

                        @block.tensor
                        def _(t, bj=bj, hg=hg, i=i):
                            if i % 16 == 0:
                                EV.wait(t)
                            if i >= 1:
                                t.wait_ge(VA.h, b0["va"] + i)
                                t.wait_ge(EB.h, b0["eb"] + i)
                            for u in range(4):
                                hh = 4 * hg + u
                                qsl = QKT[64 * (hh % 2):64 * (hh % 2) + 64,
                                          512 * (hh // 2) + 128 * bj:
                                          512 * (hh // 2) + 128 * (bj + 1)]
                                ksl = QKT[64 * (hh % 2):64 * (hh % 2) + 64,
                                          512 * (8 + hh // 2) + 128 * bj:
                                          512 * (8 + hh // 2) + 128 * (bj + 1)]
                                ins = t.matmul(PS_[u][:, 0:128],
                                               qsl, ksl, start=True, stop=True,
                                               tile_position=(64 * (hh % 2), 0))
                            MS.inc(ins)

                        @block.vector
                        def _(v, i=i, so=so):
                            v.wait_ge(MS.h, b0["ms"] + i + 1)
                            if i >= 2:
                                v.wait_ge(VM.h, b0["vm"] + i - 1)
                            v.tensor_copy(SCM[:, so:so + 128], PS_[0][:, 0:128])
                            VA.inc(v.tensor_copy(SCM[:, so + 128:so + 256],
                                                 PS_[1][:, 0:128]))

                        @block.scalar
                        def _(s, i=i, so=so):
                            s.wait_ge(MS.h, b0["ms"] + i + 1)
                            if i >= 2:
                                s.wait_ge(VM.h, b0["vm"] + i - 1)
                            s.copy(SCM[:, so + 256:so + 384], PS_[2][:, 0:128])
                            EB.inc(s.copy(SCM[:, so + 384:so + 512],
                                          PS_[3][:, 0:128]))

                        @block.vector
                        def _(v, i=i, so=so, sm=sm):
                            v.wait_ge(VA.h, b0["va"] + i + 1)
                            v.wait_ge(EB.h, b0["eb"] + i + 1)
                            if i >= 2:
                                v.wait_ge(EX.h, b0["ex"] + i - 1)
                            v.scalar_tensor_tensor(
                                SC2[:, so:so + 512].rearrange("p (g f) -> p g f",
                                                              f=128),
                                SCM[:, so:so + 512].rearrange("p (g f) -> p g f",
                                                              f=128),
                                0.125, rep_mid(TRIL[:, :], 4),
                                ALU.mult, ALU.add)
                            ins = v.reduce_max(
                                SMAX[:, sm:sm + 4],
                                SC2[:, so:so + 512].rearrange("p (g f) -> p g f",
                                                              f=128),
                                axis=AX.X, negate=True)
                            VM.inc(ins)

                        @block.scalar
                        def _(s, i=i, so=so, sm=sm):
                            s.wait_ge(VM.h, b0["vm"] + i + 1)
                            if i >= 2:
                                s.wait_ge(MT.h, b0["mt"] + i - 1)
                            for u in range(4):
                                ins = s.activation(
                                    SC1[:, so + 128 * u:so + 128 * (u + 1)],
                                    SC2[:, so + 128 * u:so + 128 * (u + 1)],
                                    AF.Exp, bias=SMAX[:, sm + u:sm + u + 1],
                                    accum_out=SMAX[:, sm + 4 + u:sm + 5 + u])
                            EX.inc(ins)

                        @block.vector
                        def _(v, i=i, so=so, sm=sm):
                            v.wait_ge(EX.h, b0["ex"] + i + 1)
                            edge(v, v.tensor_scalar_add(SMAX[:, sm + 4:sm + 8],
                                                        SMAX[:, sm + 4:sm + 8],
                                                        1e-6))
                            edge(v, v.reciprocal(SMAX[:, sm + 4:sm + 8],
                                                 SMAX[:, sm + 4:sm + 8]))
                            ins = v.tensor_mul(
                                SC1[:, so:so + 512].rearrange("p (g f) -> p g f",
                                                              f=128),
                                SC1[:, so:so + 512].rearrange("p (g f) -> p g f",
                                                              f=128),
                                bcast_inner(SMAX[:, sm + 4:sm + 8], 128))
                            VP.inc(ins)

                        @block.tensor
                        def _(t, i=i, ptb=ptb, so=so):
                            t.wait_ge(VP.h, b0["vp"] + i + 1)
                            if i >= 2:
                                t.wait_ge(EC.h, b0["ec"] + i - 1)
                            for u in range(4):
                                ins = t.transpose(ptb[:, 128 * u:128 * (u + 1)],
                                                  SC1[:, so + 128 * u:
                                                      so + 128 * (u + 1)],
                                                  IDF[:, :])
                            MT.inc(ins)

                        @block.scalar
                        def _(s, i=i, ptb=ptb):
                            s.wait_ge(MT.h, b0["mt"] + i + 1)
                            if i >= 2:
                                s.wait_ge(MO.h, b0["mo"] + 2 * i - 2)
                            bo = 512 * (i % 2)
                            ins = s.copy(SCB[:, bo:bo + 512], ptb[:, 0:512])
                            EC.inc(ins)

                        for pair in range(2):
                            @block.tensor
                            def _(t, bj=bj, hg=hg, i=i, pair=pair):
                                if pair == 0:
                                    t.wait_ge(EC.h, b0["ec"] + i + 1)
                                    if i >= 1:
                                        t.wait_ge(VD.h, b0["vd"] + 2 * i)
                                else:
                                    t.wait_ge(VD.h, b0["vd"] + 2 * i + 1)
                                bo = 512 * (i % 2)
                                for w in range(2):
                                    u = 2 * pair + w
                                    hh = 4 * hg + u
                                    vsl = VT[:, 1024 * bj + 64 * hh:
                                             1024 * bj + 64 * (hh + 1)]
                                    ins = t.matmul(
                                        PS_[6 + w][64 * w:64 * w + 64, 0:128],
                                        vsl,
                                        SCB[:, bo + 128 * u:bo + 128 * (u + 1)],
                                        start=True, stop=True,
                                        tile_position=(0, 64 * w))
                                MO.inc(ins)

                            @block.vector
                            def _(v, bj=bj, hg=hg, i=i, pair=pair):
                                v.wait_ge(MO.h, b0["mo"] + 2 * i + pair + 1)
                                ck = 2 * hg + pair
                                col = 512 * ck + 128 * bj
                                v.tensor_copy(AFM[0:64, col:col + 128],
                                              PS_[6][0:64, 0:128])
                                VD.inc(v.tensor_copy(AFM[64:128, col:col + 128],
                                                     PS_[7][64:128, 0:128]))

                for ds in range(NCH if APH >= 7 else 0):
                    @block.tensor
                    def _(t, ds=ds, tt=tt):
                        t.wait_ge(LP[5].h, pw0)
                        if ds == 0:
                            VD.wait(t)
                            VA.wait(t)
                            EB.wait(t)
                        if ds >= 2:
                            EV.wait(t, EV.n - 1)
                        bank = ds % 2
                        for kc in range(NCH):
                            ins = t.matmul(
                                PS_[bank][:, 0:512],
                                WS[:, 1024 * kc + 128 * ds:
                                   1024 * kc + 128 * (ds + 1)],
                                AFM[:, 512 * kc:512 * (kc + 1)],
                                start=(kc == 0), stop=(kc == NCH - 1))
                        MM.inc(ins)

                    @block.vector
                    def _(v, ds=ds, tt=tt):
                        MM.wait(v)
                        bank = ds % 2
                        EV.inc(v.tensor_scalar_add(
                            S0[:, HT * ds + 512 * tt:HT * ds + 512 * (tt + 1)],
                            PS_[bank][:, 0:512], BPROJ[:, ds:ds + 1]))

            if APH == 6:
                @block.sync
                def _(sy):
                    VD.wait(sy)
                    EC.wait(sy)
                    ST.inc(sy.dma_start(out=x1_out[0, :, 0:512],
                                        in_=SCM[:, 512:1024]), 16)
                    ST.inc(sy.dma_start(out=x1_out[1, :, 0:512],
                                        in_=SC2[:, 512:1024]), 16)
                    ST.inc(sy.dma_start(out=x1_out[2, :, 0:512],
                                        in_=SC1[:, 512:1024]), 16)
                    ST.inc(sy.dma_start(out=x1_out[3, :, 0:16],
                                        in_=SMAX[:, 0:16]), 16)
                    ST.inc(sy.dma_start(out=xn1_out[0, :, 0:512],
                                        in_=SCB[:, 512:1024]), 16)

            if APH in (6, 7):
                @block.sync
                def _(sy):
                    EV.wait(sy)
                    EA.wait(sy)
                    VD.wait(sy)
                    for k in range(NCH):
                        ST.inc(sy.dma_start(out=x1_out[k, :, t0:t0 + HT],
                                            in_=S0[:, HT * k:HT * (k + 1)]), 16)
                    for k in range(NCH):
                        ST.inc(sy.dma_start(out=xn1_out[k, :, t0:t0 + 512],
                                            in_=AFM[:, 512 * k:512 * (k + 1)]), 16)
            if APH <= 7:
                return
            @block.gpsimd
            def _(g):
                MM.wait(g)
                wload(g, LP[5], slab_ap(r0w, 0, C, 0, 16, 16), 8064, NCH, 16)
            rw0 = LP[5].n

            for j in range(8):
                @block.tensor
                def _(t, j=j):
                    t.wait_ge(LP[5].h, rw0)
                    if j == 0:
                        EV.wait(t)
                    if j >= 2:
                        EV.wait(t, EV.n - 1)
                    bank = 6 + j % 2
                    for kc in range(NCH):
                        ins = t.matmul(
                            PS_[bank][:, 0:16],
                            XN[:, HT * kc + 128 * j:HT * kc + 128 * (j + 1)],
                            WS[:, 8064 + 16 * kc:8064 + 16 * (kc + 1)],
                            start=(kc == 0), stop=(kc == NCH - 1))
                    MM.inc(ins)

                @block.vector
                def _(v, j=j):
                    MM.wait(v)
                    bank = 6 + j % 2
                    EV.inc(v.tensor_copy(RT[:, 16 * j:16 * (j + 1)],
                                         PS_[bank][:, 0:16]))

            R3 = RT[:, :].rearrange("p (j e) -> p j e", e=16)

            @block.scalar
            def _(s):
                EV.wait(s)
                EA.inc(s.activation(RT[:, :], RT[:, :], AF.Exp))

            @block.vector
            def _(v):
                EA.wait(v)
                edge(v, v.reduce_sum(RS[:, 0:8], R3, axis=AX.X))
                edge(v, v.reciprocal(RS[:, 0:8], RS[:, 0:8]))
                edge(v, v.tensor_mul(R3, R3, bcast_inner(RS[:, 0:8], 16)))
                EV.inc(v.reduce_max(RS[:, 8:16], R3, axis=AX.X))

            for j in range(8):
                @block.vector
                def _(v, j=j):
                    EV.wait(v)
                    sl = slice(16 * j, 16 * (j + 1))
                    r = RT[:, sl]
                    m1c = RS[:, 8 + j:9 + j]
                    edge(v, v.tensor_scalar(SC2[:, 0:16], r, m1c, None,
                                            op0=ALU.is_ge))
                    edge(v, v.tensor_tensor_scan(SC1[:, 0:16], SC2[:, 0:16],
                                                 ZR16[:, 0:16], 0.0,
                                                 ALU.add, ALU.add))
                    edge(v, v.tensor_scalar(SC1[:, 16:32], SC1[:, 0:16], 1.0,
                                            None, op0=ALU.is_le))
                    edge(v, v.tensor_mul(SC2[:, 0:16], SC2[:, 0:16],
                                         SC1[:, 16:32]))
                    edge(v, v.scalar_tensor_tensor(SC1[:, 32:48], SC2[:, 0:16],
                                                   -1e30, r, ALU.mult, ALU.add))
                    edge(v, v.reduce_max(RS[:, 24 + j:25 + j], SC1[:, 32:48],
                                         axis=AX.X))
                    m2c = RS[:, 24 + j:25 + j]
                    edge(v, v.tensor_scalar(SC2[:, 16:32], SC1[:, 32:48], m2c,
                                            None, op0=ALU.is_ge))
                    edge(v, v.tensor_tensor_scan(SC1[:, 0:16], SC2[:, 16:32],
                                                 ZR16[:, 0:16], 0.0,
                                                 ALU.add, ALU.add))
                    edge(v, v.tensor_scalar(SC1[:, 16:32], SC1[:, 0:16], 1.0,
                                            None, op0=ALU.is_le))
                    edge(v, v.tensor_mul(SC2[:, 16:32], SC2[:, 16:32],
                                         SC1[:, 16:32]))
                    edge(v, v.tensor_add(RS[:, 32 + j:33 + j], m1c, m2c))
                    edge(v, v.tensor_scalar_add(RS[:, 32 + j:33 + j],
                                                RS[:, 32 + j:33 + j], 1e-9))
                    edge(v, v.reciprocal(RS[:, 32 + j:33 + j],
                                         RS[:, 32 + j:33 + j]))
                    edge(v, v.tensor_mul(RS[:, 40 + j:41 + j], m1c,
                                         RS[:, 32 + j:33 + j]))
                    edge(v, v.tensor_mul(RS[:, 48 + j:49 + j], m2c,
                                         RS[:, 32 + j:33 + j]))
                    edge(v, v.tensor_scalar(SC2[:, 32:48], SC2[:, 0:16],
                                            RS[:, 40 + j:41 + j], None,
                                            op0=ALU.mult))
                    edge(v, v.tensor_scalar(SC2[:, 16:32], SC2[:, 16:32],
                                            RS[:, 48 + j:49 + j], None,
                                            op0=ALU.mult))
                    edge(v, v.tensor_add(SC2[:, 32:48], SC2[:, 32:48],
                                         SC2[:, 16:32]))
                    EV.inc(v.tensor_copy(RT[:, sl], SC2[:, 32:48]))

                @block.tensor
                def _(t, j=j):
                    EV.wait(t)
                    MM.inc(t.transpose(PS_[6 + (j % 2)][0:16, 0:128],
                                       RT[:, 16 * j:16 * (j + 1)], IDF[:, :]))

                @block.vector
                def _(v, j=j):
                    MM.wait(v)
                    EV.inc(v.tensor_copy(CT[0:16, 128 * j:128 * (j + 1)],
                                         PS_[6 + (j % 2)][0:16, 0:128]))

            @block.gpsimd
            def _(g):
                EV.wait(g)
                LP[4].inc(g.dma_start(out=comb_dram[:, :], in_=CT[:, :]), 16)

            moe_prev = {"gu": None, "all": None}

            def moe_block(wg_d, wu_d, wd_d, mid, e_base_col, comb_col, first):
                nms = mid // 128
                wu0 = 8 * mid
                wd0 = 16 * mid

                gu_prev = moe_prev["gu"] if moe_prev["gu"] is not None else MM.n
                all_prev = moe_prev["all"] if moe_prev["all"] is not None else MM.n

                @block.gpsimd
                def _(g):
                    g.wait_ge(MM.h, gu_prev)
                    wload(g, LP[5], wg_d, 0, NCH, mid)
                    wload(g, LP[5], wu_d, wu0, NCH, mid)
                lwgu = LP[5].n

                @block.gpsimd
                def _(g):
                    g.wait_ge(MM.h, all_prev)
                    wload(g, LP[6], wd_d, wd0, nms, C)
                lw0 = LP[6].n

                for ms in range(nms):
                    for tt in range(2):
                        @block.tensor
                        def _(t, ms=ms, tt=tt):
                            t.wait_ge(LP[5].h, lwgu)
                            if ms + tt == 0:
                                EV.wait(t)
                                EA.wait(t)
                            gi = ms * 2 + tt
                            if gi >= 2:
                                EV.wait(t, EV.n - 1)
                            bg = gi % 2
                            bu = 2 + gi % 2
                            for kc in range(NCH):
                                t.matmul(PS_[bg][:, 0:512],
                                         WS[:, mid * kc + 128 * ms:
                                            mid * kc + 128 * (ms + 1)],
                                         XN[:, HT * kc + 512 * tt:
                                            HT * kc + 512 * tt + 512],
                                         start=(kc == 0), stop=(kc == NCH - 1))
                                ins = t.matmul(PS_[bu][:, 0:512],
                                               WS[:, wu0 + mid * kc + 128 * ms:
                                                  wu0 + mid * kc + 128 * (ms + 1)],
                                               XN[:, HT * kc + 512 * tt:
                                                  HT * kc + 512 * tt + 512],
                                               start=(kc == 0), stop=(kc == NCH - 1))
                            MM.inc(ins)

                        @block.scalar
                        def _(s, ms=ms, tt=tt):
                            MM.wait(s)
                            gi = ms * 2 + tt
                            bg = gi % 2
                            EA.inc(s.activation(SCB2[:, 0:512], PS_[bg][:, 0:512],
                                                AF.Silu))

                        @block.vector
                        def _(v, ms=ms, tt=tt):
                            EA.wait(v)
                            gi = ms * 2 + tt
                            bu = 2 + gi % 2
                            dst = AFM[:, 512 * (2 * ms + tt):512 * (2 * ms + tt + 1)]
                            if comb_col is not None:
                                v.tensor_tensor(SCB[:, 0:512], PS_[bu][:, 0:512],
                                                CMB[:, 512 * tt:512 * (tt + 1)],
                                                ALU.mult)
                                ins = v.tensor_mul(dst, SCB2[:, 0:512],
                                                   SCB[:, 0:512])
                            else:
                                v.tensor_copy(SCB[:, 0:512], PS_[bu][:, 0:512])
                                ins = v.tensor_mul(dst, SCB2[:, 0:512],
                                                   SCB[:, 0:512])
                            EV.inc(ins)

                moe_prev["gu"] = MM.n

                for ds in range(NCH):
                    for tt in range(2):
                        @block.tensor
                        def _(t, ds=ds, tt=tt):
                            if ds + tt == 0:
                                t.wait_ge(LP[6].h, lw0)
                                EV.wait(t)
                            gi = ds * 2 + tt
                            if gi >= 2:
                                EV.wait(t, EV.n - 1)
                            bank = 4 + gi % 2
                            for kg in range(nms):
                                ins = t.matmul(
                                    PS_[bank][:, 0:512],
                                    WS[:, wd0 + C * kg + 128 * ds:
                                       wd0 + C * kg + 128 * (ds + 1)],
                                    AFM[:, 512 * (2 * kg + tt):
                                        512 * (2 * kg + tt + 1)],
                                    start=(kg == 0), stop=(kg == nms - 1))
                            MM.inc(ins)

                        @block.vector
                        def _(v, ds=ds, tt=tt):
                            MM.wait(v)
                            gi = ds * 2 + tt
                            bank = 4 + gi % 2
                            dst = G0[:, HT * ds + 512 * tt:HT * ds + 512 * (tt + 1)]
                            if first:
                                ins = v.tensor_copy(dst, PS_[bank][:, 0:512])
                            else:
                                ins = v.tensor_add(dst, dst, PS_[bank][:, 0:512])
                            EV.inc(ins)

                moe_prev["all"] = MM.n

            if APH == 8:
                @block.sync
                def _(sy):
                    EV.wait(sy)
                    EA.wait(sy)
                    LP[4].wait(sy)
                    ST.inc(sy.dma_start(out=x1_out[0, 0:16, t0:t0 + HT],
                                        in_=CT[:, :]), 16)
            if APH <= 8:
                return
            for sh in range(4):
                moe_block(slab_ap(wgs, 0, C, 256 * sh, 256, 1024),
                          slab_ap(wus, 0, C, 256 * sh, 256, 1024),
                          slab_ap(wds, 256 * sh, 256, 0, C, C),
                          256, None, None, first=(sh == 0))

            if APH <= 9:
                return
            for e in range(16):
                @block.gpsimd
                def _(g, e=e):
                    EV.wait(g, EV.n)
                    LP[4].inc(g.dma_start(out=ROW0[0:1, 0:HT],
                                          in_=comb_dram[e:e + 1, :]), 16)
                cb0 = LP[4].n

                for tt in range(2):
                    @block.tensor
                    def _(t, tt=tt):
                        t.wait_ge(LP[4].h, cb0)
                        if tt == 0:
                            EV.wait(t)
                        MM.inc(t.matmul(PS_[6 + tt][:, 0:512], ONES1[0:1, :],
                                        ROW0[0:1, 512 * tt:512 * (tt + 1)],
                                        start=True, stop=True))

                    @block.vector
                    def _(v, tt=tt):
                        MM.wait(v)
                        EV.inc(v.tensor_copy(CMB[:, 512 * tt:512 * (tt + 1)],
                                             PS_[6 + tt][:, 0:512]))

                moe_block(slab_ap(wgr, 0, C, 256 * e, 256, 4096),
                          slab_ap(wur, 0, C, 256 * e, 256, 4096),
                          slab_ap(wdr, 256 * e, 256, 0, C, C),
                          256, None, 0, first=False)

            if APH in (9, 10):
                @block.sync
                def _(sy):
                    EV.wait(sy)
                    EA.wait(sy)
                    for k in range(NCH):
                        ST.inc(sy.dma_start(out=x1_out[k, :, t0:t0 + HT],
                                            in_=G0[:, HT * k:HT * (k + 1)]), 16)
            if APH <= 10:
                return
            for k in range(NCH):
                for tt in range(2):
                    @block.scalar
                    def _(s, k=k, tt=tt):
                        EV.wait(s)
                        sl = slice(HT * k + 512 * tt, HT * k + 512 * tt + 512)
                        EA.inc(s.activation(SC1[:, 512 * (tt % 2):
                                                512 * (tt % 2) + 512],
                                            G0[:, sl], AF.Silu))

                    @block.vector
                    def _(v, k=k, tt=tt):
                        EA.wait(v)
                        sl = slice(HT * k + 512 * tt, HT * k + 512 * tt + 512)
                        v.tensor_mul(SC2[:, 0:512],
                                     S0[:, sl], SC1[:, 512 * (tt % 2):
                                                    512 * (tt % 2) + 512])
                        EV.inc(v.tensor_add(XF[:, sl], XF[:, sl], SC2[:, 0:512]))

            @block.sync
            def _(sy):
                EV.wait(sy)
                for k in range(NCH):
                    ST.inc(sy.dma_start(out=x1_out[k, :, t0:t0 + HT],
                                        in_=XF[:, HT * k:HT * (k + 1)]), 16)

            if APH <= 11:
                return
            emit_ln_stats(XF, XN, out_dram=xn1_out)

            @block.gpsimd
            def _(g):
                MM.wait(g)
                wload(g, LP[5], slab_ap(l1rw, 0, C, 0, 8, 8), 8128, NCH, 8)
                LP[4].inc(g.dma_start(out=ROW0[0:1, 0:8], in_=l1rb[0:1, :]), 16)
            lb0 = LP[5].n
            lb1 = LP[4].n

            @block.tensor
            def _(t):
                t.wait_ge(LP[5].h, lb0)
                t.wait_ge(LP[4].h, lb1)
                EV.wait(t)
                MM.inc(t.matmul(PS_[5][:, 0:8], ONES1[0:1, :], ROW0[0:1, 0:8],
                                start=True, stop=True))

            @block.vector
            def _(v):
                MM.wait(v)
                EV.inc(v.tensor_copy(RS[:, 56:64], PS_[5][:, 0:8]))

            for j in range(8):
                @block.tensor
                def _(t, j=j):
                    if j >= 2:
                        EV.wait(t, EV.n - 1)
                    bank = 6 + j % 2
                    for kc in range(NCH):
                        ins = t.matmul(
                            PS_[bank][:, 0:8],
                            XN[:, HT * kc + 128 * j:HT * kc + 128 * (j + 1)],
                            WS[:, 8128 + 8 * kc:8128 + 8 * (kc + 1)],
                            start=(kc == 0), stop=(kc == NCH - 1))
                    MM.inc(ins)

                @block.vector
                def _(v, j=j):
                    MM.wait(v)
                    bank = 6 + j % 2
                    EV.inc(v.tensor_add(RT[:, 16 * j:16 * j + 8],
                                        PS_[bank][:, 0:8], RS[:, 56:64]))

            RL = AP(RT[:, :].tensor, RT[:, :].offset,
                    [list(RT[:, :].ap[0]), [16, 8], [1, 8]])

            @block.scalar
            def _(s):
                EV.wait(s)
                EA.inc(s.activation(RT[:, :], RT[:, :], AF.Exp))

            @block.vector
            def _(v):
                EA.wait(v)
                edge(v, v.reduce_sum(RS[:, 0:8], RL, axis=AX.X))
                edge(v, v.reciprocal(RS[:, 0:8], RS[:, 0:8]))
                EV.inc(v.tensor_mul(RL, RL, bcast_inner(RS[:, 0:8], 8)))

            @block.sync
            def _(sy):
                EV.wait(sy)
                for j in range(8):
                    ST.inc(sy.dma_start(
                        out=sc_out[t0 + 128 * j:t0 + 128 * (j + 1), :],
                        in_=RT[:, 16 * j:16 * j + 8]), 16)

        emit_half(0)
        emit_half(1)

        @block.sync
        def _(sy):
            ST.wait(sy)

    al.close()
    return nc



_A_CACHE = {}


def prep_A_weights(inp):
    w = {}
    w["wqkv"] = inp["L0_attn_W_qkv"].astype(np.float32)
    w["wproj"] = inp["L0_attn_W_proj"].astype(np.float32)
    w["bproj"] = inp["L0_attn_b_proj"].astype(np.float32).reshape(1, C)
    w["r0w"] = inp["L0_ffn_router_w"].astype(np.float32)
    w["wgs"] = np.concatenate([inp["L0_ffn_wg_s"][i] for i in range(2)],
                              axis=1).astype(np.float32)
    w["wus"] = np.concatenate([inp["L0_ffn_wu_s"][i] for i in range(2)],
                              axis=1).astype(np.float32)
    w["wds"] = np.concatenate([inp["L0_ffn_wd_s"][i] for i in range(2)],
                              axis=0).astype(np.float32)
    w["wgr"] = np.concatenate([inp["L0_ffn_wg_r"][i] for i in range(16)],
                              axis=1).astype(np.float32)
    w["wur"] = np.concatenate([inp["L0_ffn_wu_r"][i] for i in range(16)],
                              axis=1).astype(np.float32)
    w["wdr"] = np.concatenate([inp["L0_ffn_wd_r"][i] for i in range(16)],
                              axis=0).astype(np.float32)
    w["l1rw"] = inp["L1_ffn_router_w"].astype(np.float32)
    w["l1rb"] = inp["L1_ffn_router_b"].astype(np.float32).reshape(1, 8)
    tril = np.where(np.tril(np.ones((128, 128), bool)), 0.0, -1e30)
    w["trilm"] = tril.astype(np.float32)
    return {k: np.ascontiguousarray(v) for k, v in w.items()}


def run_launch_A(inp):
    x = np.asarray(inp["L0_attn_x"], np.float32).reshape(16384, C)
    w = prep_A_weights(inp)
    nc = build_launch_A()
    in_maps = []
    for c in range(N_CORES):
        m = dict(w)
        m["x_in"] = np.ascontiguousarray(x[P * c:P * (c + 1)])
        in_maps.append(m)
    res = run_bass_kernel_spmd(nc, in_maps, core_ids=list(range(N_CORES)))
    x1 = np.empty((16384, C), np.float32)
    xn1 = np.empty((16384, C), BF)
    scores = np.empty((16384, 8), np.float32)
    for c in range(N_CORES):
        r = res.results[c]
        x1[P * c:P * (c + 1)] = r["x1_out"].reshape(C, P).T
        xn1[P * c:P * (c + 1)] = r["xn1_out"].reshape(C, P).T
        scores[P * c:P * (c + 1)] = r["sc_out"]
    return x1, xn1, scores





TB = 512
Q = 256


def declare_B_drams(nc):
    dr = {}
    for nm, shape in (("x1b", [NCH, 128, 4096]), ("tmcd", [128, 6 * NCH]),
                      ("maw1", [C, 128]), ("maw2", [128, C]),
                      ("dcw1", [C, 64]), ("dcw2", [64, 512]),
                      ("faaT", [64, 8]), ("wr_", [C, 512]), ("wk_", [C, 512]),
                      ("wv_", [C, 512]), ("hindm", [128, 32]),
                      ("strictT", [2, 128, 256])):
        dr[nm] = nc.dram_tensor(nm, shape, F32, kind="ExternalInput").ap()
    dr["wrows_d"] = nc.dram_tensor("wrows_d", [16, 520], F32).ap()
    return dr


def build_launch_B():
    nc = bass.Bass("TRN2", target_bir_lowering=False)
    dr = declare_B_drams(nc)
    y_out = nc.dram_tensor("y_out", [4, 128, 4096], F32,
                           kind="ExternalOutput").ap()
    alP = Alloc(nc)
    alB = Alloc(nc)
    with nc.Block() as block:
        emit_B_part(nc, block, alB, alP, dr, ("dram", y_out))
    alB.close()
    alP.close()
    return nc


def emit_B_part(nc, block, alB, alP, dr, y_sink):
    x1b = dr["x1b"]; tmcd = dr["tmcd"]; maw1 = dr["maw1"]; maw2 = dr["maw2"]
    dcw1 = dr["dcw1"]; dcw2 = dr["dcw2"]; faaT = dr["faaT"]
    wr_ = dr["wr_"]; wk_ = dr["wk_"]; wv_ = dr["wv_"]
    hindm = dr["hindm"]; strictT = dr["strictT"]; wrows_d = dr["wrows_d"]
    sb, ps = alB.sb, alB.ps
    sem = alP.sem
    CONSTB = alP.sb("CONSTB", [1, 4], F32)

    XT = sb("XT", [128, NCH * TB], F32)
    XX = sb("XX", [128, NCH * TB], F32)
    XXB = sb("XXB", [128, NCH * TB], BF16)
    XNB = sb("XNB", [128, NCH * TB], BF16)
    MIX = sb("MIX", [128, NCH * TB], BF16)
    MF = sb("MF", [128, NCH * TB], BF16)
    M1 = sb("M1", [128, TB], BF16)
    RF = sb("RF", [128, 4 * TB], BF16)
    KF = sb("KF", [128, 4 * TB], BF16)
    VF = sb("VF", [128, 4 * TB], BF16)
    KTM = sb("KTM", [128, 4 * TB], BF16)
    VTM = sb("VTM", [128, 4 * TB], BF16)
    KW = sb("KW", [128, 4 * TB], BF16)
    WD = sb("WD", [128, 4 * TB], F32)
    WH = sb("WH", [8, TB], F32)
    T1B = sb("T1B", [64, TB], BF16)
    WCC = sb("WCC", [8, 256], F32)
    WCUM = sb("WCUM", [8, 256], F32)
    WINT = sb("WINT", [8, 256], F32)
    WROWS = sb("WROWS", [8, 520], F32)
    WRB = sb("WRB", [1, 2 * 520], F32)
    WIT = sb("WIT", [128, 4 * 8], F32)
    WCT = sb("WCT", [128, 4 * 8], F32)
    SCT = sb("SCT", [128, 2 * 256], BF16)
    MSK = sb("MSK", [128, 2 * 256], F32)
    RSW = sb("RSW", [64, 256], BF16)
    KV = sb("KV", [64, 512], F32)
    STATE = sb("STATE", [64, 512], F32)
    STB = sb("STB", [64, 512], BF16)
    YC = sb("YC", [128, 4 * 256], F32)
    DGR = sb("DGR", [1, 3 * 256], F32)
    WSROW = sb("WSROW", [1, 768], F32)
    WSR = sb("WSR", [128, NCH * 512], BF16)
    WSK = sb("WSK", [128, NCH * 512], BF16)
    WSV = sb("WSV", [128, NCH * 512], BF16)
    WSA = sb("WSA", [128, NCH * 128], BF16)
    WSA2 = sb("WSA2", [128, C], BF16)
    WSD1 = sb("WSD1", [128, NCH * 64], BF16)
    WSD2 = sb("WSD2", [64, 512], BF16)
    SRW = sb("SRW", [1, 3 * TB], F32)
    SC1B = sb("SC1B", [128, 2048], F32)
    SC2B = sb("SC2B", [64, 512], F32)
    FAA = sb("FAA", [64, 8], F32)
    HIND = sb("HIND", [128, 32], F32)
    STT = sb("STT", [128, 2 * 256], F32)
    PREV = sb("PREV", [128, NCH], F32)
    ONES1b = sb("ONES1b", [1, 128], F32)
    ONESCb = sb("ONESCb", [128, 1], F32)
    IDFb = sb("IDFb", [128, 128], F32)
    IDB = sb("IDB", [128, 128], BF16)
    ZR256 = sb("ZR256", [8, 256], F32)
    TMC = sb("TMC", [128, 6 * NCH], F32)

    PSF = [ps(f"PBF{i}", [128, 512], F32) for i in range(6)]
    PSB = [ps(f"PBB{i}", [128, 512], BF16) for i in range(2)]

    LX = sem("LX")
    LW = sem("LW")
    EV = sem("EV")
    EA = sem("EA")
    MMG = sem("MMG")
    ST = sem("ST")
    VE = sem("VE")
    BK = [sem(f"BKF{i}") for i in range(6)] + [sem(f"BKB{i}") for i in range(2)]
    LPB = [sem(f"LPB{i}") for i in range(5)]
    WRC = sem("WRC")

    SUMS = MEAN = MR = SRW[0:1, 0 * TB:1 * TB]
    SSQ = VAR = SRW[0:1, 1 * TB:2 * TB]
    RSCR = RSTD = SRW[0:1, 2 * TB:3 * TB]

    def slab_ap(dram, r0, nr, c0, ncols, row_len):
        t = dram.tensor
        off = dram.offset + r0 * row_len + c0
        return AP(t, off, [[row_len, 128], [128 * row_len, max(nr // 128, 1)],
                           [1, ncols]])

    def sb_dst(tile, col0, nkc, ncols, width):
        full = tile[:, :]
        return AP(full.tensor, full.offset + col0,
                  [[width, 128], [ncols, nkc], [1, ncols]])

    if True:

        nc.const_aps.register(F32, LNEPS, CONSTB[0:1, 0:1])

        def edge(eng, inst):
            VE.inc(inst)
            VE.wait(eng)

        @block.gpsimd
        def _(g):
            g.memset(ONES1b[:, :], 1.0)
            g.memset(ONESCb[:, :], 1.0)
            g.memset(ZR256[:, :], 0.0)
            g.memset(PREV[:, :], 0.0)
            g.memset(STATE[:, :], 0.0)
            g.memset(STB[:, :], 0.0)
            g.memset(WROWS[:, :], 0.0)
            g.memset(CONSTB[0:1, 0:1], LNEPS)
            g.memset(IDB[:, :], 1.0)
            LW.inc(g.memset(IDFb[:, :], 1.0))
            g.wait_ge(LW.h, 1)
            LW.inc(g.affine_select(IDFb[:, :], IDFb[:, :], pattern=[[1, 128]],
                                   compare_op=ALU.is_equal, fill=0.0,
                                   channel_multiplier=-1))
            g.wait_ge(LW.h, 2)
            LW.inc(g.affine_select(IDB[:, :], IDB[:, :], pattern=[[1, 128]],
                                   compare_op=ALU.is_equal, fill=0.0,
                                   channel_multiplier=-1))
            LPB[0].inc(g.dma_start(out=sb_dst(WSR, 0, NCH, 512, NCH * 512),
                                   in_=slab_ap(wr_, 0, C, 0, 512, 512)), 16)
            LPB[0].inc(g.dma_start(out=sb_dst(WSK, 0, NCH, 512, NCH * 512),
                                   in_=slab_ap(wk_, 0, C, 0, 512, 512)), 16)
            LPB[0].inc(g.dma_start(out=sb_dst(WSV, 0, NCH, 512, NCH * 512),
                                   in_=slab_ap(wv_, 0, C, 0, 512, 512)), 16)
            LPB[0].inc(g.dma_start(out=sb_dst(WSA, 0, NCH, 128, NCH * 128),
                                   in_=slab_ap(maw1, 0, C, 0, 128, 128)), 16)
            LPB[0].inc(g.dma_start(out=WSA2[:, :], in_=maw2[:, :]), 16)
            LPB[0].inc(g.dma_start(out=sb_dst(WSD1, 0, NCH, 64, NCH * 64),
                                   in_=slab_ap(dcw1, 0, C, 0, 64, 64)), 16)
            LPB[0].inc(g.dma_start(out=WSD2[:, :], in_=dcw2[:, :]), 16)

        @block.sync
        def _(sy):
            LX.inc(sy.dma_start(out=HIND[:, :], in_=hindm[:, :]), 16)
            LX.inc(sy.dma_start(out=FAA[:, :], in_=faaT[:, :]), 16)
            LX.inc(sy.dma_start(out=STT[:, 0:256], in_=strictT[0, :, :]), 16)
            LX.inc(sy.dma_start(out=STT[:, 256:512], in_=strictT[1, :, :]), 16)
            LX.inc(sy.dma_start(out=TMC[:, :], in_=tmcd[:, :]), 16)

        for it in range(8):
            t0 = it * TB

            @block.sync
            def _(sy, it=it, t0=t0):
                EV.wait(sy)
                EA.wait(sy)
                for kc in range(NCH):
                    LX.inc(sy.dma_start(out=XT[:, TB * kc:TB * (kc + 1)],
                                        in_=x1b[kc, :, t0:t0 + TB]), 16)
            lx_t = LX.n

            for grp in range(2):
                @block.scalar
                def _(s, grp=grp):
                    s.wait_ge(LX.h, lx_t)
                    EV.wait(s)
                    if grp == 1:
                        MMG.wait(s)
                    for kk in range(4):
                        kc = grp * 4 + kk
                        ins = s.activation(SC1B[:, 512 * kk:512 * (kk + 1)],
                                           XT[:, TB * kc:TB * (kc + 1)],
                                           AF.Square)
                    EA.inc(ins)

                @block.tensor
                def _(t, grp=grp):
                    EA.wait(t)
                    if grp == 0:
                        t.wait_ge(LW.h, 3)
                        LPB[0].wait(t)
                        BK[4].wait(t)
                        BK[5].wait(t)
                    for kk in range(4):
                        kc = grp * 4 + kk
                        t.matmul(PSF[4][0:1, 0:512], ONESCb[:, 0:1],
                                 XT[:, TB * kc:TB * (kc + 1)],
                                 start=(kc == 0), stop=(kc == 7))
                        ins = t.matmul(PSF[5][0:1, 0:512], ONESCb[:, 0:1],
                                       SC1B[:, 512 * kk:512 * (kk + 1)],
                                       start=(kc == 0), stop=(kc == 7))
                    MMG.inc(ins)

            @block.vector
            def _(v):
                MMG.wait(v)
                v.tensor_copy(SUMS, PSF[4][0:1, 0:512])
                ins = v.tensor_copy(SSQ, PSF[5][0:1, 0:512])
                BK[4].inc(ins)
                BK[5].inc(v.engine_nop())
                edge(v, v.tensor_scalar_mul(MEAN, SUMS, 1.0 / C))
                edge(v, v.tensor_scalar_mul(VAR, SSQ, 1.0 / C))
                edge(v, v.tensor_mul(RSCR, MEAN, MEAN))
                EV.inc(v.tensor_sub(VAR, VAR, RSCR))

            @block.scalar
            def _(s):
                EV.wait(s)
                edge(s, s.activation(RSCR, VAR, AF.Ln, bias=LNEPS))
                EA.inc(s.activation(RSTD, RSCR, AF.Exp, scale=-0.5))

            @block.vector
            def _(v):
                EA.wait(v)
                EV.inc(v.tensor_mul(MR, MEAN, RSTD))

            @block.tensor
            def _(t):
                EV.wait(t)
                BK[4].wait(t)
                BK[5].wait(t)
                t.matmul(PSF[4][:, 0:512], ONES1b[0:1, :], RSTD,
                         start=True, stop=True)
                MMG.inc(t.matmul(PSF[5][:, 0:512], ONES1b[0:1, :], MR,
                                 start=True, stop=True))

            @block.vector
            def _(v):
                MMG.wait(v)
                for kc in range(NCH):
                    sl = slice(TB * kc, TB * (kc + 1))
                    v.tensor_mul(SC1B[:, 0:512], XT[:, sl], PSF[4][:, 0:512])
                    ins = v.tensor_sub(XT[:, sl], SC1B[:, 0:512], PSF[5][:, 0:512])
                BK[4].inc(ins)
                BK[5].inc(v.engine_nop())
                for kc in range(NCH):
                    sl0 = TB * kc
                    v.tensor_sub(XX[:, sl0 + 1:sl0 + TB], XT[:, sl0:sl0 + TB - 1],
                                 XT[:, sl0 + 1:sl0 + TB])
                    ins = v.tensor_sub(XX[:, sl0:sl0 + 1], PREV[:, kc:kc + 1],
                                       XT[:, sl0:sl0 + 1])
                    VE.inc(ins)
                for kc in range(NCH):
                    v.tensor_copy(PREV[:, kc:kc + 1],
                                  XT[:, TB * kc + TB - 1:TB * kc + TB])
                VE.wait(v)
                v.tensor_copy(XXB[:, :], XX[:, :])
                EV.inc(v.tensor_copy(XNB[:, :], XT[:, :]))

            @block.vector
            def _(v):
                EV.wait(v)
                for kc in range(NCH):
                    sl = slice(TB * kc, TB * (kc + 1))
                    ins = v.scalar_tensor_tensor(MIX[:, sl], XX[:, sl],
                                                 TMC[:, kc:kc + 1], XT[:, sl],
                                                 ALU.mult, ALU.add)
                EV.inc(ins)

            @block.tensor
            def _(t):
                EV.wait(t)
                LPB[0].wait(t)
                BK[0].wait(t)
                for kc in range(NCH):
                    ins = t.matmul(PSF[0][:, 0:512],
                                   WSA[:, 128 * kc:128 * (kc + 1)],
                                   MIX[:, TB * kc:TB * (kc + 1)],
                                   start=(kc == 0), stop=(kc == 7))
                MMG.inc(ins)

            @block.scalar
            def _(s):
                MMG.wait(s)
                ins = s.activation(M1[:, :], PSF[0][:, 0:512], AF.Tanh)
                EA.inc(ins)
                BK[0].inc(s.nop())

            for f, (dst, tmblk) in enumerate(
                    [(None, 1), (KF, 2), (VF, 3), (RF, 4)]):
                for sl8 in range(NCH):
                    b = sl8 % 2

                    @block.tensor
                    def _(t, f=f, sl8=sl8, b=b):
                        if f == 0 and sl8 == 0:
                            EA.wait(t)
                        BK[b].wait(t)
                        ins = t.matmul(
                            PSF[b][:, 0:512],
                            WSA2[32 * f:32 * (f + 1), 128 * sl8:128 * (sl8 + 1)],
                            M1[32 * f:32 * (f + 1), :],
                            start=True, stop=True,
                            tile_position=(32 * f, 0))
                        MMG.inc(ins)

                    @block.vector
                    def _(v, f=f, sl8=sl8, b=b, tmblk=tmblk):
                        MMG.wait(v)
                        ins = v.tensor_scalar(
                            MF[:, TB * sl8:TB * (sl8 + 1)], PSF[b][:, 0:512],
                            TMC[:, NCH * tmblk + sl8:NCH * tmblk + sl8 + 1],
                            None, op0=ALU.add)
                        BK[b].inc(ins)
                        if sl8 == NCH - 1:
                            EV.inc(v.engine_nop())

                @block.vector
                def _(v, f=f):
                    EV.wait(v)
                    for kc in range(NCH):
                        sl = slice(TB * kc, TB * (kc + 1))
                        if f == 0:
                            v.tensor_mul(SC1B[:, 0:512], XX[:, sl], MF[:, sl])
                            ins = v.tensor_add(MIX[:, sl], SC1B[:, 0:512],
                                               XT[:, sl])
                        else:
                            v.tensor_mul(SC1B[:, 0:512], XXB[:, sl], MF[:, sl])
                            ins = v.tensor_add(MIX[:, sl], SC1B[:, 0:512],
                                               XNB[:, sl])
                    EV.inc(ins)

                if f == 0:
                    @block.tensor
                    def _(t):
                        EV.wait(t)
                        BK[2].wait(t)
                        for kc in range(NCH):
                            ins = t.matmul(PSF[2][0:64, 0:512],
                                           WSD1[:, 64 * kc:64 * (kc + 1)],
                                           MIX[:, TB * kc:TB * (kc + 1)],
                                           start=(kc == 0), stop=(kc == 7))
                        MMG.inc(ins)

                    @block.scalar
                    def _(s):
                        MMG.wait(s)
                        ins = s.activation(T1B[:, :], PSF[2][0:64, 0:512], AF.Tanh)
                        EA.inc(ins)
                        BK[2].inc(s.nop())

                    for s4 in range(4):
                        b = s4 % 2

                        @block.tensor
                        def _(t, s4=s4, b=b):
                            if s4 == 0:
                                EA.wait(t)
                            BK[b].wait(t)
                            ins = t.matmul(PSF[b][:, 0:512],
                                           WSD2[:, 128 * s4:128 * (s4 + 1)],
                                           T1B[:, :], start=True, stop=True)
                            MMG.inc(ins)

                        @block.vector
                        def _(v, s4=s4, b=b):
                            MMG.wait(v)
                            ins = v.tensor_scalar(
                                WD[:, TB * s4:TB * (s4 + 1)], PSF[b][:, 0:512],
                                TMC[:, NCH * 5 + s4:NCH * 5 + s4 + 1],
                                None, op0=ALU.add)
                            BK[b].inc(ins)
                            if s4 == 3:
                                EV.inc(v.engine_nop())

                    @block.tensor
                    def _(t):
                        EV.wait(t)
                        BK[3].wait(t)
                        for s4 in range(4):
                            ins = t.matmul(PSF[3][0:8, 0:512],
                                           HIND[:, 8 * s4:8 * (s4 + 1)],
                                           WD[:, TB * s4:TB * (s4 + 1)],
                                           start=(s4 == 0), stop=(s4 == 3))
                        MMG.inc(ins)

                    @block.vector
                    def _(v):
                        MMG.wait(v)
                        ins = v.tensor_copy(WH[:, :], PSF[3][0:8, 0:512])
                        BK[3].inc(ins)
                        EV.inc(v.engine_nop())
                else:
                    for s4 in range(4):
                        b = s4 % 2
                        wsl = [None, WSK, WSV, WSR][f]

                        @block.tensor
                        def _(t, f=f, s4=s4, b=b, wsl=wsl):
                            if s4 == 0:
                                EV.wait(t)
                            BK[b].wait(t)
                            for kc in range(NCH):
                                ins = t.matmul(
                                    PSF[b][:, 0:512],
                                    wsl[:, 512 * kc + 128 * s4:
                                        512 * kc + 128 * (s4 + 1)],
                                    MIX[:, TB * kc:TB * (kc + 1)],
                                    start=(kc == 0), stop=(kc == 7))
                            MMG.inc(ins)

                        @block.vector
                        def _(v, f=f, s4=s4, b=b, dst=dst):
                            MMG.wait(v)
                            ins = v.tensor_copy(dst[:, TB * s4:TB * (s4 + 1)],
                                                PSF[b][:, 0:512])
                            BK[b].inc(ins)
                            if s4 == 3:
                                EV.inc(v.engine_nop())

            for (srcf, dstf) in ((KF, KTM), (VF, VTM)):
                for s4 in range(4):
                    bb = s4 % 2

                    @block.tensor
                    def _(t, srcf=srcf, s4=s4, bb=bb):
                        if srcf is KF and s4 == 0:
                            EV.wait(t)
                        BK[6 + bb].wait(t)
                        for qt in range(4):
                            ins = t.transpose(
                                PSB[bb][:, 128 * qt:128 * (qt + 1)],
                                srcf[:, TB * s4 + 128 * qt:TB * s4 + 128 * (qt + 1)],
                                IDB[:, :])
                        MMG.inc(ins)

                    @block.vector
                    def _(v, dstf=dstf, s4=s4, bb=bb):
                        MMG.wait(v)
                        dst = AP(dstf[:, :].tensor,
                                 dstf[:, :].offset + 128 * s4,
                                 [[4 * TB, 128], [512, 4], [1, 128]])
                        src = PSB[bb][:, 0:512].rearrange("p (g f) -> p g f", f=128)
                        ins = v.tensor_copy(dst, src)
                        BK[6 + bb].inc(ins)
                        if dstf is VTM and s4 == 3:
                            EV.inc(v.engine_nop())

            for cc in range(2):
                c_glob = 2 * it + cc

                @block.scalar
                def _(s, cc=cc):
                    EV.wait(s)
                    EA.inc(s.activation(WCC[:, :], WH[0:8, 256 * cc:256 * (cc + 1)],
                                        AF.Exp))

                @block.vector
                def _(v, cc=cc):
                    EA.wait(v)
                    edge(v, v.tensor_scalar_mul(WCC[:, :], WCC[:, :], -1.0))
                    edge(v, v.tensor_tensor_scan(WCUM[:, :], WCC[:, :],
                                                 ZR256[:, :], 0.0,
                                                 ALU.add, ALU.add))
                    edge(v, v.tensor_sub(WROWS[0:8, 0:256], WCC[:, :],
                                         WCUM[:, :]))
                    edge(v, v.tensor_copy(WROWS[0:8, 256:512], WCC[:, :]))
                    EV.inc(v.tensor_copy(WROWS[0:8, 512:513],
                                         WCUM[:, 255:256]))

                @block.scalar
                def _(s, cc=cc):
                    EV.wait(s)
                    EA.inc(s.activation(WINT[:, :], WCUM[:, :], AF.Exp,
                                        scale=-1.0,
                                        bias=WCUM[:, 255:256]))

                @block.tensor
                def _(t, cc=cc):
                    EA.wait(t)
                    BK[2].wait(t)
                    for qt in range(2):
                        t.transpose(PSF[2][0:128, 16 * qt:16 * qt + 8],
                                    WINT[0:8, 128 * qt:128 * (qt + 1)],
                                    IDFb[0:8, 0:8])
                        ins = t.transpose(PSF[2][0:128, 16 * qt + 8:16 * qt + 16],
                                          WCUM[0:8, 128 * qt:128 * (qt + 1)],
                                          IDFb[0:8, 0:8])
                    ins = t.transpose(PSF[2][0:1, 40:48],
                                      WCUM[0:8, 255:256], IDFb[0:8, 0:8])
                    MMG.inc(ins)

                @block.vector
                def _(v, cc=cc):
                    MMG.wait(v)
                    for qt in range(2):
                        v.tensor_copy(WIT[:, 8 * (2 * cc + qt):8 * (2 * cc + qt) + 8],
                                      PSF[2][0:128, 16 * qt:16 * qt + 8])
                        v.tensor_copy(
                            WCT[:, 8 * (2 * cc + qt):8 * (2 * cc + qt) + 8],
                            PSF[2][0:128, 16 * qt + 8:16 * qt + 16])
                    ins = v.tensor_copy(WSROW[0:1, 0:8], PSF[2][0:1, 40:48])
                    BK[2].inc(ins)
                    EV.inc(v.engine_nop())

                @block.gpsimd
                def _(g, cc=cc):
                    EV.wait(g)
                    LPB[1].inc(g.dma_start(out=wrows_d[8 * cc:8 * cc + 8, :],
                                           in_=WROWS[0:8, :]), 16)

                @block.vector
                def _(v, cc=cc):
                    EV.wait(v)
                    for qt in range(2):
                        qcol = 512 * (2 * cc + qt)
                        ins = v.tensor_mul(
                            KW[:, qcol:qcol + 512].rearrange("p (h n) -> p h n", n=64),
                            KTM[:, qcol:qcol + 512].rearrange("p (h n) -> p h n", n=64),
                            bcast_inner(WIT[:, 8 * (2 * cc + qt):
                                            8 * (2 * cc + qt) + 8], 64))
                    EV.inc(ins)

                for h in range(8):
                    hp = h % 2
                    hc = h // 2

                    @block.gpsimd
                    def _(g, h=h, cc=cc):
                        if h < 2:
                            LPB[1].wait(g)
                        else:
                            g.wait_ge(WRC.h, WRC.n - 1)
                        LPB[2 + h % 2].inc(g.dma_start(
                            out=WRB[0:1, 520 * (h % 2):520 * (h % 2) + 520],
                            in_=wrows_d[8 * cc + h:8 * cc + h + 1, :]), 16)
                    row0 = 520 * (h % 2)

                    @block.tensor
                    def _(t, h=h, cc=cc):
                        if h == 0:
                            EV.wait(t)
                        BK[4].wait(t)
                        for qt in range(2):
                            qcol = 512 * (2 * cc + qt)
                            ins = t.matmul(PSF[4][0:64, 0:64],
                                           KW[:, qcol + 64 * h:qcol + 64 * (h + 1)],
                                           VTM[:, qcol + 64 * h:qcol + 64 * (h + 1)],
                                           start=(qt == 0), stop=(qt == 1))
                        MMG.inc(ins)

                    @block.vector
                    def _(v, h=h):
                        MMG.wait(v)
                        ins = v.tensor_copy(KV[:, 64 * h:64 * (h + 1)],
                                            PSF[4][0:64, 0:64])
                        BK[4].inc(ins)

                    @block.tensor
                    def _(t, h=h, cc=cc, row0=row0):
                        LPB[2 + h % 2].wait(t)
                        BK[0].wait(t)
                        ins = t.matmul(PSF[0][:, 0:256], ONES1b[0:1, :],
                                       WRB[0:1, row0:row0 + 256],
                                       start=True, stop=True)
                        MMG.inc(ins)

                    @block.vector
                    def _(v, h=h, cc=cc):
                        MMG.wait(v)
                        for kt in range(2):
                            wcol = 8 * (2 * cc + kt) + h
                            ins = v.tensor_scalar(
                                MSK[:, 256 * kt:256 * (kt + 1)],
                                PSF[0][:, 0:256],
                                WCT[:, wcol:wcol + 1], None, op0=ALU.add)
                        BK[0].inc(ins)
                        EV.inc(v.engine_nop())

                    @block.scalar
                    def _(s, h=h):
                        EV.wait(s)
                        s.activation(MSK[:, 0:512], MSK[:, 0:512], AF.Relu)
                        EA.inc(s.activation(MSK[:, 0:512], MSK[:, 0:512],
                                            AF.Exp, scale=-1.0))

                    @block.vector
                    def _(v, h=h):
                        EA.wait(v)
                        EV.inc(v.tensor_mul(MSK[:, 0:512], MSK[:, 0:512],
                                            STT[:, 0:512]))

                    for kt in range(2):
                        @block.tensor
                        def _(t, h=h, cc=cc, kt=kt, hp=hp, hc=hc):
                            if kt == 0:
                                EV.wait(t)
                            BK[1 + kt].wait(t)
                            ins = t.matmul(
                                PSF[1 + kt][:, 0:256],
                                KF[64 * hp:64 * hp + 64,
                                   TB * hc + 256 * cc + 128 * kt:
                                   TB * hc + 256 * cc + 128 * (kt + 1)],
                                RF[64 * hp:64 * hp + 64,
                                   TB * hc + 256 * cc:TB * hc + 256 * (cc + 1)],
                                start=True, stop=True,
                                tile_position=(64 * hp, 0))
                            MMG.inc(ins)

                        @block.vector
                        def _(v, kt=kt):
                            MMG.wait(v)
                            ins = v.tensor_mul(SCT[:, 256 * kt:256 * (kt + 1)],
                                               PSF[1 + kt][:, 0:256],
                                               MSK[:, 256 * kt:256 * (kt + 1)])
                            BK[1 + kt].inc(ins)
                            if kt == 1:
                                EV.inc(v.engine_nop())

                    @block.scalar
                    def _(s, h=h, row0=row0):
                        EV.wait(s)
                        EA.inc(s.activation(DGR[0:1, 0:256],
                                            WRB[0:1, row0:row0 + 256],
                                            AF.Exp, scale=-1.0))

                    @block.tensor
                    def _(t, h=h):
                        EA.wait(t)
                        BK[0].wait(t)
                        ins = t.matmul(PSF[0][0:64, 0:256], ONES1b[0:1, 0:64],
                                       DGR[0:1, 0:256], start=True, stop=True)
                        MMG.inc(ins)

                    @block.vector
                    def _(v, h=h, cc=cc, hp=hp, hc=hc):
                        MMG.wait(v)
                        ins = v.tensor_mul(
                            RSW[:, :],
                            RF[64 * hp:64 * hp + 64,
                               TB * hc + 256 * cc:TB * hc + 256 * (cc + 1)],
                            PSF[0][0:64, 0:256])
                        BK[0].inc(ins)
                        EV.inc(v.engine_nop())

                    @block.tensor
                    def _(t, h=h, cc=cc, hp=hp, hc=hc):
                        EV.wait(t)
                        BK[5].wait(t)
                        for kt in range(2):
                            qcol = 512 * (2 * cc) + 0
                            t.matmul(
                                PSF[5][64 * hp:64 * hp + 64, 0:256],
                                VTM[:, 512 * (2 * cc + kt) + 64 * h:
                                    512 * (2 * cc + kt) + 64 * (h + 1)],
                                SCT[:, 256 * kt:256 * (kt + 1)],
                                start=(kt == 0), stop=False,
                                tile_position=(0, 64 * hp))
                        ins = t.matmul(
                            PSF[5][64 * hp:64 * hp + 64, 0:256],
                            STB[:, 64 * h:64 * (h + 1)],
                            RSW[:, :], start=False, stop=True,
                            tile_position=(0, 64 * hp))
                        MMG.inc(ins)

                    @block.vector
                    def _(v, h=h, cc=cc, hp=hp, hc=hc):
                        sl = slice(TB * hc + 256 * cc, TB * hc + 256 * (cc + 1))
                        v.tensor_mul(SC2B[:, 0:256], RF[64 * hp:64 * hp + 64, sl],
                                     KF[64 * hp:64 * hp + 64, sl])
                        EV.inc(v.tensor_scalar(SC2B[:, 0:256], SC2B[:, 0:256],
                                               FAA[:, h:h + 1], None,
                                               op0=ALU.mult))

                    @block.tensor
                    def _(t, h=h):
                        EV.wait(t)
                        BK[3].wait(t)
                        ins = t.matmul(PSF[3][0:1, 0:256], ONESCb[0:64, 0:1],
                                       SC2B[:, 0:256], start=True, stop=True)
                        MMG.inc(ins)

                    @block.scalar
                    def _(s, h=h, row0=row0):
                        MMG.wait(s)
                        ins = s.activation(DGR[0:1, 256:512],
                                           WRB[0:1, row0 + 256:row0 + 512],
                                           AF.Exp)
                        EA.inc(ins)
                        WRC.inc(s.nop())

                    @block.vector
                    def _(v, h=h):
                        EA.wait(v)
                        ins = v.tensor_mul(DGR[0:1, 512:768], PSF[3][0:1, 0:256],
                                           DGR[0:1, 256:512])
                        BK[3].inc(ins)

                    @block.tensor
                    def _(t, h=h):
                        BK[3].wait(t)
                        BK[0].wait(t)
                        ins = t.matmul(PSF[0][0:64, 0:256], ONES1b[0:1, 0:64],
                                       DGR[0:1, 512:768], start=True, stop=True)
                        MMG.inc(ins)

                    @block.vector
                    def _(v, h=h, cc=cc, hp=hp, hc=hc):
                        MMG.wait(v)
                        sl = slice(TB * hc + 256 * cc, TB * hc + 256 * (cc + 1))
                        ins = v.tensor_mul(SC2B[:, 256:512],
                                           VF[64 * hp:64 * hp + 64, sl],
                                           PSF[0][0:64, 0:256])
                        BK[0].inc(ins)
                        ins = v.tensor_add(
                            YC[64 * hp:64 * hp + 64, 256 * hc:256 * (hc + 1)],
                            PSF[5][64 * hp:64 * hp + 64, 0:256],
                            SC2B[:, 256:512])
                        BK[5].inc(ins)
                        EV.inc(v.engine_nop())

                @block.scalar
                def _(s, cc=cc):
                    EV.wait(s)
                    EA.inc(s.activation(WSROW[0:1, 8:16], WSROW[0:1, 0:8], AF.Exp))

                @block.vector
                def _(v, cc=cc):
                    EA.wait(v)
                    ins = v.tensor_copy(
                        AP(WSROW[0:1, :].tensor, WSROW[0:1, :].offset + 16,
                           [[768, 1], [64, 8], [1, 64]]),
                        bcast_inner(WSROW[0:1, 8:16], 64))
                    EV.inc(ins)

                @block.tensor
                def _(t, cc=cc):
                    EV.wait(t)
                    BK[0].wait(t)
                    ins = t.matmul(PSF[0][0:64, 0:512], ONES1b[0:1, 0:64],
                                   WSROW[0:1, 16:528], start=True, stop=True)
                    MMG.inc(ins)

                @block.vector
                def _(v, cc=cc):
                    MMG.wait(v)
                    ins = v.tensor_mul(STATE[:, :], STATE[:, :], PSF[0][0:64, 0:512])
                    BK[0].inc(ins)
                    v.tensor_add(STATE[:, :], STATE[:, :], KV[:, :])
                    EV.inc(v.tensor_copy(STB[:, :], STATE[:, :]))

                if y_sink[0] == "dram":
                    y_out = y_sink[1]

                    @block.sync
                    def _(sy, cc=cc, c_glob=c_glob):
                        EV.wait(sy)
                        for j4 in range(4):
                            ST.inc(sy.dma_start(
                                out=y_out[j4, :, 256 * c_glob:256 * (c_glob + 1)],
                                in_=YC[:, 256 * j4:256 * (j4 + 1)]), 16)
                else:
                    ysh, YIW = y_sink[1], y_sink[2]

                    @block.gpsimd
                    def _(g, cc=cc, c_glob=c_glob):
                        EV.wait(g)
                        for j4 in range(4):
                            col = 4 * j4 + c_glob // 4
                            ST.inc(g.indirect_dma_start(
                                out=ysh[:, :],
                                out_offset=IndirectOffsetOnAxis(
                                    YIW[:, col:col + 1], 0),
                                in_=YC[:, 256 * j4:256 * (j4 + 1)],
                                in_offset=None,
                                element_offset=256 * (c_glob % 4)), 16)

        @block.sync
        def _(sy):
            ST.wait(sy)

    return {"ST": ST, "EV": EV}




def prep_B_inputs(inp, x1_full, core):
    b, hg = core // 2, core % 2
    w = {}
    w["x1b"] = np.ascontiguousarray(
        x1_full[4096 * b:4096 * (b + 1)].T.reshape(NCH, 128, 4096).astype(np.float32))
    cols = []
    for key in ["L1_attn_time_maa_x", "L1_attn_time_maa_w", "L1_attn_time_maa_k",
                "L1_attn_time_maa_v", "L1_attn_time_maa_r"]:
        cols.append(np.asarray(inp[key], np.float32).reshape(C).reshape(NCH, 128).T)
    td = np.asarray(inp["L1_attn_time_decay"], np.float32).reshape(C)
    tds = td[512 * hg:512 * (hg + 1)].reshape(4, 128).T
    tmcd = np.zeros((128, 6 * NCH), np.float32)
    for i, c_ in enumerate(cols):
        tmcd[:, NCH * i:NCH * (i + 1)] = c_
    tmcd[:, NCH * 5:NCH * 5 + 4] = tds
    w["tmcd"] = np.ascontiguousarray(tmcd)
    w["maw1"] = np.ascontiguousarray(np.asarray(inp["L1_attn_maa_w1"], np.float32))
    w["maw2"] = np.ascontiguousarray(
        np.asarray(inp["L1_attn_maa_w2"], np.float32).reshape(128, C))
    w["dcw1"] = np.ascontiguousarray(np.asarray(inp["L1_attn_decay_w1"], np.float32))
    w["dcw2"] = np.ascontiguousarray(
        np.asarray(inp["L1_attn_decay_w2"], np.float32)[:, 512 * hg:512 * (hg + 1)])
    faa = np.asarray(inp["L1_attn_time_faaaa"], np.float32).reshape(16, 64)
    w["faaT"] = np.ascontiguousarray(faa[8 * hg:8 * (hg + 1)].T)
    for nm, key in (("wr_", "L1_attn_Wr"), ("wk_", "L1_attn_Wk"),
                    ("wv_", "L1_attn_Wv")):
        w[nm] = np.ascontiguousarray(
            np.asarray(inp[key], np.float32)[:, 512 * hg:512 * (hg + 1)])
    hindm = np.zeros((128, 32), np.float32)
    for s4 in range(4):
        for p in range(128):
            hindm[p, 8 * s4 + 2 * s4 + p // 64] = 1.0 / 64
    w["hindm"] = hindm
    st = np.zeros((2, 128, 256), np.float32)
    for kt in range(2):
        for p in range(128):
            st[kt, p, 128 * kt + p + 1:] = 1.0
    w["strictT"] = st
    return w


def run_launch_B(inp, x1_full):
    nc = build_launch_B()
    in_maps = [prep_B_inputs(inp, x1_full, c) for c in range(N_CORES)]
    res = run_bass_kernel_spmd(nc, in_maps, core_ids=list(range(N_CORES)))
    y = np.empty((16384, C), np.float32)
    for core in range(N_CORES):
        b, hg = core // 2, core % 2
        yo = res.results[core]["y_out"]
        y[4096 * b:4096 * (b + 1), 512 * hg:512 * (hg + 1)] = \
            yo.reshape(512, 4096).T
    return y





def declare_C_drams(nc):
    dr = {}
    dr["lngb"] = nc.dram_tensor("lngb", [128, 2 * NCH], F32,
                                kind="ExternalInput").ap()
    dr["wo"] = nc.dram_tensor("wo", [C, C], F32, kind="ExternalInput").ap()
    dr["scrow"] = nc.dram_tensor("scrow", [1, 16384], F32,
                                 kind="ExternalInput").ap()
    dr["xn1f"] = nc.dram_tensor("xn1f", [16384, C], BF16,
                                kind="ExternalInput").ap()
    dr["w1e"] = nc.dram_tensor("w1e", [C, 4096], F32, kind="ExternalInput").ap()
    dr["b1e"] = nc.dram_tensor("b1e", [128, 32], F32, kind="ExternalInput").ap()
    dr["w2e"] = nc.dram_tensor("w2e", [4096, C], F32, kind="ExternalInput").ap()
    dr["b2e"] = nc.dram_tensor("b2e", [1, C], F32, kind="ExternalInput").ap()
    dr["triu"] = nc.dram_tensor("triu", [128, 128], F32,
                                kind="ExternalInput").ap()
    dr["s1_out"] = nc.dram_tensor("s1_out", [NCH, 128, P], F32,
                                  kind="ExternalOutput").ap()
    dr["co_out"] = nc.dram_tensor("co_out", [16384, C], BF16,
                                  kind="ExternalOutput").ap()
    dr["idx_d"] = nc.dram_tensor("idx_d", [2049, 2], F32).ap()
    return dr


def build_launch_C():
    nc = bass.Bass("TRN2", target_bir_lowering=False)
    dr = declare_C_drams(nc)
    dr["y_in"] = nc.dram_tensor("y_in", [NCH, 128, P], F32,
                                kind="ExternalInput").ap()
    alP = Alloc(nc)
    alC = Alloc(nc)
    with nc.Block() as block:
        emit_C_part(nc, block, alC, alP, dr, ("dram", dr["y_in"]))
    alC.close()
    alP.close()
    return nc


def emit_C_part(nc, block, alC, alP, dr, y_src, prezero=None):
    fused = y_src[0] != "dram"
    lngb = dr["lngb"]; wo = dr["wo"]; scrow = dr["scrow"]; xn1f = dr["xn1f"]
    w1e = dr["w1e"]; b1e = dr["b1e"]; w2e = dr["w2e"]; b2e = dr["b2e"]
    triu = dr["triu"]; s1_out = dr["s1_out"]; co_out = dr["co_out"]
    idx_d = dr["idx_d"]
    if fused:
        _, ysh, BARC, JN, yidxr = y_src
    else:
        y_in = y_src[1]
    sb, ps = alC.sb, alC.ps
    _tag = "c" if fused else ""

    def sem(n):
        return alP.sem(n + _tag)

    if not fused:
        CONSTc = alP.sb("CONSTc", [1, 4], F32)
    if fused:
        YIR = alP.sb("YIR", [128, 16], I32)

    YF = sb("YF", [128, NCH * HT], F32)
    S1 = sb("S1", [128, NCH * HT], F32)
    WSC = sb("WSC", [128, 8192], BF16)
    XG = sb("XG", [128, NCH * P], BF16)
    YLN = XG[:, 0:NCH * HT]
    XGR = sb("XGR", [128, 2048], BF16)
    XGF = sb("XGF", [128, 2048], F32)
    H1 = sb("H1", [128, 32 * 512], BF16)
    CONTRIB = sb("CONTRIB", [128, 4 * 1024], BF16)
    SRWc = sb("SRWc", [1, 3 * HT], F32)
    SCL = sb("SCL", [128, 128], F32)
    TAU = sb("TAU", [128, 8], F32)
    CMP = sb("CMP", [128, 128], F32)
    MSKC = sb("MSKC", [128, 128], F32)
    RNK = sb("RNK", [128, 128], F32)
    VALS = sb("VALS", [128, 256], F32)
    RIDXI = sb("RIDXI", [128, 128], I32)
    TOKI = sb("TOKI", [128, 16], I32)
    TOKF = sb("TOKF", [128, 32], F32)
    GCOL = sb("GCOL", [128, 16], F32)
    TOKV = sb("TOKV", [128, 128], I32)
    TRIU = sb("TRIU", [128, 128], F32)
    B1C = sb("B1C", [128, 32], F32)
    B2R = sb("B2R", [1, C], F32)
    ZB = sb("ZB", [128, 1024], F32)
    SC1c = sb("SC1c", [128, 2048], F32)
    ONES1c = sb("ONES1c", [1, 128], F32)
    ONESCc = sb("ONESCc", [128, 1], F32)
    LNG = sb("LNG", [128, 2 * NCH], F32)
    IDFc = sb("IDFc", [128, 128], F32)

    PSC = [ps(f"PCC{i}", [128, 512], F32) for i in range(8)]

    LX = sem("LX")
    LW = sem("LW")
    EV = sem("EV")
    EA = sem("EA")
    MMG = sem("MMG")
    ST = sem("ST")
    VE = sem("VE")
    GD = sem("GD")
    BK = [sem(f"BKC{i}") for i in range(8)]
    LP = [sem(f"LPC{i}") for i in range(4)]
    LQ = [sem(f"LQC{i}") for i in range(8)]

    SUMS = MEAN = MR = SRWc[0:1, 0 * HT:1 * HT]
    SSQ = VAR = SRWc[0:1, 1 * HT:2 * HT]
    RSCR = RSTD = SRWc[0:1, 2 * HT:3 * HT]

    def slab_ap(dram, r0, nr, c0, ncols, row_len):
        t = dram.tensor
        off = dram.offset + r0 * row_len + c0
        return AP(t, off, [[row_len, 128], [128 * row_len, max(nr // 128, 1)],
                           [1, ncols]])

    def ws_dst(col0, nkc, ncols):
        full = WSC[:, :]
        return AP(full.tensor, full.offset + col0,
                  [[8192, 128], [ncols, nkc], [1, ncols]])

    if True:

        if not fused:
            nc.const_aps.register(F32, LNEPS, CONSTc[0:1, 0:1])

        def edge(eng, inst):
            VE.inc(inst)
            VE.wait(eng)

        if fused:
            @block.tensor
            def _(t):
                t.wait_ge(JN.h, 4)

            @block.vector
            def _(v):
                v.wait_ge(JN.h, 4)

            @block.scalar
            def _(s):
                s.wait_ge(JN.h, 4)

            @block.sync
            def _(sy):
                sy.wait_ge(JN.h, 4)

        @block.gpsimd
        def _(g):
            if fused:
                g.wait_ge(JN.h, 4)
            g.memset(ONES1c[:, :], 1.0)
            g.memset(ONESCc[:, :], 1.0)
            g.memset(ZB[:, :], 0.0)
            if not fused:
                g.memset(CONSTc[0:1, 0:1], LNEPS)
            g.memset(TAU[:, 0:1], 0.0)
            g.memset(TAU[:, 1:2], 1.01)
            g.memset(TAU[:, 2:3], 0.505)
            LW.inc(g.iota(TOKV[:, :], pattern=[[1, 128]], base=0,
                          channel_multiplier=128))
            g.wait_ge(LW.h, 1)
            LW.inc(g.memset(IDFc[:, :], 1.0))
            g.wait_ge(LW.h, 2)
            LW.inc(g.affine_select(IDFc[:, :], IDFc[:, :], pattern=[[1, 128]],
                                   compare_op=ALU.is_equal, fill=0.0,
                                   channel_multiplier=-1))
            LP[0].inc(g.dma_start(out=ws_dst(0, NCH, C),
                                  in_=slab_ap(wo, 0, C, 0, C, C)), 16)

        @block.sync
        def _(sy):
            LX.inc(sy.dma_start(out=TRIU[:, :], in_=triu[:, :]), 16)
            LX.inc(sy.dma_start(out=LNG[:, :], in_=lngb[:, :]), 16)
            LX.inc(sy.dma_start(out=B1C[:, :], in_=b1e[:, :]), 16)
            LX.inc(sy.dma_start(out=B2R[0:1, :], in_=b2e[0:1, :]), 16)
            LX.inc(sy.dma_start(
                out=AP(SCL[:, :].tensor, SCL[:, :].offset,
                       [[128, 128], [1, 128]]),
                in_=scrow[0:1, :]), 16)
            if fused:
                LX.inc(sy.dma_start(out=YIR[:, :], in_=yidxr[:, :]), 16)

        for h in range(2):
            t0 = h * HT

            if not fused:
                @block.sync
                def _(sy, h=h, t0=t0):
                    EV.wait(sy)
                    EA.wait(sy)
                    for kc in range(NCH):
                        LX.inc(sy.dma_start(out=YF[:, HT * kc:HT * (kc + 1)],
                                            in_=y_in[kc, :, t0:t0 + HT]), 16)
            else:
                @block.gpsimd
                def _(g, h=h):
                    EV.wait(g)
                    EA.wait(g)
                    if h == 0:
                        g.wait_ge(BARC.h, 1)
                        g.wait_ge(LX.h, 96)
                    for kc in range(NCH):
                        LX.inc(g.indirect_dma_start(
                            out=YF[:, HT * kc:HT * (kc + 1)], out_offset=None,
                            in_=ysh[:, :],
                            in_offset=IndirectOffsetOnAxis(
                                YIR[:, 2 * kc + h:2 * kc + h + 1], 0)), 16)
            lx_t = LX.n

            for tt in range(2):
                for grp in range(2):
                    @block.scalar
                    def _(s, tt=tt, grp=grp, lx_t=lx_t):
                        s.wait_ge(LX.h, lx_t)
                        EV.wait(s)
                        if grp == 1 or tt == 1:
                            MMG.wait(s)
                        for kk in range(4):
                            kc = grp * 4 + kk
                            ins = s.activation(
                                SC1c[:, 512 * kk:512 * (kk + 1)],
                                YF[:, HT * kc + 512 * tt:HT * kc + 512 * tt + 512],
                                AF.Square)
                        EA.inc(ins)

                    @block.tensor
                    def _(t, tt=tt, grp=grp):
                        EA.wait(t)
                        if tt + grp == 0:
                            BK[4].wait(t)
                            BK[5].wait(t)
                        for kk in range(4):
                            kc = grp * 4 + kk
                            t.matmul(PSC[4][0:1, 0:512], ONESCc[:, 0:1],
                                     YF[:, HT * kc + 512 * tt:
                                        HT * kc + 512 * tt + 512],
                                     start=(kc == 0), stop=(kc == 7))
                            ins = t.matmul(PSC[5][0:1, 0:512], ONESCc[:, 0:1],
                                           SC1c[:, 512 * kk:512 * (kk + 1)],
                                           start=(kc == 0), stop=(kc == 7))
                        MMG.inc(ins)

                @block.vector
                def _(v, tt=tt):
                    MMG.wait(v)
                    v.tensor_copy(SUMS[0:1, 512 * tt:512 * (tt + 1)],
                                  PSC[4][0:1, 0:512])
                    ins = v.tensor_copy(SSQ[0:1, 512 * tt:512 * (tt + 1)],
                                        PSC[5][0:1, 0:512])
                    BK[4].inc(ins)
                    BK[5].inc(v.engine_nop())

            @block.vector
            def _(v):
                EV.wait(v)
                MMG.wait(v)
                edge(v, v.tensor_scalar_mul(MEAN, SUMS, 1.0 / C))
                edge(v, v.tensor_scalar_mul(VAR, SSQ, 1.0 / C))
                edge(v, v.tensor_mul(RSCR, MEAN, MEAN))
                EV.inc(v.tensor_sub(VAR, VAR, RSCR))

            @block.scalar
            def _(s):
                EV.wait(s)
                s.wait_ge(LW.h, 3)
                edge(s, s.activation(RSCR, VAR, AF.Ln, bias=LNEPS))
                EA.inc(s.activation(RSTD, RSCR, AF.Exp, scale=-0.5))

            @block.vector
            def _(v):
                EA.wait(v)
                EV.inc(v.tensor_mul(MR, MEAN, RSTD))

            for tt in range(2):
                @block.tensor
                def _(t, tt=tt):
                    EV.wait(t)
                    BK[4].wait(t)
                    BK[5].wait(t)
                    t.matmul(PSC[4][:, 0:512], ONES1c[0:1, :],
                             RSTD[0:1, 512 * tt:512 * (tt + 1)],
                             start=True, stop=True)
                    MMG.inc(t.matmul(PSC[5][:, 0:512], ONES1c[0:1, :],
                                     MR[0:1, 512 * tt:512 * (tt + 1)],
                                     start=True, stop=True))

                @block.vector
                def _(v, tt=tt):
                    MMG.wait(v)
                    v.wait_ge(LX.h, 80)
                    for kc in range(NCH):
                        sl = slice(HT * kc + 512 * tt, HT * kc + 512 * tt + 512)
                        v.tensor_mul(SC1c[:, 0:512], YF[:, sl], PSC[4][:, 0:512])
                        v.tensor_sub(SC1c[:, 512:1024], SC1c[:, 0:512],
                                     PSC[5][:, 0:512])
                        v.tensor_scalar(SC1c[:, 0:512], SC1c[:, 512:1024],
                                        LNG[:, kc:kc + 1], None, op0=ALU.mult)
                        ins = v.tensor_scalar(YLN[:, sl], SC1c[:, 0:512],
                                              LNG[:, NCH + kc:NCH + kc + 1],
                                              None, op0=ALU.add)
                    BK[4].inc(ins)
                    BK[5].inc(v.engine_nop())
                    EV.inc(v.engine_nop())

            for ds in range(NCH):
                for tt in range(2):
                    gi = 2 * ds + tt
                    b = gi % 2

                    @block.tensor
                    def _(t, ds=ds, tt=tt, b=b, gi=gi):
                        if gi == 0:
                            EV.wait(t)
                            LP[0].wait(t)
                        BK[b].wait(t)
                        for kc in range(NCH):
                            ins = t.matmul(
                                PSC[b][:, 0:512],
                                WSC[:, C * kc + 128 * ds:C * kc + 128 * (ds + 1)],
                                YLN[:, HT * kc + 512 * tt:HT * kc + 512 * tt + 512],
                                start=(kc == 0), stop=(kc == 7))
                        MMG.inc(ins)

                    @block.vector
                    def _(v, ds=ds, tt=tt, b=b):
                        MMG.wait(v)
                        if ds == 0 and tt == 0:
                            ST.wait(v)
                        ins = v.tensor_copy(
                            S1[:, HT * ds + 512 * tt:HT * ds + 512 * (tt + 1)],
                            PSC[b][:, 0:512])
                        BK[b].inc(ins)
                        if ds == NCH - 1 and tt == 1:
                            EV.inc(v.engine_nop())

            @block.sync
            def _(sy, h=h, t0=t0):
                EV.wait(sy)
                for kc in range(NCH):
                    ST.inc(sy.dma_start(out=s1_out[kc, :, t0:t0 + HT],
                                        in_=S1[:, HT * kc:HT * (kc + 1)]), 16)

        if prezero is None:
            @block.gpsimd
            def _(g):
                g.wait_ge(LW.h, 1)
                for r in range(32):
                    GD.inc(g.dma_start(
                        out=co_out[512 * r:512 * (r + 1), :],
                        in_=AP(ZB[:, :].tensor, ZB[:, :].offset,
                               [[1024, 128], [0, 4], [1, 1024]])), 16)
        gd_zero = GD.n

        for it in range(38):
            @block.vector
            def _(v, it=it):
                if it == 0:
                    v.wait_ge(LW.h, 3)
                    LX.wait(v)
                edge(v, v.tensor_scalar(CMP[:, :], SCL[:, :], TAU[:, 2:3],
                                        None, op0=ALU.is_ge))
                EV.inc(v.reduce_sum(TAU[:, 4:5], CMP[:, :], axis=AX.X))

            @block.tensor
            def _(t, it=it):
                EV.wait(t)
                BK[2].wait(t)
                MMG.inc(t.matmul(PSC[2][0:1, 0:1], TAU[:, 4:5], ONESCc[:, 0:1],
                                 start=True, stop=True))

            @block.vector
            def _(v, it=it):
                MMG.wait(v)
                ins = v.tensor_copy(TAU[0:1, 5:6], PSC[2][0:1, 0:1])
                BK[2].inc(ins)
                edge(v, v.engine_nop())

            @block.tensor
            def _(t, it=it):
                VE.wait(t)
                BK[3].wait(t)
                MMG.inc(t.matmul(PSC[3][:, 0:1], ONES1c[0:1, :], TAU[0:1, 5:6],
                                 start=True, stop=True))

            @block.vector
            def _(v, it=it):
                MMG.wait(v)
                ins = v.tensor_scalar(TAU[:, 3:4], PSC[3][:, 0:1], 2048.0, None,
                                      op0=ALU.is_ge)
                BK[3].inc(ins)
                edge(v, v.engine_nop())
                edge(v, v.tensor_sub(TAU[:, 6:7], TAU[:, 2:3], TAU[:, 0:1]))
                edge(v, v.tensor_mul(TAU[:, 6:7], TAU[:, 6:7], TAU[:, 3:4]))
                edge(v, v.tensor_add(TAU[:, 0:1], TAU[:, 0:1], TAU[:, 6:7]))
                edge(v, v.tensor_sub(TAU[:, 6:7], TAU[:, 1:2], TAU[:, 2:3]))
                edge(v, v.tensor_mul(TAU[:, 6:7], TAU[:, 6:7], TAU[:, 3:4]))
                edge(v, v.tensor_add(TAU[:, 1:2], TAU[:, 2:3], TAU[:, 6:7]))
                edge(v, v.tensor_add(TAU[:, 2:3], TAU[:, 0:1], TAU[:, 1:2]))
                edge(v, v.tensor_scalar_mul(TAU[:, 2:3], TAU[:, 2:3], 0.5))
                EV.inc(v.engine_nop())

        @block.vector
        def _(v):
            EV.wait(v)
            edge(v, v.tensor_scalar(MSKC[:, :], SCL[:, :], TAU[:, 0:1], None,
                                    op0=ALU.is_ge))
            edge(v, v.tensor_tensor_scan(RNK[:, :], MSKC[:, :], MSKC[:, :],
                                         0.0, ALU.add, ALU.bypass))
            edge(v, v.tensor_sub(RNK[:, :], RNK[:, :], MSKC[:, :]))
            EV.inc(v.reduce_sum(TAU[:, 7:8], MSKC[:, :], axis=AX.X))

        @block.tensor
        def _(t):
            EV.wait(t)
            t.wait_ge(LX.h, 80)
            BK[2].wait(t)
            MMG.inc(t.matmul(PSC[2][:, 0:1], TRIU[:, :], TAU[:, 7:8],
                             start=True, stop=True))

        @block.vector
        def _(v):
            MMG.wait(v)
            ins = v.tensor_copy(TAU[:, 6:7], PSC[2][:, 0:1])
            BK[2].inc(ins)
            edge(v, v.engine_nop())
            edge(v, v.tensor_scalar(RNK[:, :], RNK[:, :], TAU[:, 6:7], None,
                                    op0=ALU.add))
            edge(v, v.tensor_mul(RNK[:, :], RNK[:, :], MSKC[:, :]))
            edge(v, v.tensor_scalar(CMP[:, :], MSKC[:, :], -300000.0, 300000.0,
                                    op0=ALU.mult, op1=ALU.add))
            edge(v, v.tensor_add(RNK[:, :], RNK[:, :], CMP[:, :]))
            edge(v, v.tensor_copy(RIDXI[:, :], RNK[:, :]))
            edge(v, v.tensor_copy(
                AP(VALS[:, :].tensor, VALS[:, :].offset, [[256, 128], [2, 128]]),
                TOKV[:, :]))
            EV.inc(v.tensor_copy(
                AP(VALS[:, :].tensor, VALS[:, :].offset + 1, [[256, 128], [2, 128]]),
                SCL[:, :]))

        @block.gpsimd
        def _(g):
            EV.wait(g)
            if prezero is None:
                g.wait_ge(GD.h, gd_zero)
            else:
                g.wait_ge(prezero[0].h, prezero[1])
            rbc = g.alloc_register("rbc")
            g.reg_mov(rbc, 2047)
            for f in range(128):
                GD.inc(g.indirect_dma_start(
                    out=idx_d[:, :],
                    out_offset=IndirectOffsetOnAxis(RIDXI[:, f:f + 1], 0),
                    in_=VALS[:, 2 * f:2 * f + 2], in_offset=None,
                    bounds_check=rbc, oob_is_err=False), 16)
            g.wait_ge(GD.h, GD.n)
            GD.inc(g.dma_start(
                out=TOKF[:, :],
                in_=AP(idx_d.tensor, idx_d.offset, [[32, 128], [1, 32]])), 16)

        @block.vector
        def _(v):
            GD.wait(v)
            edge(v, v.tensor_copy(
                TOKI[:, :],
                AP(TOKF[:, :].tensor, TOKF[:, :].offset, [[32, 128], [2, 16]])))
            EV.inc(v.tensor_copy(
                GCOL[:, :],
                AP(TOKF[:, :].tensor, TOKF[:, :].offset + 1, [[32, 128], [2, 16]])))

        ev_g0 = EV.n
        for g16 in range(16):
            @block.gpsimd
            def _(g, g16=g16):
                if g16 == 0:
                    EV.wait(g)
                elif g16 >= 2:
                    g.wait_ge(EV.h, ev_g0 + g16 - 1)
                LP[1 + g16 % 2].inc(g.indirect_dma_start(
                    out=XGR[:, 1024 * (g16 % 2):1024 * (g16 % 2) + 1024],
                    out_offset=None,
                    in_=xn1f[:, :],
                    in_offset=IndirectOffsetOnAxis(TOKI[:, g16:g16 + 1], 0)), 16)

            @block.vector
            def _(v, g16=g16):
                LP[1 + g16 % 2].wait(v)
                if g16 >= 2:
                    MMG.wait(v, MMG.n - 1)
                EV.inc(v.tensor_copy(XGF[:, 1024 * (g16 % 2):
                                         1024 * (g16 % 2) + 1024],
                                     XGR[:, 1024 * (g16 % 2):
                                         1024 * (g16 % 2) + 1024]))

            @block.tensor
            def _(t, g16=g16):
                t.wait_ge(EV.h, ev_g0 + g16 + 1)
                BK[2].wait(t)
                BK[3].wait(t)
                sl0 = 1024 * (g16 % 2)
                for kc in range(NCH):
                    b = 2 + kc // 4
                    ins = t.transpose(
                        PSC[b][:, 128 * (kc % 4):128 * (kc % 4) + 128],
                        XGF[:, sl0 + 128 * kc:sl0 + 128 * (kc + 1)],
                        IDFc[:, :])
                MMG.inc(ins)

            @block.vector
            def _(v, g16=g16):
                MMG.wait(v)
                for kc in range(NCH):
                    b = 2 + kc // 4
                    ins = v.tensor_copy(
                        XG[:, P * kc + 128 * g16:P * kc + 128 * (g16 + 1)],
                        PSC[b][:, 128 * (kc % 4):128 * (kc % 4) + 128])
                BK[2].inc(ins)
                BK[3].inc(v.engine_nop())

        w1_free = {}
        w2_free = {}
        for ps_ in range(4):
            tok0 = 512 * ps_

            for ds32 in range(32):
                wslot = 1024 * (ds32 % 4)

                @block.gpsimd
                def _(g, ps_=ps_, ds32=ds32, wslot=wslot):
                    g.wait_ge(MMG.h, w1_free.get((ps_, ds32), MMG.n - 1))
                    LQ[ds32 % 4].inc(
                        g.dma_start(out=ws_dst(wslot, NCH, 128),
                                    in_=slab_ap(w1e, 0, C, 128 * ds32, 128, 4096)),
                        16)

                @block.tensor
                def _(t, ps_=ps_, ds32=ds32, wslot=wslot):
                    LQ[ds32 % 4].wait(t)
                    if ps_ == 0 and ds32 == 0:
                        BK[2].wait(t)
                        BK[3].wait(t)
                    b = ds32 % 2
                    BK[b].wait(t)
                    for kc in range(NCH):
                        ins = t.matmul(
                            PSC[b][:, 0:512],
                            WSC[:, wslot + 128 * kc:wslot + 128 * (kc + 1)],
                            XG[:, P * kc + tok0:P * kc + tok0 + 512],
                            start=(kc == 0), stop=(kc == 7))
                    MMG.inc(ins)
                nxt = (ps_, ds32 + 4) if ds32 + 4 < 32 else (ps_ + 1, ds32 % 4)
                w1_free[nxt] = MMG.n

                @block.scalar
                def _(s, ds32=ds32):
                    MMG.wait(s)
                    s.wait_ge(LX.h, 80)
                    b = ds32 % 2
                    ins = s.activation(H1[:, 512 * ds32:512 * (ds32 + 1)],
                                       PSC[b][:, 0:512], AF.Relu,
                                       bias=B1C[:, ds32:ds32 + 1])
                    BK[b].inc(ins)
                    if ds32 == 31:
                        EA.inc(s.nop())

            @block.tensor
            def _(t, ps_=ps_):
                EA.wait(t)
                t.wait_ge(LX.h, 80)
                for b in range(8):
                    BK[b].wait(t)
                for tt4 in range(4):
                    for nhh in range(2):
                        b = 2 * tt4 + nhh
                        ins = t.matmul(PSC[b][:, 0:512],
                                       ONES1c[0:1, :],
                                       B2R[0:1, 512 * nhh:512 * (nhh + 1)],
                                       start=True, stop=False)
                MMG.inc(ins)

            for kc32 in range(32):
                wslot = 4096 + 1024 * (kc32 % 4)

                @block.gpsimd
                def _(g, ps_=ps_, kc32=kc32, wslot=wslot):
                    g.wait_ge(MMG.h, w2_free.get((ps_, kc32), MMG.n - 1))
                    LQ[4 + kc32 % 4].inc(
                        g.dma_start(out=ws_dst(wslot, 1, C),
                                    in_=slab_ap(w2e, 128 * kc32, 128, 0, C, C)),
                        16)

                @block.tensor
                def _(t, ps_=ps_, kc32=kc32, wslot=wslot):
                    LQ[4 + kc32 % 4].wait(t)
                    for tt4 in range(4):
                        for nhh in range(2):
                            b = 2 * tt4 + nhh
                            ins = t.matmul(
                                PSC[b][:, 0:512],
                                H1[:, 512 * kc32 + 128 * tt4:
                                   512 * kc32 + 128 * (tt4 + 1)],
                                WSC[:, wslot + 512 * nhh:wslot + 512 * (nhh + 1)],
                                start=False, stop=(kc32 == 31))
                    MMG.inc(ins)
                nxt = (ps_, kc32 + 4) if kc32 + 4 < 32 else (ps_ + 1, kc32 % 4)
                w2_free[nxt] = MMG.n

            for tt4 in range(4):
                @block.vector
                def _(v, ps_=ps_, tt4=tt4):
                    MMG.wait(v)
                    if ps_ > 0 and tt4 == 0:
                        GD.wait(v)
                    for nhh in range(2):
                        b = 2 * tt4 + nhh
                        ins = v.tensor_scalar(
                            CONTRIB[:, 1024 * tt4 + 512 * nhh:
                                    1024 * tt4 + 512 * (nhh + 1)],
                            PSC[b][:, 0:512],
                            GCOL[:, 4 * ps_ + tt4:4 * ps_ + tt4 + 1],
                            None, op0=ALU.mult)
                        BK[b].inc(ins)
                    EV.inc(v.engine_nop())

                @block.gpsimd
                def _(g, ps_=ps_, tt4=tt4):
                    EV.wait(g)
                    GD.inc(g.indirect_dma_start(
                        out=co_out[:, :],
                        out_offset=IndirectOffsetOnAxis(
                            TOKI[:, 4 * ps_ + tt4:4 * ps_ + tt4 + 1], 0),
                        in_=CONTRIB[:, 1024 * tt4:1024 * (tt4 + 1)],
                        in_offset=None), 16)

        @block.gpsimd
        def _(g):
            GD.wait(g)

        @block.sync
        def _(sy):
            ST.wait(sy)

    return {"ST": ST, "GD": GD}



def build_launch_BC():
    nc = bass.Bass("TRN2", target_bir_lowering=False)
    drB = declare_B_drams(nc)
    drC = declare_C_drams(nc)
    yidxw = nc.dram_tensor("yidxw", [128, 16], I32, kind="ExternalInput").ap()
    yidxr = nc.dram_tensor("yidxr", [128, 16], I32, kind="ExternalInput").ap()
    fidx = nc.dram_tensor("fidx", [128, 1], I32, kind="ExternalInput").ap()
    sidx = nc.dram_tensor("sidx", [128, 1], I32, kind="ExternalInput").ap()
    fval = nc.dram_tensor("fval", [128, 4], I32, kind="ExternalInput").ap()
    rid = nc.dram_tensor("rid", [1, 4], I32, kind="ExternalInput").ap()
    ysh = nc.dram_tensor("ysh", [NCH * 128 * 4, 1024], F32,
                         addr_space="Shared").ap()
    flgsh = nc.dram_tensor("flgsh", [256, 4], I32, addr_space="Shared").ap()
    zcon = nc.dram_tensor("zcon", [512, C], BF16).ap()

    alP = Alloc(nc)
    JN = alP.sem("JN")
    BARC = alP.sem("BARC")
    FW = alP.sem("FWb")
    PSEM = alP.sem("PSb")
    YIW = alP.sb("YIW", [128, 16], I32)
    FID = alP.sb("FID", [128, 1], I32)
    SID = alP.sb("SID", [128, 1], I32)
    FVAL = alP.sb("FVAL", [128, 4], I32)
    RV = alP.sb("RV", [1, 4], I32)
    FLG = alP.sb("FLG", [128, 4], I32)
    GDZ = alP.sem("GDZ")
    ZBP = alP.sb("ZBP", [128, 1024], F32)

    with nc.Block() as block:
        @block.gpsimd
        def _(g):
            GDZ.inc(g.memset(ZBP[:, :], 0.0))
            g.wait_ge(GDZ.h, 1)
            GDZ.inc(g.dma_start(
                out=zcon[:, :],
                in_=AP(ZBP[:, :].tensor, ZBP[:, :].offset,
                       [[1024, 128], [0, 4], [1, 1024]])), 16)
            FW.inc(g.dma_start(out=YIW[:, :], in_=yidxw[:, :]), 16)
            FW.inc(g.dma_start(out=FID[:, :], in_=fidx[:, :]), 16)
            FW.inc(g.dma_start(out=SID[:, :], in_=sidx[:, :]), 16)
            FW.inc(g.dma_start(out=FVAL[:, :], in_=fval[:, :]), 16)
            FW.inc(g.dma_start(out=RV[0:1, :], in_=rid[0:1, :]), 16)
            FW.wait(g)

        @block.scalar
        def _(s):
            s.wait_ge(GDZ.h, 17)
            co_out_bc = drC["co_out"]
            for r in range(32):
                GDZ.inc(s.dma_start(out=co_out_bc[512 * r:512 * (r + 1), :],
                                    in_=zcon[:, :]), 16)

        alB = Alloc(nc)
        SB = emit_B_part(nc, block, alB, alP, drB, ("shared", ysh, YIW))

        @block.tensor
        def _(t):
            JN.inc(t.nop())

        @block.vector
        def _(v):
            JN.inc(v.engine_nop())

        @block.scalar
        def _(s):
            JN.inc(s.nop())

        @block.sync
        def _(sy):
            JN.inc(sy.nop())

        alB.close()

        @block.gpsimd
        def _(g):
            SB["ST"].wait(g)
            FW.inc(g.indirect_dma_start(
                out=flgsh[:, :],
                out_offset=IndirectOffsetOnAxis(FID[:, 0:1], 0),
                in_=FVAL[:, :], in_offset=None), 16)
            FW.wait(g)
            with (
                g.register("cnt") as cnt,
                g.register("go") as go,
                g.register("val") as val,
                g.register("rr") as rr,
            ):
                g.reg_mov(cnt, 0)
                g.reg_mov(go, 1)
                g.reg_load(rr, RV[0:1, 0:1])
                with g.While(go):
                    g.reg_add(cnt, cnt, 16)
                    g.indirect_dma_start(
                        out=FLG[:, :], out_offset=None,
                        in_=flgsh[:, :],
                        in_offset=IndirectOffsetOnAxis(SID[:, 0:1], 0)
                        ).then_inc(PSEM.h, 16)
                    g.wait_ge(PSEM.h, cnt)
                    g.reg_load(val, FLG[0:1, 0:1])
                    g.reg_alu(go, val, rr, ALU.not_equal)
            BARC.inc(g.nop())

        alC = Alloc(nc)
        emit_C_part(nc, block, alC, alP, drC,
                    ("shared", ysh, BARC, JN, yidxr),
                    prezero=(GDZ, 17 + 32 * 16))
        alC.close()
    alP.close()
    return nc


_BC_RUN_COUNTER = [int(time.time() * 1000.0) & 0x3FFFFFFF]


def prep_BC_inputs(inp, x1_full, xn1_bf, scores, core, runid):
    hg = core % 2
    w = prep_B_inputs(inp, x1_full, core)
    w.update(prep_C_inputs(inp, None, xn1_bf, scores, core))
    p = np.arange(128)
    yidxw = np.zeros((128, 16), np.int32)
    yidxr = np.zeros((128, 16), np.int32)
    for j4 in range(4):
        for tb in range(4):
            yidxw[:, 4 * j4 + tb] = ((4 * hg + j4) * 128 + p) * 4 + tb
    for kc in range(NCH):
        for h in range(2):
            yidxr[:, 2 * kc + h] = (kc * 128 + p) * 4 + 2 * hg + h
    w["yidxw"] = yidxw
    w["yidxr"] = yidxr
    w["fidx"] = (hg * 128 + p).reshape(128, 1).astype(np.int32)
    w["sidx"] = ((1 - hg) * 128 + p).reshape(128, 1).astype(np.int32)
    w["fval"] = np.full((128, 4), runid, np.int32)
    w["rid"] = np.full((1, 4), runid, np.int32)
    return w


def run_launch_BC(inp, x1_full, xn1_bf, scores):
    _BC_RUN_COUNTER[0] += 1
    runid = _BC_RUN_COUNTER[0]
    nc = build_launch_BC()
    in_maps = [prep_BC_inputs(inp, x1_full, xn1_bf, scores, c, runid)
               for c in range(N_CORES)]
    res = run_bass_kernel_spmd(nc, in_maps, core_ids=list(range(N_CORES)))
    s1 = np.empty((16384, C), np.float32)
    co_list = []
    for c in range(N_CORES):
        r = res.results[c]
        s1[P * c:P * (c + 1)] = r["s1_out"].reshape(C, P).T
        co_list.append(r["co_out"])
    return s1, co_list


def prep_C_inputs(inp, y_full, xn1_bf, scores, core):
    e = core
    w = {}
    if y_full is not None:
        w["y_in"] = np.ascontiguousarray(
            y_full[P * e:P * (e + 1)].T.reshape(NCH, 128, P).astype(np.float32))
    lngb = np.zeros((128, 2 * NCH), np.float32)
    lngb[:, 0:NCH] = np.asarray(inp["L1_attn_ln_g"],
                                np.float32).reshape(NCH, 128).T
    lngb[:, NCH:2 * NCH] = np.asarray(inp["L1_attn_ln_b"],
                                      np.float32).reshape(NCH, 128).T
    w["lngb"] = lngb
    w["wo"] = np.ascontiguousarray(np.asarray(inp["L1_attn_Wo"], np.float32))
    w["scrow"] = np.ascontiguousarray(
        scores[:, e].reshape(1, 16384).astype(np.float32))
    w["xn1f"] = np.ascontiguousarray(xn1_bf)
    w["w1e"] = np.ascontiguousarray(np.asarray(inp["L1_ffn_w1"], np.float32)[e])
    w["b1e"] = np.ascontiguousarray(
        np.asarray(inp["L1_ffn_b1"], np.float32)[e].reshape(32, 128).T)
    w["w2e"] = np.ascontiguousarray(np.asarray(inp["L1_ffn_w2"], np.float32)[e])
    w["b2e"] = np.ascontiguousarray(
        np.asarray(inp["L1_ffn_b2"], np.float32)[e].reshape(1, C))
    w["triu"] = np.ascontiguousarray(np.triu(np.ones((128, 128), np.float32), 1))
    return w


def run_launch_C(inp, y_full, xn1_bf, scores):
    nc = build_launch_C()
    in_maps = [prep_C_inputs(inp, y_full, xn1_bf, scores, c)
               for c in range(N_CORES)]
    res = run_bass_kernel_spmd(nc, in_maps, core_ids=list(range(N_CORES)))
    s1 = np.empty((16384, C), np.float32)
    co_list = []
    for c in range(N_CORES):
        r = res.results[c]
        s1[P * c:P * (c + 1)] = r["s1_out"].reshape(C, P).T
        co_list.append(r["co_out"])
    return s1, co_list



def build_launch_D():
    nc = bass.Bass("TRN2", target_bir_lowering=False)

    x1d = nc.dram_tensor("x1d", [P, C], F32, kind="ExternalInput").ap()
    s1d = nc.dram_tensor("s1d", [P, C], F32, kind="ExternalInput").ap()
    cod = nc.dram_tensor("cod", [8, P, C], BF16, kind="ExternalInput").ap()
    outd = nc.dram_tensor("outd", [P, C], F32, kind="ExternalOutput").ap()

    al = Alloc(nc)
    sb, sem = al.sb, al.sem
    CO = [sb(f"CO{sl}", [128, 8 * C], BF16) for sl in range(2)]
    X1T = [sb(f"X1T{sl}", [128, C], F32) for sl in range(2)]
    S1T = [sb(f"S1T{sl}", [128, C], F32) for sl in range(2)]
    ACC = [sb(f"ACC{sl}", [128, C], F32) for sl in range(2)]
    GT = [sb(f"GT{sl}", [128, C], F32) for sl in range(2)]
    OUT = [sb(f"OUT{sl}", [128, C], F32) for sl in range(2)]
    LX = sem("LXd")
    EV = sem("EVd")
    EA = sem("EAd")
    ST = sem("STd")

    with nc.Block() as block:
        for t in range(16):
            r0 = 128 * t
            sl = t % 2

            @block.sync
            def _(sy, t=t, r0=r0, sl=sl):
                sy.wait_ge(EV.h, max(0, 2 * (t - 1)))
                for e in range(8):
                    LX.inc(sy.dma_start(out=CO[sl][:, C * e:C * (e + 1)],
                                        in_=cod[e, r0:r0 + 128, :]), 16)
                LX.inc(sy.dma_start(out=X1T[sl][:, :],
                                    in_=x1d[r0:r0 + 128, :]), 16)
                LX.inc(sy.dma_start(out=S1T[sl][:, :],
                                    in_=s1d[r0:r0 + 128, :]), 16)

            @block.vector
            def _(v, t=t, sl=sl):
                LX.wait(v)
                v.tensor_add(ACC[sl][:, :], CO[sl][:, 0:C], CO[sl][:, C:2 * C])
                for e in range(2, 8):
                    ins = v.tensor_add(ACC[sl][:, :], ACC[sl][:, :],
                                       CO[sl][:, C * e:C * (e + 1)])
                EV.inc(ins)

            @block.scalar
            def _(s, t=t, sl=sl):
                EV.wait(s)
                EA.inc(s.activation(GT[sl][:, :], ACC[sl][:, :], AF.Silu))

            @block.vector
            def _(v, t=t, sl=sl):
                EA.wait(v)
                v.wait_ge(ST.h, 16 * max(0, t - 1))
                v.tensor_mul(OUT[sl][:, :], S1T[sl][:, :], GT[sl][:, :])
                EV.inc(v.tensor_add(OUT[sl][:, :], OUT[sl][:, :],
                                    X1T[sl][:, :]))

            @block.sync
            def _(sy, t=t, r0=r0, sl=sl):
                EV.wait(sy)
                ST.inc(sy.dma_start(out=outd[r0:r0 + 128, :],
                                    in_=OUT[sl][:, :]), 16)

        @block.sync
        def _(sy):
            ST.wait(sy)

    al.close()
    return nc


def run_launch_D(x1_full, s1_full, co_list):
    nc = build_launch_D()
    in_maps = []
    for d in range(N_CORES):
        sl = slice(P * d, P * (d + 1))
        in_maps.append({
            "x1d": np.ascontiguousarray(x1_full[sl]),
            "s1d": np.ascontiguousarray(s1_full[sl]),
            "cod": np.ascontiguousarray(
                np.stack([co[sl] for co in co_list], axis=0)),
        })
    res = run_bass_kernel_spmd(nc, in_maps, core_ids=list(range(N_CORES)))
    out = np.empty((16384, C), np.float32)
    for d in range(N_CORES):
        out[P * d:P * (d + 1)] = res.results[d]["outd"]
    return out


def kernel(**inputs):
    x1, xn1, scores = run_launch_A(inputs)
    s1, co_list = run_launch_BC(inputs, x1, xn1, scores)
    out = run_launch_D(x1, s1, co_list)
    return out.reshape(4, 4096, C)



# version 08
# speedup vs baseline: 1.2341x; 1.2341x over previous
import os
import time
import numpy as np
import ml_dtypes
import concourse.bass as bass
import concourse.mybir as mybir
from concourse.ap import AP
from concourse.bass_utils import run_bass_kernel_spmd
from concourse.bass_core_rust import IndirectOffsetOnAxis

F32 = mybir.dt.float32
BF16 = mybir.dt.bfloat16
I32 = mybir.dt.int32
U32 = mybir.dt.uint32
AF = mybir.ActivationFunctionType
ALU = mybir.AluOpType
AX = mybir.AxisListType

BF = ml_dtypes.bfloat16

N_CORES = 8
P = 2048
C = 1024
NCH = C // 128
HT = 1024
LNEPS = 1e-5
APH = int(os.environ.get("APH", "99"))
SUB6 = int(os.environ.get("SUB6", "9"))


def rep_mid(ap2d, n):
    b = ap2d.ap
    return AP(ap2d.tensor, ap2d.offset, [list(b[0]), [0, n], list(b[1])])


def bcast_inner(ap2d, n):
    b = ap2d.ap
    return AP(ap2d.tensor, ap2d.offset, [list(b[0]), list(b[1]), [0, n]])


class Ctr:
    def __init__(self, h):
        self.h = h
        self.n = 0

    def inc(self, inst, k=1):
        inst.then_inc(self.h, k)
        self.n += k
        return inst

    def wait(self, eng, n=None):
        eng.wait_ge(self.h, self.n if n is None else n)


class Alloc:
    def __init__(self, nc):
        self.nc = nc
        self.ctx = []

    def sb(self, name, shape, dt):
        t = self.nc.sbuf_tensor(name, shape, dt)
        self.ctx.append(t)
        return t.__enter__()

    def ps(self, name, shape, dt):
        t = self.nc.psum_tensor(name, shape, dt)
        self.ctx.append(t)
        return t.__enter__()

    def sem(self, name):
        t = self.nc.semaphore(name)
        self.ctx.append(t)
        return Ctr(t.__enter__())

    def close(self):
        for t in reversed(self.ctx):
            t.__exit__(None, None, None)



def build_launch_A():
    nc = bass.Bass("TRN2", target_bir_lowering=False)

    x_in = nc.dram_tensor("x_in", [P, C], F32, kind="ExternalInput").ap()
    wqkv = nc.dram_tensor("wqkv", [C, 3 * C], F32, kind="ExternalInput").ap()
    wproj = nc.dram_tensor("wproj", [C, C], F32, kind="ExternalInput").ap()
    bproj = nc.dram_tensor("bproj", [1, C], F32, kind="ExternalInput").ap()
    r0w = nc.dram_tensor("r0w", [C, 16], F32, kind="ExternalInput").ap()
    wgs = nc.dram_tensor("wgs", [C, 1024], F32, kind="ExternalInput").ap()
    wus = nc.dram_tensor("wus", [C, 1024], F32, kind="ExternalInput").ap()
    wds = nc.dram_tensor("wds", [1024, C], F32, kind="ExternalInput").ap()
    wgr = nc.dram_tensor("wgr", [C, 4096], F32, kind="ExternalInput").ap()
    wur = nc.dram_tensor("wur", [C, 4096], F32, kind="ExternalInput").ap()
    wdr = nc.dram_tensor("wdr", [4096, C], F32, kind="ExternalInput").ap()
    l1rw = nc.dram_tensor("l1rw", [C, 8], F32, kind="ExternalInput").ap()
    l1rb = nc.dram_tensor("l1rb", [1, 8], F32, kind="ExternalInput").ap()
    trilm = nc.dram_tensor("trilm", [128, 128], F32, kind="ExternalInput").ap()

    x1_out = nc.dram_tensor("x1_out", [NCH, 128, P], F32, kind="ExternalOutput").ap()
    xn1_out = nc.dram_tensor("xn1_out", [NCH, 128, P], BF16,
                             kind="ExternalOutput").ap()
    sc_out = nc.dram_tensor("sc_out", [P, 8], F32, kind="ExternalOutput").ap()

    comb_dram = nc.dram_tensor("comb_dram", [16, HT], F32).ap()

    al = Alloc(nc)
    sb, ps, sem = al.sb, al.ps, al.sem

    XF = sb("XF", [128, NCH * HT], F32)
    XN = sb("XN", [128, NCH * HT], BF16)
    S0 = sb("S0", [128, NCH * HT], F32)
    G0 = sb("G0", [128, NCH * HT], F32)
    QKT = sb("QKT", [128, 16 * 512], BF16)
    AFM = sb("AFM", [128, NCH * 512], BF16)
    VT = sb("VT", [128, 4 * 1024], BF16)
    WS = sb("WS", [128, 8192], BF16)
    SC1 = sb("SC1", [128, 2048], F32)
    SC2 = sb("SC2", [128, 1024], F32)
    SCB = sb("SCB", [128, 1024], BF16)
    SCB2 = sb("SCB2", [128, 512], BF16)
    SCM = sb("SCM", [128, 1024], F32)
    ROWS = sb("ROWS", [1, 3 * 1024], F32)
    ROW0 = sb("ROW0", [1, 1024], F32)
    ONES1 = sb("ONES1", [1, 128], F32)
    ONESC = sb("ONESC", [128, 1], F32)
    IDF = sb("IDF", [128, 128], F32)
    TRIL = sb("TRIL", [128, 128], F32)
    BPROJ = sb("BPROJ", [128, NCH], F32)
    SMAX = sb("SMAX", [128, 16], F32)
    CT = sb("CT", [16, 1024], F32)
    CMB = sb("CMB", [128, 1024], BF16)
    RT = sb("RT", [128, 8 * 16], F32)
    RS = sb("RS", [128, 64], F32)
    ZR16 = sb("ZR16", [128, 16], F32)
    CONSTS = sb("CONSTS", [1, 4], F32)

    PS_ = [ps(f"PSU{i}", [128, 512], F32) for i in range(8)]

    LW = sem("LW")
    LX = sem("LX")
    MM = sem("MM")
    EV = sem("EV")
    EA = sem("EA")
    ST = sem("ST")
    MS = sem("MS"); VA = sem("VA"); VB = sem("VB")
    MT = sem("MT"); VC = sem("VC"); MO = sem("MO"); VD = sem("VD")
    EB = sem("EB"); VM = sem("VM"); EX = sem("EX")
    VP = sem("VP"); EC = sem("EC")
    VE = sem("VE")
    LP = [sem(f"LP{i}") for i in range(7)]

    SUMS = MEAN = MR = ROWS[0:1, 0 * HT:1 * HT]
    SSQ = VAR = ROWS[0:1, 1 * HT:2 * HT]
    RSCR = RSTD = ROWS[0:1, 2 * HT:3 * HT]

    def slab_ap(dram, r0, nr, c0, ncols, row_len):
        assert nr % 128 == 0
        t = dram.tensor
        off = dram.offset + r0 * row_len + c0
        return AP(t, off, [[row_len, 128], [128 * row_len, nr // 128], [1, ncols]])

    def ws_dst(col0, nkc, ncols):
        return AP(WS[:, :].tensor, WS[:, :].offset + col0,
                  [[8192, 128], [ncols, nkc], [1, ncols]])

    with nc.Block() as block:

        @block.sync
        def _(sy):
            LX.inc(sy.dma_start(out=TRIL[:, :], in_=trilm[:, :]), 16)
            LX.inc(sy.dma_start(out=ROW0[0:1, :], in_=bproj[0:1, :]), 16)

        nc.const_aps.register(F32, LNEPS, CONSTS[0:1, 0:1])

        @block.gpsimd
        def _(g):
            g.memset(ONES1[:, :], 1.0)
            g.memset(ONESC[:, :], 1.0)
            g.memset(ZR16[:, :], 0.0)
            g.memset(CONSTS[0:1, 0:1], LNEPS)
            LW.inc(g.memset(IDF[:, :], 1.0))
            g.wait_ge(LW.h, 1)
            LW.inc(g.affine_select(IDF[:, :], IDF[:, :], pattern=[[1, 128]],
                                   compare_op=ALU.is_equal, fill=0.0,
                                   channel_multiplier=-1))

        @block.tensor
        def _(t):
            t.wait_ge(LW.h, 2)
            t.wait_ge(LX.h, 32)
            for k in range(NCH):
                ins = t.transpose(PS_[6][:, k:k + 1],
                                  ROW0[0:1, 128 * k:128 * (k + 1)], IDF[0:1, 0:1])
            MM.inc(ins)

        @block.vector
        def _(v):
            MM.wait(v)
            EV.inc(v.tensor_copy(BPROJ[:, :], PS_[6][:, 0:NCH]))

        def edge(eng, inst):
            VE.inc(inst)
            VE.wait(eng)

        def tensor_barrier(t):
            EV.wait(t)
            EA.wait(t)
            VA.wait(t)
            EB.wait(t)
            VM.wait(t)
            EX.wait(t)
            VP.wait(t)
            EC.wait(t)
            VD.wait(t)

        def emit_half(h):
            t0 = h * HT

            ev0, mm0, lx0 = EV.n, MM.n, LX.n
            for j in range(8):
                slot = j % 2

                @block.sync
                def _(sy, j=j, slot=slot):
                    if j < 2:
                        EA.wait(sy)
                        EV.wait(sy)
                        VP.wait(sy)
                        MT.wait(sy)
                        EX.wait(sy)
                    else:
                        sy.wait_ge(EV.h, ev0 + j - 1)
                    LP[j % 2].inc(sy.dma_start(
                        out=SC1[:, 1024 * slot:1024 * (slot + 1)],
                        in_=x_in[t0 + 128 * j: t0 + 128 * (j + 1), :]), 16)

                @block.tensor
                def _(t, j=j, slot=slot):
                    LP[j % 2].wait(t)
                    t.wait_ge(LW.h, 2)
                    if j == 0:
                        tensor_barrier(t)
                    if j >= 2:
                        t.wait_ge(EV.h, ev0 + j - 1)
                    pb = (j % 2) * 2
                    for k in range(NCH):
                        ins = t.transpose(
                            PS_[pb + k // 4][:, 128 * (k % 4):128 * (k % 4 + 1)],
                            SC1[:, 1024 * slot + 128 * k:1024 * slot + 128 * (k + 1)],
                            IDF[:, :])
                    MM.inc(ins)

                @block.vector
                def _(v, j=j):
                    v.wait_ge(MM.h, mm0 + j + 1)
                    pb = (j % 2) * 2
                    for k in range(NCH):
                        ins = v.tensor_copy(
                            XF[:, HT * k + 128 * j: HT * k + 128 * (j + 1)],
                            PS_[pb + k // 4][:, 128 * (k % 4):128 * (k % 4 + 1)])
                    EV.inc(ins)

            def emit_ln_stats(src, dst_xn, out_dram=None, scores=False):
                for tt in range(2):
                    for grp in range(2):
                        @block.scalar
                        def _(s, tt=tt, grp=grp):
                            EV.wait(s)
                            if grp == 1 or tt == 1:
                                MM.wait(s)
                            for kk in range(4):
                                k = grp * 4 + kk
                                ins = s.activation(
                                    SC1[:, 512 * kk:512 * (kk + 1)],
                                    src[:, HT * k + 512 * tt:HT * k + 512 * tt + 512],
                                    AF.Square)
                            EA.inc(ins)

                        @block.tensor
                        def _(t, tt=tt, grp=grp):
                            EA.wait(t)
                            if tt + grp == 0:
                                EV.wait(t)
                            for kk in range(4):
                                k = grp * 4 + kk
                                t.matmul(PS_[6][0:1, 0:512], ONESC[:, 0:1],
                                         src[:, HT * k + 512 * tt:
                                             HT * k + 512 * tt + 512],
                                         start=(k == 0), stop=(k == 7))
                                ins = t.matmul(PS_[7][0:1, 0:512], ONESC[:, 0:1],
                                               SC1[:, 512 * kk:512 * (kk + 1)],
                                               start=(k == 0), stop=(k == 7))
                            MM.inc(ins)

                    @block.vector
                    def _(v, tt=tt):
                        MM.wait(v)
                        v.tensor_copy(SUMS[0:1, 512 * tt:512 * (tt + 1)],
                                      PS_[6][0:1, 0:512])
                        EV.inc(v.tensor_copy(SSQ[0:1, 512 * tt:512 * (tt + 1)],
                                             PS_[7][0:1, 0:512]))

                @block.vector
                def _(v):
                    EV.wait(v)
                    edge(v, v.tensor_scalar_mul(MEAN, SUMS, 1.0 / C))
                    edge(v, v.tensor_scalar_mul(VAR, SSQ, 1.0 / C))
                    edge(v, v.tensor_mul(RSCR, MEAN, MEAN))
                    EV.inc(v.tensor_sub(VAR, VAR, RSCR))

                @block.scalar
                def _(s):
                    EV.wait(s)
                    s.wait_ge(LW.h, 2)
                    edge(s, s.activation(RSCR, VAR, AF.Ln, bias=LNEPS))
                    EA.inc(s.activation(RSTD, RSCR, AF.Exp, scale=-0.5))

                @block.vector
                def _(v):
                    EA.wait(v)
                    EV.inc(v.tensor_mul(MR, MEAN, RSTD))

                for tt in range(2):
                    @block.tensor
                    def _(t, tt=tt):
                        EV.wait(t)
                        t.matmul(PS_[6][:, 0:512], ONES1[0:1, :],
                                 RSTD[0:1, 512 * tt:512 * (tt + 1)],
                                 start=True, stop=True)
                        MM.inc(t.matmul(PS_[7][:, 0:512], ONES1[0:1, :],
                                        MR[0:1, 512 * tt:512 * (tt + 1)],
                                        start=True, stop=True))

                    @block.vector
                    def _(v, tt=tt):
                        MM.wait(v)
                        for k in range(NCH):
                            sl = slice(HT * k + 512 * tt, HT * k + 512 * tt + 512)
                            v.tensor_mul(SC2[:, 0:512], src[:, sl], PS_[6][:, 0:512])
                            ins = v.tensor_sub(dst_xn[:, sl], SC2[:, 0:512],
                                               PS_[7][:, 0:512])
                        EV.inc(ins)

                if out_dram is not None:
                    @block.sync
                    def _(sy):
                        EV.wait(sy)
                        for k in range(NCH):
                            ST.inc(sy.dma_start(
                                out=out_dram[k, :, t0:t0 + HT],
                                in_=dst_xn[:, HT * k:HT * (k + 1)]), 16)

            def emit_debug_dump():
                @block.sync
                def _(sy):
                    EV.wait(sy)
                    EA.wait(sy)
                    VD.wait(sy)
                    for k in range(NCH):
                        ST.inc(sy.dma_start(out=x1_out[k, :, t0:t0 + HT],
                                            in_=XF[:, HT * k:HT * (k + 1)]), 16)
                    for k in range(NCH):
                        ST.inc(sy.dma_start(out=xn1_out[k, :, t0:t0 + HT],
                                            in_=XN[:, HT * k:HT * (k + 1)]), 16)

            if APH <= 1:
                emit_debug_dump()
                return
            emit_ln_stats(XF, XN)
            if APH <= 2:
                emit_debug_dump()
                return

            def wload(g, ctr, dram_block, ws_col0, nkc, ncols):
                ins = g.dma_start(out=ws_dst(ws_col0, nkc, ncols),
                                  in_=dram_block)
                ctr.inc(ins, 16)
                return ins

            attn_i = {"i": 0}
            for tt in range(2 if APH >= 4 else 0):
                for nh in range(2):
                    @block.gpsimd
                    def _(g, tt=tt, nh=nh):
                        MM.wait(g)
                        wload(g, LP[4],
                              slab_ap(wqkv, 0, C, 2048 + 512 * nh, 512, 3 * C),
                              2048, NCH, 512)

                    for jb in range(4):
                        j = 4 * tt + jb

                        @block.tensor
                        def _(t, nh=nh, j=j, jb=jb):
                            LP[4].wait(t)
                            if nh == 0 and jb == 0:
                                tensor_barrier(t)
                            if nh * 4 + jb >= 2:
                                EV.wait(t, EV.n - 1)
                            bank = jb % 2
                            for kc in range(NCH):
                                ins = t.matmul(
                                    PS_[bank][:, 0:512],
                                    XN[:, HT * kc + 128 * j:HT * kc + 128 * (j + 1)],
                                    WS[:, 2048 + 512 * kc:2048 + 512 * (kc + 1)],
                                    start=(kc == 0), stop=(kc == NCH - 1))
                            MM.inc(ins)

                        @block.vector
                        def _(v, nh=nh, jb=jb):
                            MM.wait(v)
                            bank = jb % 2
                            EV.inc(v.tensor_copy(
                                VT[:, 1024 * jb + 512 * nh:1024 * jb + 512 * (nh + 1)],
                                PS_[bank][:, 0:512]))

                for s16 in range(16 if APH >= 5 else 0):
                    wb = 1024 * (s16 % 2)

                    @block.gpsimd
                    def _(g, s16=s16, wb=wb, tt=tt):
                        MM.wait(g, MM.n - 1)
                        wload(g, LP[2 + s16 % 2],
                              slab_ap(wqkv, 0, C, 128 * s16, 128, 3 * C),
                              wb, NCH, 128)

                    @block.tensor
                    def _(t, s16=s16, wb=wb, tt=tt):
                        LP[2 + s16 % 2].wait(t)
                        if s16 == 0:
                            tensor_barrier(t)
                        if s16 >= 2:
                            EV.wait(t, EV.n - 1)
                        bank = s16 % 2
                        for kc in range(NCH):
                            ins = t.matmul(
                                PS_[bank][:, 0:512],
                                WS[:, wb + 128 * kc:wb + 128 * (kc + 1)],
                                XN[:, HT * kc + 512 * tt:HT * kc + 512 * tt + 512],
                                start=(kc == 0), stop=(kc == NCH - 1))
                        MM.inc(ins)

                    @block.vector
                    def _(v, s16=s16):
                        MM.wait(v)
                        bank = s16 % 2
                        EV.inc(v.tensor_copy(QKT[:, 512 * s16:512 * (s16 + 1)],
                                             PS_[bank][:, 0:512]))

                if APH in (4, 5) and tt == 1:
                    @block.sync
                    def _(sy):
                        EV.wait(sy)
                        if APH == 4:
                            for k in range(NCH):
                                ST.inc(sy.dma_start(
                                    out=xn1_out[k, :, t0:t0 + 512],
                                    in_=VT[:, 512 * k:512 * (k + 1)]), 16)
                        else:
                            for k in range(NCH):
                                ST.inc(sy.dma_start(
                                    out=xn1_out[k, :, t0:t0 + 1024],
                                    in_=QKT[:, 1024 * k:1024 * (k + 1)]), 16)

                @block.gpsimd
                def _(g, tt=tt):
                    MM.wait(g)
                    wload(g, LP[5], slab_ap(wproj, 0, C, 0, C, C), 0, NCH, C)
                pw0 = LP[5].n

                if tt == 0:
                    attn_i["base"] = dict(
                        ms=MS.n, va=VA.n, eb=EB.n, vm=VM.n, ex=EX.n,
                        vp=VP.n, mt=MT.n, ec=EC.n, mo=MO.n, vd=VD.n)

                for bj in range(4 if APH >= 6 else 0):
                    for hg in range(4):
                        i = attn_i["i"]
                        attn_i["i"] += 1
                        b0 = attn_i["base"]
                        ptb = PS_[4 + i % 2]
                        so = 512 * (i % 2)
                        sm = 8 * (i % 2)

                        @block.tensor
                        def _(t, bj=bj, hg=hg, i=i):
                            if i % 16 == 0:
                                EV.wait(t)
                            if i >= 1:
                                t.wait_ge(VA.h, b0["va"] + i)
                                t.wait_ge(EB.h, b0["eb"] + i)
                            for u in range(4):
                                hh = 4 * hg + u
                                qsl = QKT[64 * (hh % 2):64 * (hh % 2) + 64,
                                          512 * (hh // 2) + 128 * bj:
                                          512 * (hh // 2) + 128 * (bj + 1)]
                                ksl = QKT[64 * (hh % 2):64 * (hh % 2) + 64,
                                          512 * (8 + hh // 2) + 128 * bj:
                                          512 * (8 + hh // 2) + 128 * (bj + 1)]
                                ins = t.matmul(PS_[u][:, 0:128],
                                               qsl, ksl, start=True, stop=True,
                                               tile_position=(64 * (hh % 2), 0))
                            MS.inc(ins)

                        @block.vector
                        def _(v, i=i, so=so):
                            v.wait_ge(MS.h, b0["ms"] + i + 1)
                            if i >= 2:
                                v.wait_ge(VM.h, b0["vm"] + i - 1)
                            v.tensor_copy(SCM[:, so:so + 128], PS_[0][:, 0:128])
                            VA.inc(v.tensor_copy(SCM[:, so + 128:so + 256],
                                                 PS_[1][:, 0:128]))

                        @block.scalar
                        def _(s, i=i, so=so):
                            s.wait_ge(MS.h, b0["ms"] + i + 1)
                            if i >= 2:
                                s.wait_ge(VM.h, b0["vm"] + i - 1)
                            s.copy(SCM[:, so + 256:so + 384], PS_[2][:, 0:128])
                            EB.inc(s.copy(SCM[:, so + 384:so + 512],
                                          PS_[3][:, 0:128]))

                        @block.vector
                        def _(v, i=i, so=so, sm=sm):
                            v.wait_ge(VA.h, b0["va"] + i + 1)
                            v.wait_ge(EB.h, b0["eb"] + i + 1)
                            if i >= 2:
                                v.wait_ge(EX.h, b0["ex"] + i - 1)
                            v.scalar_tensor_tensor(
                                SC2[:, so:so + 512].rearrange("p (g f) -> p g f",
                                                              f=128),
                                SCM[:, so:so + 512].rearrange("p (g f) -> p g f",
                                                              f=128),
                                0.125, rep_mid(TRIL[:, :], 4),
                                ALU.mult, ALU.add)
                            ins = v.reduce_max(
                                SMAX[:, sm:sm + 4],
                                SC2[:, so:so + 512].rearrange("p (g f) -> p g f",
                                                              f=128),
                                axis=AX.X, negate=True)
                            VM.inc(ins)

                        @block.scalar
                        def _(s, i=i, so=so, sm=sm):
                            s.wait_ge(VM.h, b0["vm"] + i + 1)
                            if i >= 2:
                                s.wait_ge(MT.h, b0["mt"] + i - 1)
                            for u in range(4):
                                ins = s.activation(
                                    SC1[:, so + 128 * u:so + 128 * (u + 1)],
                                    SC2[:, so + 128 * u:so + 128 * (u + 1)],
                                    AF.Exp, bias=SMAX[:, sm + u:sm + u + 1],
                                    accum_out=SMAX[:, sm + 4 + u:sm + 5 + u])
                            EX.inc(ins)

                        @block.vector
                        def _(v, i=i, so=so, sm=sm):
                            v.wait_ge(EX.h, b0["ex"] + i + 1)
                            edge(v, v.tensor_scalar_add(SMAX[:, sm + 4:sm + 8],
                                                        SMAX[:, sm + 4:sm + 8],
                                                        1e-6))
                            edge(v, v.reciprocal(SMAX[:, sm + 4:sm + 8],
                                                 SMAX[:, sm + 4:sm + 8]))
                            ins = v.tensor_mul(
                                SC1[:, so:so + 512].rearrange("p (g f) -> p g f",
                                                              f=128),
                                SC1[:, so:so + 512].rearrange("p (g f) -> p g f",
                                                              f=128),
                                bcast_inner(SMAX[:, sm + 4:sm + 8], 128))
                            VP.inc(ins)

                        @block.tensor
                        def _(t, i=i, ptb=ptb, so=so):
                            t.wait_ge(VP.h, b0["vp"] + i + 1)
                            if i >= 2:
                                t.wait_ge(EC.h, b0["ec"] + i - 1)
                            for u in range(4):
                                ins = t.transpose(ptb[:, 128 * u:128 * (u + 1)],
                                                  SC1[:, so + 128 * u:
                                                      so + 128 * (u + 1)],
                                                  IDF[:, :])
                            MT.inc(ins)

                        @block.scalar
                        def _(s, i=i, ptb=ptb):
                            s.wait_ge(MT.h, b0["mt"] + i + 1)
                            if i >= 2:
                                s.wait_ge(MO.h, b0["mo"] + 2 * i - 2)
                            bo = 512 * (i % 2)
                            ins = s.copy(SCB[:, bo:bo + 512], ptb[:, 0:512])
                            EC.inc(ins)

                        for pair in range(2):
                            @block.tensor
                            def _(t, bj=bj, hg=hg, i=i, pair=pair):
                                if pair == 0:
                                    t.wait_ge(EC.h, b0["ec"] + i + 1)
                                    if i >= 1:
                                        t.wait_ge(VD.h, b0["vd"] + 2 * i)
                                else:
                                    t.wait_ge(VD.h, b0["vd"] + 2 * i + 1)
                                bo = 512 * (i % 2)
                                for w in range(2):
                                    u = 2 * pair + w
                                    hh = 4 * hg + u
                                    vsl = VT[:, 1024 * bj + 64 * hh:
                                             1024 * bj + 64 * (hh + 1)]
                                    ins = t.matmul(
                                        PS_[6 + w][64 * w:64 * w + 64, 0:128],
                                        vsl,
                                        SCB[:, bo + 128 * u:bo + 128 * (u + 1)],
                                        start=True, stop=True,
                                        tile_position=(0, 64 * w))
                                MO.inc(ins)

                            @block.vector
                            def _(v, bj=bj, hg=hg, i=i, pair=pair):
                                v.wait_ge(MO.h, b0["mo"] + 2 * i + pair + 1)
                                ck = 2 * hg + pair
                                col = 512 * ck + 128 * bj
                                v.tensor_copy(AFM[0:64, col:col + 128],
                                              PS_[6][0:64, 0:128])
                                VD.inc(v.tensor_copy(AFM[64:128, col:col + 128],
                                                     PS_[7][64:128, 0:128]))

                for ds in range(NCH if APH >= 7 else 0):
                    @block.tensor
                    def _(t, ds=ds, tt=tt):
                        t.wait_ge(LP[5].h, pw0)
                        if ds == 0:
                            VD.wait(t)
                            VA.wait(t)
                            EB.wait(t)
                        if ds >= 2:
                            EV.wait(t, EV.n - 1)
                        bank = ds % 2
                        for kc in range(NCH):
                            ins = t.matmul(
                                PS_[bank][:, 0:512],
                                WS[:, 1024 * kc + 128 * ds:
                                   1024 * kc + 128 * (ds + 1)],
                                AFM[:, 512 * kc:512 * (kc + 1)],
                                start=(kc == 0), stop=(kc == NCH - 1))
                        MM.inc(ins)

                    @block.vector
                    def _(v, ds=ds, tt=tt):
                        MM.wait(v)
                        bank = ds % 2
                        EV.inc(v.tensor_scalar_add(
                            S0[:, HT * ds + 512 * tt:HT * ds + 512 * (tt + 1)],
                            PS_[bank][:, 0:512], BPROJ[:, ds:ds + 1]))

            if APH == 6:
                @block.sync
                def _(sy):
                    VD.wait(sy)
                    EC.wait(sy)
                    ST.inc(sy.dma_start(out=x1_out[0, :, 0:512],
                                        in_=SCM[:, 512:1024]), 16)
                    ST.inc(sy.dma_start(out=x1_out[1, :, 0:512],
                                        in_=SC2[:, 512:1024]), 16)
                    ST.inc(sy.dma_start(out=x1_out[2, :, 0:512],
                                        in_=SC1[:, 512:1024]), 16)
                    ST.inc(sy.dma_start(out=x1_out[3, :, 0:16],
                                        in_=SMAX[:, 0:16]), 16)
                    ST.inc(sy.dma_start(out=xn1_out[0, :, 0:512],
                                        in_=SCB[:, 512:1024]), 16)

            if APH in (6, 7):
                @block.sync
                def _(sy):
                    EV.wait(sy)
                    EA.wait(sy)
                    VD.wait(sy)
                    for k in range(NCH):
                        ST.inc(sy.dma_start(out=x1_out[k, :, t0:t0 + HT],
                                            in_=S0[:, HT * k:HT * (k + 1)]), 16)
                    for k in range(NCH):
                        ST.inc(sy.dma_start(out=xn1_out[k, :, t0:t0 + 512],
                                            in_=AFM[:, 512 * k:512 * (k + 1)]), 16)
            if APH <= 7:
                return
            @block.gpsimd
            def _(g):
                MM.wait(g)
                wload(g, LP[5], slab_ap(r0w, 0, C, 0, 16, 16), 8064, NCH, 16)
            rw0 = LP[5].n

            for j in range(8):
                @block.tensor
                def _(t, j=j):
                    t.wait_ge(LP[5].h, rw0)
                    if j == 0:
                        EV.wait(t)
                    if j >= 2:
                        EV.wait(t, EV.n - 1)
                    bank = 6 + j % 2
                    for kc in range(NCH):
                        ins = t.matmul(
                            PS_[bank][:, 0:16],
                            XN[:, HT * kc + 128 * j:HT * kc + 128 * (j + 1)],
                            WS[:, 8064 + 16 * kc:8064 + 16 * (kc + 1)],
                            start=(kc == 0), stop=(kc == NCH - 1))
                    MM.inc(ins)

                @block.vector
                def _(v, j=j):
                    MM.wait(v)
                    bank = 6 + j % 2
                    EV.inc(v.tensor_copy(RT[:, 16 * j:16 * (j + 1)],
                                         PS_[bank][:, 0:16]))

            R3 = RT[:, :].rearrange("p (j e) -> p j e", e=16)

            @block.scalar
            def _(s):
                EV.wait(s)
                EA.inc(s.activation(RT[:, :], RT[:, :], AF.Exp))

            @block.vector
            def _(v):
                EA.wait(v)
                edge(v, v.reduce_sum(RS[:, 0:8], R3, axis=AX.X))
                edge(v, v.reciprocal(RS[:, 0:8], RS[:, 0:8]))
                edge(v, v.tensor_mul(R3, R3, bcast_inner(RS[:, 0:8], 16)))
                EV.inc(v.reduce_max(RS[:, 8:16], R3, axis=AX.X))

            for j in range(8):
                @block.vector
                def _(v, j=j):
                    EV.wait(v)
                    sl = slice(16 * j, 16 * (j + 1))
                    r = RT[:, sl]
                    m1c = RS[:, 8 + j:9 + j]
                    edge(v, v.tensor_scalar(SC2[:, 0:16], r, m1c, None,
                                            op0=ALU.is_ge))
                    edge(v, v.tensor_tensor_scan(SC1[:, 0:16], SC2[:, 0:16],
                                                 ZR16[:, 0:16], 0.0,
                                                 ALU.add, ALU.add))
                    edge(v, v.tensor_scalar(SC1[:, 16:32], SC1[:, 0:16], 1.0,
                                            None, op0=ALU.is_le))
                    edge(v, v.tensor_mul(SC2[:, 0:16], SC2[:, 0:16],
                                         SC1[:, 16:32]))
                    edge(v, v.scalar_tensor_tensor(SC1[:, 32:48], SC2[:, 0:16],
                                                   -1e30, r, ALU.mult, ALU.add))
                    edge(v, v.reduce_max(RS[:, 24 + j:25 + j], SC1[:, 32:48],
                                         axis=AX.X))
                    m2c = RS[:, 24 + j:25 + j]
                    edge(v, v.tensor_scalar(SC2[:, 16:32], SC1[:, 32:48], m2c,
                                            None, op0=ALU.is_ge))
                    edge(v, v.tensor_tensor_scan(SC1[:, 0:16], SC2[:, 16:32],
                                                 ZR16[:, 0:16], 0.0,
                                                 ALU.add, ALU.add))
                    edge(v, v.tensor_scalar(SC1[:, 16:32], SC1[:, 0:16], 1.0,
                                            None, op0=ALU.is_le))
                    edge(v, v.tensor_mul(SC2[:, 16:32], SC2[:, 16:32],
                                         SC1[:, 16:32]))
                    edge(v, v.tensor_add(RS[:, 32 + j:33 + j], m1c, m2c))
                    edge(v, v.tensor_scalar_add(RS[:, 32 + j:33 + j],
                                                RS[:, 32 + j:33 + j], 1e-9))
                    edge(v, v.reciprocal(RS[:, 32 + j:33 + j],
                                         RS[:, 32 + j:33 + j]))
                    edge(v, v.tensor_mul(RS[:, 40 + j:41 + j], m1c,
                                         RS[:, 32 + j:33 + j]))
                    edge(v, v.tensor_mul(RS[:, 48 + j:49 + j], m2c,
                                         RS[:, 32 + j:33 + j]))
                    edge(v, v.tensor_scalar(SC2[:, 32:48], SC2[:, 0:16],
                                            RS[:, 40 + j:41 + j], None,
                                            op0=ALU.mult))
                    edge(v, v.tensor_scalar(SC2[:, 16:32], SC2[:, 16:32],
                                            RS[:, 48 + j:49 + j], None,
                                            op0=ALU.mult))
                    edge(v, v.tensor_add(SC2[:, 32:48], SC2[:, 32:48],
                                         SC2[:, 16:32]))
                    EV.inc(v.tensor_copy(RT[:, sl], SC2[:, 32:48]))

                @block.tensor
                def _(t, j=j):
                    EV.wait(t)
                    MM.inc(t.transpose(PS_[6 + (j % 2)][0:16, 0:128],
                                       RT[:, 16 * j:16 * (j + 1)], IDF[:, :]))

                @block.vector
                def _(v, j=j):
                    MM.wait(v)
                    EV.inc(v.tensor_copy(CT[0:16, 128 * j:128 * (j + 1)],
                                         PS_[6 + (j % 2)][0:16, 0:128]))

            @block.gpsimd
            def _(g):
                EV.wait(g)
                LP[4].inc(g.dma_start(out=comb_dram[:, :], in_=CT[:, :]), 16)

            moe_prev = {"gu": None, "all": None}

            def moe_block(wg_d, wu_d, wd_d, mid, e_base_col, comb_col, first):
                nms = mid // 128
                wu0 = 8 * mid
                wd0 = 16 * mid

                gu_prev = moe_prev["gu"] if moe_prev["gu"] is not None else MM.n
                all_prev = moe_prev["all"] if moe_prev["all"] is not None else MM.n

                @block.gpsimd
                def _(g):
                    g.wait_ge(MM.h, gu_prev)
                    wload(g, LP[5], wg_d, 0, NCH, mid)
                    wload(g, LP[5], wu_d, wu0, NCH, mid)
                lwgu = LP[5].n

                @block.gpsimd
                def _(g):
                    g.wait_ge(MM.h, all_prev)
                    wload(g, LP[6], wd_d, wd0, nms, C)
                lw0 = LP[6].n

                for ms in range(nms):
                    for tt in range(2):
                        @block.tensor
                        def _(t, ms=ms, tt=tt):
                            t.wait_ge(LP[5].h, lwgu)
                            if ms + tt == 0:
                                EV.wait(t)
                                EA.wait(t)
                            gi = ms * 2 + tt
                            if gi >= 2:
                                EV.wait(t, EV.n - 1)
                            bg = gi % 2
                            bu = 2 + gi % 2
                            for kc in range(NCH):
                                t.matmul(PS_[bg][:, 0:512],
                                         WS[:, mid * kc + 128 * ms:
                                            mid * kc + 128 * (ms + 1)],
                                         XN[:, HT * kc + 512 * tt:
                                            HT * kc + 512 * tt + 512],
                                         start=(kc == 0), stop=(kc == NCH - 1))
                                ins = t.matmul(PS_[bu][:, 0:512],
                                               WS[:, wu0 + mid * kc + 128 * ms:
                                                  wu0 + mid * kc + 128 * (ms + 1)],
                                               XN[:, HT * kc + 512 * tt:
                                                  HT * kc + 512 * tt + 512],
                                               start=(kc == 0), stop=(kc == NCH - 1))
                            MM.inc(ins)

                        @block.scalar
                        def _(s, ms=ms, tt=tt):
                            MM.wait(s)
                            gi = ms * 2 + tt
                            bg = gi % 2
                            EA.inc(s.activation(SCB2[:, 0:512], PS_[bg][:, 0:512],
                                                AF.Silu))

                        @block.vector
                        def _(v, ms=ms, tt=tt):
                            EA.wait(v)
                            gi = ms * 2 + tt
                            bu = 2 + gi % 2
                            dst = AFM[:, 512 * (2 * ms + tt):512 * (2 * ms + tt + 1)]
                            if comb_col is not None:
                                v.tensor_tensor(SCB[:, 0:512], PS_[bu][:, 0:512],
                                                CMB[:, 512 * tt:512 * (tt + 1)],
                                                ALU.mult)
                                ins = v.tensor_mul(dst, SCB2[:, 0:512],
                                                   SCB[:, 0:512])
                            else:
                                v.tensor_copy(SCB[:, 0:512], PS_[bu][:, 0:512])
                                ins = v.tensor_mul(dst, SCB2[:, 0:512],
                                                   SCB[:, 0:512])
                            EV.inc(ins)

                moe_prev["gu"] = MM.n

                for ds in range(NCH):
                    for tt in range(2):
                        @block.tensor
                        def _(t, ds=ds, tt=tt):
                            if ds + tt == 0:
                                t.wait_ge(LP[6].h, lw0)
                                EV.wait(t)
                            gi = ds * 2 + tt
                            if gi >= 2:
                                EV.wait(t, EV.n - 1)
                            bank = 4 + gi % 2
                            for kg in range(nms):
                                ins = t.matmul(
                                    PS_[bank][:, 0:512],
                                    WS[:, wd0 + C * kg + 128 * ds:
                                       wd0 + C * kg + 128 * (ds + 1)],
                                    AFM[:, 512 * (2 * kg + tt):
                                        512 * (2 * kg + tt + 1)],
                                    start=(kg == 0), stop=(kg == nms - 1))
                            MM.inc(ins)

                        @block.vector
                        def _(v, ds=ds, tt=tt):
                            MM.wait(v)
                            gi = ds * 2 + tt
                            bank = 4 + gi % 2
                            dst = G0[:, HT * ds + 512 * tt:HT * ds + 512 * (tt + 1)]
                            if first:
                                ins = v.tensor_copy(dst, PS_[bank][:, 0:512])
                            else:
                                ins = v.tensor_add(dst, dst, PS_[bank][:, 0:512])
                            EV.inc(ins)

                moe_prev["all"] = MM.n

            if APH == 8:
                @block.sync
                def _(sy):
                    EV.wait(sy)
                    EA.wait(sy)
                    LP[4].wait(sy)
                    ST.inc(sy.dma_start(out=x1_out[0, 0:16, t0:t0 + HT],
                                        in_=CT[:, :]), 16)
            if APH <= 8:
                return
            for sh in range(4):
                moe_block(slab_ap(wgs, 0, C, 256 * sh, 256, 1024),
                          slab_ap(wus, 0, C, 256 * sh, 256, 1024),
                          slab_ap(wds, 256 * sh, 256, 0, C, C),
                          256, None, None, first=(sh == 0))

            if APH <= 9:
                return
            for e in range(16):
                @block.gpsimd
                def _(g, e=e):
                    EV.wait(g, EV.n)
                    LP[4].inc(g.dma_start(out=ROW0[0:1, 0:HT],
                                          in_=comb_dram[e:e + 1, :]), 16)
                cb0 = LP[4].n

                for tt in range(2):
                    @block.tensor
                    def _(t, tt=tt):
                        t.wait_ge(LP[4].h, cb0)
                        if tt == 0:
                            EV.wait(t)
                        MM.inc(t.matmul(PS_[6 + tt][:, 0:512], ONES1[0:1, :],
                                        ROW0[0:1, 512 * tt:512 * (tt + 1)],
                                        start=True, stop=True))

                    @block.vector
                    def _(v, tt=tt):
                        MM.wait(v)
                        EV.inc(v.tensor_copy(CMB[:, 512 * tt:512 * (tt + 1)],
                                             PS_[6 + tt][:, 0:512]))

                moe_block(slab_ap(wgr, 0, C, 256 * e, 256, 4096),
                          slab_ap(wur, 0, C, 256 * e, 256, 4096),
                          slab_ap(wdr, 256 * e, 256, 0, C, C),
                          256, None, 0, first=False)

            if APH in (9, 10):
                @block.sync
                def _(sy):
                    EV.wait(sy)
                    EA.wait(sy)
                    for k in range(NCH):
                        ST.inc(sy.dma_start(out=x1_out[k, :, t0:t0 + HT],
                                            in_=G0[:, HT * k:HT * (k + 1)]), 16)
            if APH <= 10:
                return
            for k in range(NCH):
                for tt in range(2):
                    @block.scalar
                    def _(s, k=k, tt=tt):
                        EV.wait(s)
                        sl = slice(HT * k + 512 * tt, HT * k + 512 * tt + 512)
                        EA.inc(s.activation(SC1[:, 512 * (tt % 2):
                                                512 * (tt % 2) + 512],
                                            G0[:, sl], AF.Silu))

                    @block.vector
                    def _(v, k=k, tt=tt):
                        EA.wait(v)
                        sl = slice(HT * k + 512 * tt, HT * k + 512 * tt + 512)
                        v.tensor_mul(SC2[:, 0:512],
                                     S0[:, sl], SC1[:, 512 * (tt % 2):
                                                    512 * (tt % 2) + 512])
                        EV.inc(v.tensor_add(XF[:, sl], XF[:, sl], SC2[:, 0:512]))

            @block.sync
            def _(sy):
                EV.wait(sy)
                for k in range(NCH):
                    ST.inc(sy.dma_start(out=x1_out[k, :, t0:t0 + HT],
                                        in_=XF[:, HT * k:HT * (k + 1)]), 16)

            if APH <= 11:
                return
            emit_ln_stats(XF, XN, out_dram=xn1_out)

            @block.gpsimd
            def _(g):
                MM.wait(g)
                wload(g, LP[5], slab_ap(l1rw, 0, C, 0, 8, 8), 8128, NCH, 8)
                LP[4].inc(g.dma_start(out=ROW0[0:1, 0:8], in_=l1rb[0:1, :]), 16)
            lb0 = LP[5].n
            lb1 = LP[4].n

            @block.tensor
            def _(t):
                t.wait_ge(LP[5].h, lb0)
                t.wait_ge(LP[4].h, lb1)
                EV.wait(t)
                MM.inc(t.matmul(PS_[5][:, 0:8], ONES1[0:1, :], ROW0[0:1, 0:8],
                                start=True, stop=True))

            @block.vector
            def _(v):
                MM.wait(v)
                EV.inc(v.tensor_copy(RS[:, 56:64], PS_[5][:, 0:8]))

            for j in range(8):
                @block.tensor
                def _(t, j=j):
                    if j >= 2:
                        EV.wait(t, EV.n - 1)
                    bank = 6 + j % 2
                    for kc in range(NCH):
                        ins = t.matmul(
                            PS_[bank][:, 0:8],
                            XN[:, HT * kc + 128 * j:HT * kc + 128 * (j + 1)],
                            WS[:, 8128 + 8 * kc:8128 + 8 * (kc + 1)],
                            start=(kc == 0), stop=(kc == NCH - 1))
                    MM.inc(ins)

                @block.vector
                def _(v, j=j):
                    MM.wait(v)
                    bank = 6 + j % 2
                    EV.inc(v.tensor_add(RT[:, 16 * j:16 * j + 8],
                                        PS_[bank][:, 0:8], RS[:, 56:64]))

            RL = AP(RT[:, :].tensor, RT[:, :].offset,
                    [list(RT[:, :].ap[0]), [16, 8], [1, 8]])

            @block.scalar
            def _(s):
                EV.wait(s)
                EA.inc(s.activation(RT[:, :], RT[:, :], AF.Exp))

            @block.vector
            def _(v):
                EA.wait(v)
                edge(v, v.reduce_sum(RS[:, 0:8], RL, axis=AX.X))
                edge(v, v.reciprocal(RS[:, 0:8], RS[:, 0:8]))
                EV.inc(v.tensor_mul(RL, RL, bcast_inner(RS[:, 0:8], 8)))

            @block.sync
            def _(sy):
                EV.wait(sy)
                for j in range(8):
                    ST.inc(sy.dma_start(
                        out=sc_out[t0 + 128 * j:t0 + 128 * (j + 1), :],
                        in_=RT[:, 16 * j:16 * j + 8]), 16)

        emit_half(0)
        emit_half(1)

        @block.sync
        def _(sy):
            ST.wait(sy)

    al.close()
    return nc



_A_CACHE = {}


def prep_A_weights(inp):
    w = {}
    w["wqkv"] = inp["L0_attn_W_qkv"].astype(np.float32)
    w["wproj"] = inp["L0_attn_W_proj"].astype(np.float32)
    w["bproj"] = inp["L0_attn_b_proj"].astype(np.float32).reshape(1, C)
    w["r0w"] = inp["L0_ffn_router_w"].astype(np.float32)
    w["wgs"] = np.concatenate([inp["L0_ffn_wg_s"][i] for i in range(2)],
                              axis=1).astype(np.float32)
    w["wus"] = np.concatenate([inp["L0_ffn_wu_s"][i] for i in range(2)],
                              axis=1).astype(np.float32)
    w["wds"] = np.concatenate([inp["L0_ffn_wd_s"][i] for i in range(2)],
                              axis=0).astype(np.float32)
    w["wgr"] = np.concatenate([inp["L0_ffn_wg_r"][i] for i in range(16)],
                              axis=1).astype(np.float32)
    w["wur"] = np.concatenate([inp["L0_ffn_wu_r"][i] for i in range(16)],
                              axis=1).astype(np.float32)
    w["wdr"] = np.concatenate([inp["L0_ffn_wd_r"][i] for i in range(16)],
                              axis=0).astype(np.float32)
    w["l1rw"] = inp["L1_ffn_router_w"].astype(np.float32)
    w["l1rb"] = inp["L1_ffn_router_b"].astype(np.float32).reshape(1, 8)
    tril = np.where(np.tril(np.ones((128, 128), bool)), 0.0, -1e30)
    w["trilm"] = tril.astype(np.float32)
    return {k: np.ascontiguousarray(v) for k, v in w.items()}


def run_launch_A(inp):
    x = np.asarray(inp["L0_attn_x"], np.float32).reshape(16384, C)
    w = prep_A_weights(inp)
    nc = build_launch_A()
    in_maps = []
    for c in range(N_CORES):
        m = dict(w)
        m["x_in"] = np.ascontiguousarray(x[P * c:P * (c + 1)])
        in_maps.append(m)
    res = run_bass_kernel_spmd(nc, in_maps, core_ids=list(range(N_CORES)))
    x1 = np.empty((16384, C), np.float32)
    xn1 = np.empty((16384, C), BF)
    scores = np.empty((16384, 8), np.float32)
    for c in range(N_CORES):
        r = res.results[c]
        x1[P * c:P * (c + 1)] = r["x1_out"].reshape(C, P).T
        xn1[P * c:P * (c + 1)] = r["xn1_out"].reshape(C, P).T
        scores[P * c:P * (c + 1)] = r["sc_out"]
    return x1, xn1, scores





TB = 512
Q = 256


def declare_B_drams(nc):
    dr = {}
    for nm, shape in (("x1b", [NCH, 128, 4096]), ("tmcd", [128, 6 * NCH]),
                      ("maw1", [C, 128]), ("maw2", [128, C]),
                      ("dcw1", [C, 64]), ("dcw2", [64, 512]),
                      ("faaT", [64, 8]), ("wr_", [C, 512]), ("wk_", [C, 512]),
                      ("wv_", [C, 512]), ("hindm", [128, 32]),
                      ("strictT", [2, 128, 256])):
        dr[nm] = nc.dram_tensor(nm, shape, F32, kind="ExternalInput").ap()
    dr["wrows_d"] = nc.dram_tensor("wrows_d", [16, 520], F32).ap()
    return dr


def build_launch_B():
    nc = bass.Bass("TRN2", target_bir_lowering=False)
    dr = declare_B_drams(nc)
    y_out = nc.dram_tensor("y_out", [4, 128, 4096], F32,
                           kind="ExternalOutput").ap()
    alP = Alloc(nc)
    alB = Alloc(nc)
    with nc.Block() as block:
        emit_B_part(nc, block, alB, alP, dr, ("dram", y_out))
    alB.close()
    alP.close()
    return nc


def emit_B_part(nc, block, alB, alP, dr, y_sink):
    x1b = dr["x1b"]; tmcd = dr["tmcd"]; maw1 = dr["maw1"]; maw2 = dr["maw2"]
    dcw1 = dr["dcw1"]; dcw2 = dr["dcw2"]; faaT = dr["faaT"]
    wr_ = dr["wr_"]; wk_ = dr["wk_"]; wv_ = dr["wv_"]
    hindm = dr["hindm"]; strictT = dr["strictT"]; wrows_d = dr["wrows_d"]
    sb, ps = alB.sb, alB.ps
    sem = alP.sem
    CONSTB = alP.sb("CONSTB", [1, 4], F32)

    XT = sb("XT", [128, NCH * TB], F32)
    XX = sb("XX", [128, NCH * TB], F32)
    XXB = sb("XXB", [128, NCH * TB], BF16)
    XNB = sb("XNB", [128, NCH * TB], BF16)
    MIX = sb("MIX", [128, NCH * TB], BF16)
    MF = sb("MF", [128, NCH * TB], BF16)
    M1 = sb("M1", [128, TB], BF16)
    RF = sb("RF", [128, 4 * TB], BF16)
    KF = sb("KF", [128, 4 * TB], BF16)
    VF = sb("VF", [128, 4 * TB], BF16)
    KTM = sb("KTM", [128, 4 * TB], BF16)
    VTM = sb("VTM", [128, 4 * TB], BF16)
    KW = sb("KW", [128, 4 * TB], BF16)
    WD = sb("WD", [128, 4 * TB], F32)
    WH = sb("WH", [8, TB], F32)
    T1B = sb("T1B", [64, TB], BF16)
    WCC = sb("WCC", [8, 256], F32)
    WCUM = sb("WCUM", [8, 256], F32)
    WINT = sb("WINT", [8, 256], F32)
    WROWS = sb("WROWS", [8, 520], F32)
    WRB = sb("WRB", [1, 2 * 520], F32)
    WEXPA = sb("WEXPA", [8, 256], F32)
    WEXPB = sb("WEXPB", [8, 256], F32)
    IND128 = sb("IND128", [8, 8 * 128], F32)
    IND64 = sb("IND64", [8, 8 * 64], F32)
    WIT = sb("WIT", [128, 4 * 8], F32)
    WCT = sb("WCT", [128, 4 * 8], F32)
    SCT = sb("SCT", [128, 2 * 256], BF16)
    MSK = sb("MSK", [128, 2 * 256], F32)
    RSW = sb("RSW", [64, 256], BF16)
    KV = sb("KV", [64, 512], F32)
    STATE = sb("STATE", [64, 512], F32)
    STB = sb("STB", [64, 512], BF16)
    YC = sb("YC", [128, 4 * 256], F32)
    DGR = sb("DGR", [1, 3 * 256], F32)
    WSROW = sb("WSROW", [1, 768], F32)
    WSR = sb("WSR", [128, NCH * 512], BF16)
    WSK = sb("WSK", [128, NCH * 512], BF16)
    WSV = sb("WSV", [128, NCH * 512], BF16)
    WSA = sb("WSA", [128, NCH * 128], BF16)
    WSA2 = sb("WSA2", [128, C], BF16)
    WSD1 = sb("WSD1", [128, NCH * 64], BF16)
    WSD2 = sb("WSD2", [64, 512], BF16)
    SRW = sb("SRW", [1, 3 * TB], F32)
    SC1B = sb("SC1B", [128, 2048], F32)
    SC2B = sb("SC2B", [64, 512], F32)
    FAA = sb("FAA", [64, 8], F32)
    HIND = sb("HIND", [128, 32], F32)
    STT = sb("STT", [128, 2 * 256], F32)
    PREV = sb("PREV", [128, NCH], F32)
    ONES1b = sb("ONES1b", [1, 128], F32)
    ONESCb = sb("ONESCb", [128, 1], F32)
    IDFb = sb("IDFb", [128, 128], F32)
    IDB = sb("IDB", [128, 128], BF16)
    ZR256 = sb("ZR256", [8, 256], F32)
    TMC = sb("TMC", [128, 6 * NCH], F32)

    PSF = [ps(f"PBF{i}", [128, 512], F32) for i in range(6)]
    PSB = [ps(f"PBB{i}", [128, 512], BF16) for i in range(2)]

    LX = sem("LX")
    LW = sem("LW")
    EV = sem("EV")
    EA = sem("EA")
    MMG = sem("MMG")
    ST = sem("ST")
    VE = sem("VE")
    BK = [sem(f"BKF{i}") for i in range(6)] + [sem(f"BKB{i}") for i in range(2)]
    LPB = [sem(f"LPB{i}") for i in range(5)]
    WRC = sem("WRC")

    SUMS = MEAN = MR = SRW[0:1, 0 * TB:1 * TB]
    SSQ = VAR = SRW[0:1, 1 * TB:2 * TB]
    RSCR = RSTD = SRW[0:1, 2 * TB:3 * TB]

    def slab_ap(dram, r0, nr, c0, ncols, row_len):
        t = dram.tensor
        off = dram.offset + r0 * row_len + c0
        return AP(t, off, [[row_len, 128], [128 * row_len, max(nr // 128, 1)],
                           [1, ncols]])

    def sb_dst(tile, col0, nkc, ncols, width):
        full = tile[:, :]
        return AP(full.tensor, full.offset + col0,
                  [[width, 128], [ncols, nkc], [1, ncols]])

    if True:

        nc.const_aps.register(F32, LNEPS, CONSTB[0:1, 0:1])

        def edge(eng, inst):
            VE.inc(inst)
            VE.wait(eng)

        @block.gpsimd
        def _(g):
            g.memset(ONES1b[:, :], 1.0)
            g.memset(ONESCb[:, :], 1.0)
            g.memset(ZR256[:, :], 0.0)
            g.memset(PREV[:, :], 0.0)
            g.memset(STATE[:, :], 0.0)
            g.memset(STB[:, :], 0.0)
            g.memset(WROWS[:, :], 0.0)
            g.memset(CONSTB[0:1, 0:1], LNEPS)
            g.memset(IDB[:, :], 1.0)
            LW.inc(g.memset(IDFb[:, :], 1.0))
            g.wait_ge(LW.h, 1)
            LW.inc(g.affine_select(IDFb[:, :], IDFb[:, :], pattern=[[1, 128]],
                                   compare_op=ALU.is_equal, fill=0.0,
                                   channel_multiplier=-1))
            g.wait_ge(LW.h, 2)
            LW.inc(g.affine_select(IDB[:, :], IDB[:, :], pattern=[[1, 128]],
                                   compare_op=ALU.is_equal, fill=0.0,
                                   channel_multiplier=-1))
            g.wait_ge(LW.h, 3)
            LW.inc(g.memset(IND128[:, :], 1.0))
            g.wait_ge(LW.h, 4)
            LW.inc(g.affine_select(IND128[:, :], IND128[:, :],
                                   pattern=[[1, 1024]], compare_op=ALU.is_ge,
                                   fill=0.0, base=0, channel_multiplier=-128))
            g.wait_ge(LW.h, 5)
            LW.inc(g.affine_select(IND128[:, :], IND128[:, :],
                                   pattern=[[-1, 1024]], compare_op=ALU.is_ge,
                                   fill=0.0, base=127, channel_multiplier=128))
            g.wait_ge(LW.h, 6)
            LW.inc(g.memset(IND64[:, :], 1.0))
            g.wait_ge(LW.h, 7)
            LW.inc(g.affine_select(IND64[:, :], IND64[:, :],
                                   pattern=[[1, 512]], compare_op=ALU.is_ge,
                                   fill=0.0, base=0, channel_multiplier=-64))
            g.wait_ge(LW.h, 8)
            LW.inc(g.affine_select(IND64[:, :], IND64[:, :],
                                   pattern=[[-1, 512]], compare_op=ALU.is_ge,
                                   fill=0.0, base=63, channel_multiplier=64))
            LPB[0].inc(g.dma_start(out=sb_dst(WSR, 0, NCH, 512, NCH * 512),
                                   in_=slab_ap(wr_, 0, C, 0, 512, 512)), 16)
            LPB[0].inc(g.dma_start(out=sb_dst(WSK, 0, NCH, 512, NCH * 512),
                                   in_=slab_ap(wk_, 0, C, 0, 512, 512)), 16)
            LPB[0].inc(g.dma_start(out=sb_dst(WSV, 0, NCH, 512, NCH * 512),
                                   in_=slab_ap(wv_, 0, C, 0, 512, 512)), 16)
            LPB[0].inc(g.dma_start(out=sb_dst(WSA, 0, NCH, 128, NCH * 128),
                                   in_=slab_ap(maw1, 0, C, 0, 128, 128)), 16)
            LPB[0].inc(g.dma_start(out=WSA2[:, :], in_=maw2[:, :]), 16)
            LPB[0].inc(g.dma_start(out=sb_dst(WSD1, 0, NCH, 64, NCH * 64),
                                   in_=slab_ap(dcw1, 0, C, 0, 64, 64)), 16)
            LPB[0].inc(g.dma_start(out=WSD2[:, :], in_=dcw2[:, :]), 16)

        @block.sync
        def _(sy):
            LX.inc(sy.dma_start(out=HIND[:, :], in_=hindm[:, :]), 16)
            LX.inc(sy.dma_start(out=FAA[:, :], in_=faaT[:, :]), 16)
            LX.inc(sy.dma_start(out=STT[:, 0:256], in_=strictT[0, :, :]), 16)
            LX.inc(sy.dma_start(out=STT[:, 256:512], in_=strictT[1, :, :]), 16)
            LX.inc(sy.dma_start(out=TMC[:, :], in_=tmcd[:, :]), 16)

        wrows_rd = {"n": None}
        for it in range(8):
            t0 = it * TB

            @block.sync
            def _(sy, it=it, t0=t0):
                EV.wait(sy)
                EA.wait(sy)
                for kc in range(NCH):
                    LX.inc(sy.dma_start(out=XT[:, TB * kc:TB * (kc + 1)],
                                        in_=x1b[kc, :, t0:t0 + TB]), 16)
            lx_t = LX.n

            for grp in range(2):
                @block.scalar
                def _(s, grp=grp):
                    s.wait_ge(LX.h, lx_t)
                    EV.wait(s)
                    if grp == 1:
                        MMG.wait(s)
                    for kk in range(4):
                        kc = grp * 4 + kk
                        ins = s.activation(SC1B[:, 512 * kk:512 * (kk + 1)],
                                           XT[:, TB * kc:TB * (kc + 1)],
                                           AF.Square)
                    EA.inc(ins)

                @block.tensor
                def _(t, grp=grp):
                    EA.wait(t)
                    if grp == 0:
                        t.wait_ge(LW.h, 9)
                        LPB[0].wait(t)
                        BK[4].wait(t)
                        BK[5].wait(t)
                    for kk in range(4):
                        kc = grp * 4 + kk
                        t.matmul(PSF[4][0:1, 0:512], ONESCb[:, 0:1],
                                 XT[:, TB * kc:TB * (kc + 1)],
                                 start=(kc == 0), stop=(kc == 7))
                        ins = t.matmul(PSF[5][0:1, 0:512], ONESCb[:, 0:1],
                                       SC1B[:, 512 * kk:512 * (kk + 1)],
                                       start=(kc == 0), stop=(kc == 7))
                    MMG.inc(ins)

            @block.vector
            def _(v):
                MMG.wait(v)
                v.tensor_copy(SUMS, PSF[4][0:1, 0:512])
                ins = v.tensor_copy(SSQ, PSF[5][0:1, 0:512])
                BK[4].inc(ins)
                BK[5].inc(v.engine_nop())
                edge(v, v.tensor_scalar_mul(MEAN, SUMS, 1.0 / C))
                edge(v, v.tensor_scalar_mul(VAR, SSQ, 1.0 / C))
                edge(v, v.tensor_mul(RSCR, MEAN, MEAN))
                EV.inc(v.tensor_sub(VAR, VAR, RSCR))

            @block.scalar
            def _(s):
                EV.wait(s)
                edge(s, s.activation(RSCR, VAR, AF.Ln, bias=LNEPS))
                EA.inc(s.activation(RSTD, RSCR, AF.Exp, scale=-0.5))

            @block.vector
            def _(v):
                EA.wait(v)
                EV.inc(v.tensor_mul(MR, MEAN, RSTD))

            @block.tensor
            def _(t):
                EV.wait(t)
                BK[4].wait(t)
                BK[5].wait(t)
                t.matmul(PSF[4][:, 0:512], ONES1b[0:1, :], RSTD,
                         start=True, stop=True)
                MMG.inc(t.matmul(PSF[5][:, 0:512], ONES1b[0:1, :], MR,
                                 start=True, stop=True))

            @block.vector
            def _(v):
                MMG.wait(v)
                for kc in range(NCH):
                    sl = slice(TB * kc, TB * (kc + 1))
                    v.tensor_mul(SC1B[:, 0:512], XT[:, sl], PSF[4][:, 0:512])
                    ins = v.tensor_sub(XT[:, sl], SC1B[:, 0:512], PSF[5][:, 0:512])
                BK[4].inc(ins)
                BK[5].inc(v.engine_nop())
                for kc in range(NCH):
                    sl0 = TB * kc
                    v.tensor_sub(XX[:, sl0 + 1:sl0 + TB], XT[:, sl0:sl0 + TB - 1],
                                 XT[:, sl0 + 1:sl0 + TB])
                    ins = v.tensor_sub(XX[:, sl0:sl0 + 1], PREV[:, kc:kc + 1],
                                       XT[:, sl0:sl0 + 1])
                    VE.inc(ins)
                for kc in range(NCH):
                    v.tensor_copy(PREV[:, kc:kc + 1],
                                  XT[:, TB * kc + TB - 1:TB * kc + TB])
                VE.wait(v)
                v.tensor_copy(XXB[:, :], XX[:, :])
                EV.inc(v.tensor_copy(XNB[:, :], XT[:, :]))

            @block.vector
            def _(v):
                EV.wait(v)
                for kc in range(NCH):
                    sl = slice(TB * kc, TB * (kc + 1))
                    ins = v.scalar_tensor_tensor(MIX[:, sl], XX[:, sl],
                                                 TMC[:, kc:kc + 1], XT[:, sl],
                                                 ALU.mult, ALU.add)
                EV.inc(ins)

            @block.tensor
            def _(t):
                EV.wait(t)
                LPB[0].wait(t)
                BK[0].wait(t)
                for kc in range(NCH):
                    ins = t.matmul(PSF[0][:, 0:512],
                                   WSA[:, 128 * kc:128 * (kc + 1)],
                                   MIX[:, TB * kc:TB * (kc + 1)],
                                   start=(kc == 0), stop=(kc == 7))
                MMG.inc(ins)

            @block.scalar
            def _(s):
                MMG.wait(s)
                ins = s.activation(M1[:, :], PSF[0][:, 0:512], AF.Tanh)
                EA.inc(ins)
                BK[0].inc(s.nop())

            for f, (dst, tmblk) in enumerate(
                    [(None, 1), (KF, 2), (VF, 3), (RF, 4)]):
                for sl8 in range(NCH):
                    b = sl8 % 2

                    @block.tensor
                    def _(t, f=f, sl8=sl8, b=b):
                        if f == 0 and sl8 == 0:
                            EA.wait(t)
                        BK[b].wait(t)
                        ins = t.matmul(
                            PSF[b][:, 0:512],
                            WSA2[32 * f:32 * (f + 1), 128 * sl8:128 * (sl8 + 1)],
                            M1[32 * f:32 * (f + 1), :],
                            start=True, stop=True,
                            tile_position=(32 * f, 0))
                        MMG.inc(ins)

                    @block.vector
                    def _(v, f=f, sl8=sl8, b=b, tmblk=tmblk):
                        MMG.wait(v)
                        ins = v.tensor_scalar(
                            MF[:, TB * sl8:TB * (sl8 + 1)], PSF[b][:, 0:512],
                            TMC[:, NCH * tmblk + sl8:NCH * tmblk + sl8 + 1],
                            None, op0=ALU.add)
                        BK[b].inc(ins)
                        if sl8 == NCH - 1:
                            EV.inc(v.engine_nop())

                @block.vector
                def _(v, f=f):
                    EV.wait(v)
                    for kc in range(NCH):
                        sl = slice(TB * kc, TB * (kc + 1))
                        if f == 0:
                            v.tensor_mul(SC1B[:, 0:512], XX[:, sl], MF[:, sl])
                            ins = v.tensor_add(MIX[:, sl], SC1B[:, 0:512],
                                               XT[:, sl])
                        else:
                            v.tensor_mul(SC1B[:, 0:512], XXB[:, sl], MF[:, sl])
                            ins = v.tensor_add(MIX[:, sl], SC1B[:, 0:512],
                                               XNB[:, sl])
                    EV.inc(ins)

                if f == 0:
                    @block.tensor
                    def _(t):
                        EV.wait(t)
                        BK[2].wait(t)
                        for kc in range(NCH):
                            ins = t.matmul(PSF[2][0:64, 0:512],
                                           WSD1[:, 64 * kc:64 * (kc + 1)],
                                           MIX[:, TB * kc:TB * (kc + 1)],
                                           start=(kc == 0), stop=(kc == 7))
                        MMG.inc(ins)

                    @block.scalar
                    def _(s):
                        MMG.wait(s)
                        ins = s.activation(T1B[:, :], PSF[2][0:64, 0:512], AF.Tanh)
                        EA.inc(ins)
                        BK[2].inc(s.nop())

                    for s4 in range(4):
                        b = s4 % 2

                        @block.tensor
                        def _(t, s4=s4, b=b):
                            if s4 == 0:
                                EA.wait(t)
                            BK[b].wait(t)
                            ins = t.matmul(PSF[b][:, 0:512],
                                           WSD2[:, 128 * s4:128 * (s4 + 1)],
                                           T1B[:, :], start=True, stop=True)
                            MMG.inc(ins)

                        @block.vector
                        def _(v, s4=s4, b=b):
                            MMG.wait(v)
                            ins = v.tensor_scalar(
                                WD[:, TB * s4:TB * (s4 + 1)], PSF[b][:, 0:512],
                                TMC[:, NCH * 5 + s4:NCH * 5 + s4 + 1],
                                None, op0=ALU.add)
                            BK[b].inc(ins)
                            if s4 == 3:
                                EV.inc(v.engine_nop())

                    @block.tensor
                    def _(t):
                        EV.wait(t)
                        BK[3].wait(t)
                        for s4 in range(4):
                            ins = t.matmul(PSF[3][0:8, 0:512],
                                           HIND[:, 8 * s4:8 * (s4 + 1)],
                                           WD[:, TB * s4:TB * (s4 + 1)],
                                           start=(s4 == 0), stop=(s4 == 3))
                        MMG.inc(ins)

                    @block.vector
                    def _(v):
                        MMG.wait(v)
                        ins = v.tensor_copy(WH[:, :], PSF[3][0:8, 0:512])
                        BK[3].inc(ins)
                        EV.inc(v.engine_nop())
                else:
                    for s4 in range(4):
                        b = s4 % 2
                        wsl = [None, WSK, WSV, WSR][f]

                        @block.tensor
                        def _(t, f=f, s4=s4, b=b, wsl=wsl):
                            if s4 == 0:
                                EV.wait(t)
                            BK[b].wait(t)
                            for kc in range(NCH):
                                ins = t.matmul(
                                    PSF[b][:, 0:512],
                                    wsl[:, 512 * kc + 128 * s4:
                                        512 * kc + 128 * (s4 + 1)],
                                    MIX[:, TB * kc:TB * (kc + 1)],
                                    start=(kc == 0), stop=(kc == 7))
                            MMG.inc(ins)

                        @block.vector
                        def _(v, f=f, s4=s4, b=b, dst=dst):
                            MMG.wait(v)
                            ins = v.tensor_copy(dst[:, TB * s4:TB * (s4 + 1)],
                                                PSF[b][:, 0:512])
                            BK[b].inc(ins)
                            if s4 == 3:
                                EV.inc(v.engine_nop())

            for (srcf, dstf) in ((KF, KTM), (VF, VTM)):
                for s4 in range(4):
                    bb = s4 % 2

                    @block.tensor
                    def _(t, srcf=srcf, s4=s4, bb=bb):
                        if srcf is KF and s4 == 0:
                            EV.wait(t)
                        BK[6 + bb].wait(t)
                        for qt in range(4):
                            ins = t.transpose(
                                PSB[bb][:, 128 * qt:128 * (qt + 1)],
                                srcf[:, TB * s4 + 128 * qt:TB * s4 + 128 * (qt + 1)],
                                IDB[:, :])
                        MMG.inc(ins)

                    @block.vector
                    def _(v, dstf=dstf, s4=s4, bb=bb):
                        MMG.wait(v)
                        dst = AP(dstf[:, :].tensor,
                                 dstf[:, :].offset + 128 * s4,
                                 [[4 * TB, 128], [512, 4], [1, 128]])
                        src = PSB[bb][:, 0:512].rearrange("p (g f) -> p g f", f=128)
                        ins = v.tensor_copy(dst, src)
                        BK[6 + bb].inc(ins)
                        if dstf is VTM and s4 == 3:
                            EV.inc(v.engine_nop())

            for cc in range(2):
                c_glob = 2 * it + cc

                @block.scalar
                def _(s, cc=cc):
                    EV.wait(s)
                    EA.inc(s.activation(WCC[:, :], WH[0:8, 256 * cc:256 * (cc + 1)],
                                        AF.Exp))

                @block.vector
                def _(v, cc=cc):
                    EA.wait(v)
                    if wrows_rd["n"] is not None:
                        v.wait_ge(MMG.h, wrows_rd["n"])
                    edge(v, v.tensor_scalar_mul(WCC[:, :], WCC[:, :], -1.0))
                    edge(v, v.tensor_tensor_scan(WCUM[:, :], WCC[:, :],
                                                 ZR256[:, :], 0.0,
                                                 ALU.add, ALU.add))
                    edge(v, v.tensor_sub(WROWS[0:8, 0:256], WCC[:, :],
                                         WCUM[:, :]))
                    edge(v, v.tensor_copy(WROWS[0:8, 256:512], WCC[:, :]))
                    EV.inc(v.tensor_copy(WROWS[0:8, 512:513],
                                         WCUM[:, 255:256]))
                wexp_ev = EV.n

                @block.scalar
                def _(s, cc=cc):
                    EV.wait(s)
                    s.activation(WEXPA[:, :], WROWS[0:8, 0:256], AF.Exp,
                                 scale=-1.0)
                    EA.inc(s.activation(WEXPB[:, :], WROWS[0:8, 256:512],
                                        AF.Exp))
                wexp_ea = EA.n

                @block.scalar
                def _(s, cc=cc):
                    EV.wait(s)
                    EA.inc(s.activation(WINT[:, :], WCUM[:, :], AF.Exp,
                                        scale=-1.0,
                                        bias=WCUM[:, 255:256]))

                @block.tensor
                def _(t, cc=cc):
                    EA.wait(t)
                    BK[2].wait(t)
                    for qt in range(2):
                        t.transpose(PSF[2][0:128, 16 * qt:16 * qt + 8],
                                    WINT[0:8, 128 * qt:128 * (qt + 1)],
                                    IDFb[0:8, 0:8])
                        ins = t.transpose(PSF[2][0:128, 16 * qt + 8:16 * qt + 16],
                                          WCUM[0:8, 128 * qt:128 * (qt + 1)],
                                          IDFb[0:8, 0:8])
                    ins = t.transpose(PSF[2][0:1, 40:48],
                                      WCUM[0:8, 255:256], IDFb[0:8, 0:8])
                    MMG.inc(ins)

                @block.vector
                def _(v, cc=cc):
                    MMG.wait(v)
                    for qt in range(2):
                        v.tensor_copy(WIT[:, 8 * (2 * cc + qt):8 * (2 * cc + qt) + 8],
                                      PSF[2][0:128, 16 * qt:16 * qt + 8])
                        v.tensor_copy(
                            WCT[:, 8 * (2 * cc + qt):8 * (2 * cc + qt) + 8],
                            PSF[2][0:128, 16 * qt + 8:16 * qt + 16])
                    ins = v.tensor_copy(WSROW[0:1, 0:8], PSF[2][0:1, 40:48])
                    BK[2].inc(ins)
                    EV.inc(v.engine_nop())

                @block.vector
                def _(v, cc=cc):
                    EV.wait(v)
                    for qt in range(2):
                        qcol = 512 * (2 * cc + qt)
                        ins = v.tensor_mul(
                            KW[:, qcol:qcol + 512].rearrange("p (h n) -> p h n", n=64),
                            KTM[:, qcol:qcol + 512].rearrange("p (h n) -> p h n", n=64),
                            bcast_inner(WIT[:, 8 * (2 * cc + qt):
                                            8 * (2 * cc + qt) + 8], 64))
                    EV.inc(ins)

                for h in range(8):
                    hp = h % 2
                    hc = h // 2

                    @block.tensor
                    def _(t, h=h, cc=cc):
                        if h == 0:
                            EV.wait(t)
                        BK[4].wait(t)
                        for qt in range(2):
                            qcol = 512 * (2 * cc + qt)
                            ins = t.matmul(PSF[4][0:64, 0:64],
                                           KW[:, qcol + 64 * h:qcol + 64 * (h + 1)],
                                           VTM[:, qcol + 64 * h:qcol + 64 * (h + 1)],
                                           start=(qt == 0), stop=(qt == 1))
                        MMG.inc(ins)

                    @block.vector
                    def _(v, h=h):
                        MMG.wait(v)
                        ins = v.tensor_copy(KV[:, 64 * h:64 * (h + 1)],
                                            PSF[4][0:64, 0:64])
                        BK[4].inc(ins)

                    @block.tensor
                    def _(t, h=h, wexp_ea=wexp_ea):
                        t.wait_ge(EA.h, wexp_ea)
                        BK[4].wait(t)
                        ins = t.matmul(PSF[4][0:64, 0:256],
                                       IND64[0:8, 64 * h:64 * (h + 1)],
                                       WEXPB[0:8, 0:256],
                                       start=True, stop=True)
                        MMG.inc(ins)

                    @block.tensor
                    def _(t, h=h, cc=cc, wexp_ev=wexp_ev):
                        t.wait_ge(EV.h, wexp_ev)
                        BK[0].wait(t)
                        ins = t.matmul(PSF[0][:, 0:256],
                                       IND128[0:8, 128 * h:128 * (h + 1)],
                                       WROWS[0:8, 0:256],
                                       start=True, stop=True)
                        MMG.inc(ins)

                    @block.vector
                    def _(v, h=h, cc=cc):
                        MMG.wait(v)
                        for kt in range(2):
                            wcol = 8 * (2 * cc + kt) + h
                            ins = v.tensor_scalar(
                                MSK[:, 256 * kt:256 * (kt + 1)],
                                PSF[0][:, 0:256],
                                WCT[:, wcol:wcol + 1], None, op0=ALU.add)
                        BK[0].inc(ins)
                        EV.inc(v.engine_nop())

                    @block.scalar
                    def _(s, h=h):
                        EV.wait(s)
                        s.activation(MSK[:, 0:512], MSK[:, 0:512], AF.Relu)
                        EA.inc(s.activation(MSK[:, 0:512], MSK[:, 0:512],
                                            AF.Exp, scale=-1.0))

                    @block.vector
                    def _(v, h=h):
                        EA.wait(v)
                        EV.inc(v.tensor_mul(MSK[:, 0:512], MSK[:, 0:512],
                                            STT[:, 0:512]))

                    for kt in range(2):
                        @block.tensor
                        def _(t, h=h, cc=cc, kt=kt, hp=hp, hc=hc):
                            if kt == 0:
                                EV.wait(t)
                            BK[1 + kt].wait(t)
                            ins = t.matmul(
                                PSF[1 + kt][:, 0:256],
                                KF[64 * hp:64 * hp + 64,
                                   TB * hc + 256 * cc + 128 * kt:
                                   TB * hc + 256 * cc + 128 * (kt + 1)],
                                RF[64 * hp:64 * hp + 64,
                                   TB * hc + 256 * cc:TB * hc + 256 * (cc + 1)],
                                start=True, stop=True,
                                tile_position=(64 * hp, 0))
                            MMG.inc(ins)

                        @block.vector
                        def _(v, kt=kt):
                            MMG.wait(v)
                            ins = v.tensor_mul(SCT[:, 256 * kt:256 * (kt + 1)],
                                               PSF[1 + kt][:, 0:256],
                                               MSK[:, 256 * kt:256 * (kt + 1)])
                            BK[1 + kt].inc(ins)
                            if kt == 1:
                                EV.inc(v.engine_nop())

                    @block.tensor
                    def _(t, h=h, wexp_ea=wexp_ea):
                        t.wait_ge(EA.h, wexp_ea)
                        BK[0].wait(t)
                        ins = t.matmul(PSF[0][0:64, 0:256],
                                       IND64[0:8, 64 * h:64 * (h + 1)],
                                       WEXPA[0:8, 0:256],
                                       start=True, stop=True)
                        MMG.inc(ins)

                    @block.vector
                    def _(v, h=h, cc=cc, hp=hp, hc=hc):
                        MMG.wait(v)
                        ins = v.tensor_mul(
                            RSW[:, :],
                            RF[64 * hp:64 * hp + 64,
                               TB * hc + 256 * cc:TB * hc + 256 * (cc + 1)],
                            PSF[0][0:64, 0:256])
                        BK[0].inc(ins)
                        EV.inc(v.engine_nop())

                    @block.tensor
                    def _(t, h=h, cc=cc, hp=hp, hc=hc):
                        EV.wait(t)
                        BK[5].wait(t)
                        for kt in range(2):
                            qcol = 512 * (2 * cc) + 0
                            t.matmul(
                                PSF[5][64 * hp:64 * hp + 64, 0:256],
                                VTM[:, 512 * (2 * cc + kt) + 64 * h:
                                    512 * (2 * cc + kt) + 64 * (h + 1)],
                                SCT[:, 256 * kt:256 * (kt + 1)],
                                start=(kt == 0), stop=False,
                                tile_position=(0, 64 * hp))
                        ins = t.matmul(
                            PSF[5][64 * hp:64 * hp + 64, 0:256],
                            STB[:, 64 * h:64 * (h + 1)],
                            RSW[:, :], start=False, stop=True,
                            tile_position=(0, 64 * hp))
                        MMG.inc(ins)

                    @block.vector
                    def _(v, h=h, cc=cc, hp=hp, hc=hc):
                        sl = slice(TB * hc + 256 * cc, TB * hc + 256 * (cc + 1))
                        v.tensor_mul(SC2B[:, 0:256], RF[64 * hp:64 * hp + 64, sl],
                                     KF[64 * hp:64 * hp + 64, sl])
                        EV.inc(v.tensor_scalar(SC2B[:, 0:256], SC2B[:, 0:256],
                                               FAA[:, h:h + 1], None,
                                               op0=ALU.mult))

                    @block.tensor
                    def _(t, h=h):
                        EV.wait(t)
                        BK[3].wait(t)
                        ins = t.matmul(PSF[3][0:1, 0:256], ONESCb[0:64, 0:1],
                                       SC2B[:, 0:256], start=True, stop=True)
                        MMG.inc(ins)

                    @block.vector
                    def _(v, h=h):
                        MMG.wait(v)
                        ins = v.tensor_copy(DGR[0:1, 512:768],
                                            PSF[3][0:1, 0:256])
                        BK[3].inc(ins)

                    @block.tensor
                    def _(t, h=h):
                        BK[3].wait(t)
                        BK[0].wait(t)
                        ins = t.matmul(PSF[0][0:64, 0:256], ONES1b[0:1, 0:64],
                                       DGR[0:1, 512:768], start=True, stop=True)
                        MMG.inc(ins)

                    @block.vector
                    def _(v, h=h, cc=cc, hp=hp, hc=hc):
                        MMG.wait(v)
                        sl = slice(TB * hc + 256 * cc, TB * hc + 256 * (cc + 1))
                        ins = v.tensor_mul(SC2B[:, 256:512],
                                           VF[64 * hp:64 * hp + 64, sl],
                                           PSF[0][0:64, 0:256])
                        BK[0].inc(ins)
                        ins = v.tensor_mul(SC2B[:, 256:512], SC2B[:, 256:512],
                                           PSF[4][0:64, 0:256])
                        BK[4].inc(ins)
                        ins = v.tensor_add(
                            YC[64 * hp:64 * hp + 64, 256 * hc:256 * (hc + 1)],
                            PSF[5][64 * hp:64 * hp + 64, 0:256],
                            SC2B[:, 256:512])
                        BK[5].inc(ins)
                        EV.inc(v.engine_nop())

                wrows_rd["n"] = MMG.n

                @block.scalar
                def _(s, cc=cc):
                    EV.wait(s)
                    EA.inc(s.activation(WSROW[0:1, 8:16], WSROW[0:1, 0:8], AF.Exp))

                @block.vector
                def _(v, cc=cc):
                    EA.wait(v)
                    ins = v.tensor_copy(
                        AP(WSROW[0:1, :].tensor, WSROW[0:1, :].offset + 16,
                           [[768, 1], [64, 8], [1, 64]]),
                        bcast_inner(WSROW[0:1, 8:16], 64))
                    EV.inc(ins)

                @block.tensor
                def _(t, cc=cc):
                    EV.wait(t)
                    BK[0].wait(t)
                    ins = t.matmul(PSF[0][0:64, 0:512], ONES1b[0:1, 0:64],
                                   WSROW[0:1, 16:528], start=True, stop=True)
                    MMG.inc(ins)

                @block.vector
                def _(v, cc=cc):
                    MMG.wait(v)
                    ins = v.tensor_mul(STATE[:, :], STATE[:, :], PSF[0][0:64, 0:512])
                    BK[0].inc(ins)
                    v.tensor_add(STATE[:, :], STATE[:, :], KV[:, :])
                    EV.inc(v.tensor_copy(STB[:, :], STATE[:, :]))

                if y_sink[0] == "dram":
                    y_out = y_sink[1]

                    @block.sync
                    def _(sy, cc=cc, c_glob=c_glob):
                        EV.wait(sy)
                        for j4 in range(4):
                            ST.inc(sy.dma_start(
                                out=y_out[j4, :, 256 * c_glob:256 * (c_glob + 1)],
                                in_=YC[:, 256 * j4:256 * (j4 + 1)]), 16)
                else:
                    ysh, YIW = y_sink[1], y_sink[2]

                    @block.gpsimd
                    def _(g, cc=cc, c_glob=c_glob):
                        EV.wait(g)
                        for j4 in range(4):
                            col = 4 * j4 + c_glob // 4
                            ST.inc(g.indirect_dma_start(
                                out=ysh[:, :],
                                out_offset=IndirectOffsetOnAxis(
                                    YIW[:, col:col + 1], 0),
                                in_=YC[:, 256 * j4:256 * (j4 + 1)],
                                in_offset=None,
                                element_offset=256 * (c_glob % 4)), 16)

        @block.sync
        def _(sy):
            ST.wait(sy)

    return {"ST": ST, "EV": EV}




def prep_B_inputs(inp, x1_full, core):
    b, hg = core // 2, core % 2
    w = {}
    w["x1b"] = np.ascontiguousarray(
        x1_full[4096 * b:4096 * (b + 1)].T.reshape(NCH, 128, 4096).astype(np.float32))
    cols = []
    for key in ["L1_attn_time_maa_x", "L1_attn_time_maa_w", "L1_attn_time_maa_k",
                "L1_attn_time_maa_v", "L1_attn_time_maa_r"]:
        cols.append(np.asarray(inp[key], np.float32).reshape(C).reshape(NCH, 128).T)
    td = np.asarray(inp["L1_attn_time_decay"], np.float32).reshape(C)
    tds = td[512 * hg:512 * (hg + 1)].reshape(4, 128).T
    tmcd = np.zeros((128, 6 * NCH), np.float32)
    for i, c_ in enumerate(cols):
        tmcd[:, NCH * i:NCH * (i + 1)] = c_
    tmcd[:, NCH * 5:NCH * 5 + 4] = tds
    w["tmcd"] = np.ascontiguousarray(tmcd)
    w["maw1"] = np.ascontiguousarray(np.asarray(inp["L1_attn_maa_w1"], np.float32))
    w["maw2"] = np.ascontiguousarray(
        np.asarray(inp["L1_attn_maa_w2"], np.float32).reshape(128, C))
    w["dcw1"] = np.ascontiguousarray(np.asarray(inp["L1_attn_decay_w1"], np.float32))
    w["dcw2"] = np.ascontiguousarray(
        np.asarray(inp["L1_attn_decay_w2"], np.float32)[:, 512 * hg:512 * (hg + 1)])
    faa = np.asarray(inp["L1_attn_time_faaaa"], np.float32).reshape(16, 64)
    w["faaT"] = np.ascontiguousarray(faa[8 * hg:8 * (hg + 1)].T)
    for nm, key in (("wr_", "L1_attn_Wr"), ("wk_", "L1_attn_Wk"),
                    ("wv_", "L1_attn_Wv")):
        w[nm] = np.ascontiguousarray(
            np.asarray(inp[key], np.float32)[:, 512 * hg:512 * (hg + 1)])
    hindm = np.zeros((128, 32), np.float32)
    for s4 in range(4):
        for p in range(128):
            hindm[p, 8 * s4 + 2 * s4 + p // 64] = 1.0 / 64
    w["hindm"] = hindm
    st = np.zeros((2, 128, 256), np.float32)
    for kt in range(2):
        for p in range(128):
            st[kt, p, 128 * kt + p + 1:] = 1.0
    w["strictT"] = st
    return w


def run_launch_B(inp, x1_full):
    nc = build_launch_B()
    in_maps = [prep_B_inputs(inp, x1_full, c) for c in range(N_CORES)]
    res = run_bass_kernel_spmd(nc, in_maps, core_ids=list(range(N_CORES)))
    y = np.empty((16384, C), np.float32)
    for core in range(N_CORES):
        b, hg = core // 2, core % 2
        yo = res.results[core]["y_out"]
        y[4096 * b:4096 * (b + 1), 512 * hg:512 * (hg + 1)] = \
            yo.reshape(512, 4096).T
    return y





def declare_C_drams(nc):
    dr = {}
    dr["lngb"] = nc.dram_tensor("lngb", [128, 2 * NCH], F32,
                                kind="ExternalInput").ap()
    dr["wo"] = nc.dram_tensor("wo", [C, C], F32, kind="ExternalInput").ap()
    dr["scrow"] = nc.dram_tensor("scrow", [1, 16384], F32,
                                 kind="ExternalInput").ap()
    dr["xn1f"] = nc.dram_tensor("xn1f", [16384, C], BF16,
                                kind="ExternalInput").ap()
    dr["w1e"] = nc.dram_tensor("w1e", [C, 4096], F32, kind="ExternalInput").ap()
    dr["b1e"] = nc.dram_tensor("b1e", [128, 32], F32, kind="ExternalInput").ap()
    dr["w2e"] = nc.dram_tensor("w2e", [4096, C], F32, kind="ExternalInput").ap()
    dr["b2e"] = nc.dram_tensor("b2e", [1, C], F32, kind="ExternalInput").ap()
    dr["triu"] = nc.dram_tensor("triu", [128, 128], F32,
                                kind="ExternalInput").ap()
    dr["s1_out"] = nc.dram_tensor("s1_out", [NCH, 128, P], F32,
                                  kind="ExternalOutput").ap()
    dr["co_out"] = nc.dram_tensor("co_out", [16384, C], BF16,
                                  kind="ExternalOutput").ap()
    dr["idx_d"] = nc.dram_tensor("idx_d", [2049, 2], F32).ap()
    return dr


def build_launch_C():
    nc = bass.Bass("TRN2", target_bir_lowering=False)
    dr = declare_C_drams(nc)
    dr["y_in"] = nc.dram_tensor("y_in", [NCH, 128, P], F32,
                                kind="ExternalInput").ap()
    alP = Alloc(nc)
    alC = Alloc(nc)
    with nc.Block() as block:
        emit_C_part(nc, block, alC, alP, dr, ("dram", dr["y_in"]))
    alC.close()
    alP.close()
    return nc


def emit_C_part(nc, block, alC, alP, dr, y_src, prezero=None):
    fused = y_src[0] != "dram"
    lngb = dr["lngb"]; wo = dr["wo"]; scrow = dr["scrow"]; xn1f = dr["xn1f"]
    w1e = dr["w1e"]; b1e = dr["b1e"]; w2e = dr["w2e"]; b2e = dr["b2e"]
    triu = dr["triu"]; s1_out = dr["s1_out"]; co_out = dr["co_out"]
    idx_d = dr["idx_d"]
    if fused:
        _, ysh, BARC, JN, yidxr = y_src
    else:
        y_in = y_src[1]
    sb, ps = alC.sb, alC.ps
    _tag = "c" if fused else ""

    def sem(n):
        return alP.sem(n + _tag)

    if not fused:
        CONSTc = alP.sb("CONSTc", [1, 4], F32)
    if fused:
        YIR = alP.sb("YIR", [128, 16], I32)

    YF = sb("YF", [128, NCH * HT], F32)
    S1 = sb("S1", [128, NCH * HT], F32)
    WSC = sb("WSC", [128, 8192], BF16)
    XG = sb("XG", [128, NCH * P], BF16)
    YLN = XG[:, 0:NCH * HT]
    XGR = sb("XGR", [128, 2048], BF16)
    XGF = sb("XGF", [128, 2048], F32)
    H1 = sb("H1", [128, 32 * 512], BF16)
    CONTRIB = sb("CONTRIB", [128, 4 * 1024], BF16)
    SRWc = sb("SRWc", [1, 3 * HT], F32)
    SCL = sb("SCL", [128, 128], F32)
    TAU = sb("TAU", [128, 8], F32)
    CMP = sb("CMP", [128, 128], F32)
    MSKC = sb("MSKC", [128, 128], F32)
    RNK = sb("RNK", [128, 128], F32)
    VALS = sb("VALS", [128, 256], F32)
    RIDXI = sb("RIDXI", [128, 128], I32)
    TOKI = sb("TOKI", [128, 16], I32)
    TOKF = sb("TOKF", [128, 32], F32)
    GCOL = sb("GCOL", [128, 16], F32)
    TOKV = sb("TOKV", [128, 128], I32)
    TRIU = sb("TRIU", [128, 128], F32)
    B1C = sb("B1C", [128, 32], F32)
    B2R = sb("B2R", [1, C], F32)
    ZB = sb("ZB", [128, 1024], F32)
    SC1c = sb("SC1c", [128, 2048], F32)
    ONES1c = sb("ONES1c", [1, 128], F32)
    ONESCc = sb("ONESCc", [128, 1], F32)
    LNG = sb("LNG", [128, 2 * NCH], F32)
    IDFc = sb("IDFc", [128, 128], F32)

    PSC = [ps(f"PCC{i}", [128, 512], F32) for i in range(8)]

    LX = sem("LX")
    LW = sem("LW")
    EV = sem("EV")
    EA = sem("EA")
    MMG = sem("MMG")
    ST = sem("ST")
    VE = sem("VE")
    GD = sem("GD")
    BK = [sem(f"BKC{i}") for i in range(8)]
    LP = [sem(f"LPC{i}") for i in range(4)]
    LQ = [sem(f"LQC{i}") for i in range(8)]

    SUMS = MEAN = MR = SRWc[0:1, 0 * HT:1 * HT]
    SSQ = VAR = SRWc[0:1, 1 * HT:2 * HT]
    RSCR = RSTD = SRWc[0:1, 2 * HT:3 * HT]

    def slab_ap(dram, r0, nr, c0, ncols, row_len):
        t = dram.tensor
        off = dram.offset + r0 * row_len + c0
        return AP(t, off, [[row_len, 128], [128 * row_len, max(nr // 128, 1)],
                           [1, ncols]])

    def ws_dst(col0, nkc, ncols):
        full = WSC[:, :]
        return AP(full.tensor, full.offset + col0,
                  [[8192, 128], [ncols, nkc], [1, ncols]])

    if True:

        if not fused:
            nc.const_aps.register(F32, LNEPS, CONSTc[0:1, 0:1])

        def edge(eng, inst):
            VE.inc(inst)
            VE.wait(eng)

        if fused:
            @block.tensor
            def _(t):
                t.wait_ge(JN.h, 4)

            @block.vector
            def _(v):
                v.wait_ge(JN.h, 4)

            @block.scalar
            def _(s):
                s.wait_ge(JN.h, 4)

            @block.sync
            def _(sy):
                sy.wait_ge(JN.h, 4)

        @block.gpsimd
        def _(g):
            if fused:
                g.wait_ge(JN.h, 4)
            g.memset(ONES1c[:, :], 1.0)
            g.memset(ONESCc[:, :], 1.0)
            g.memset(ZB[:, :], 0.0)
            if not fused:
                g.memset(CONSTc[0:1, 0:1], LNEPS)
            g.memset(TAU[:, 0:1], 0.0)
            g.memset(TAU[:, 1:2], 1.01)
            g.memset(TAU[:, 2:3], 0.505)
            LW.inc(g.iota(TOKV[:, :], pattern=[[1, 128]], base=0,
                          channel_multiplier=128))
            g.wait_ge(LW.h, 1)
            LW.inc(g.memset(IDFc[:, :], 1.0))
            g.wait_ge(LW.h, 2)
            LW.inc(g.affine_select(IDFc[:, :], IDFc[:, :], pattern=[[1, 128]],
                                   compare_op=ALU.is_equal, fill=0.0,
                                   channel_multiplier=-1))
            LP[0].inc(g.dma_start(out=ws_dst(0, NCH, C),
                                  in_=slab_ap(wo, 0, C, 0, C, C)), 16)

        @block.sync
        def _(sy):
            LX.inc(sy.dma_start(out=TRIU[:, :], in_=triu[:, :]), 16)
            LX.inc(sy.dma_start(out=LNG[:, :], in_=lngb[:, :]), 16)
            LX.inc(sy.dma_start(out=B1C[:, :], in_=b1e[:, :]), 16)
            LX.inc(sy.dma_start(out=B2R[0:1, :], in_=b2e[0:1, :]), 16)
            LX.inc(sy.dma_start(
                out=AP(SCL[:, :].tensor, SCL[:, :].offset,
                       [[128, 128], [1, 128]]),
                in_=scrow[0:1, :]), 16)
            if fused:
                LX.inc(sy.dma_start(out=YIR[:, :], in_=yidxr[:, :]), 16)

        for h in range(2):
            t0 = h * HT

            if not fused:
                @block.sync
                def _(sy, h=h, t0=t0):
                    EV.wait(sy)
                    EA.wait(sy)
                    for kc in range(NCH):
                        LX.inc(sy.dma_start(out=YF[:, HT * kc:HT * (kc + 1)],
                                            in_=y_in[kc, :, t0:t0 + HT]), 16)
            else:
                @block.gpsimd
                def _(g, h=h):
                    EV.wait(g)
                    EA.wait(g)
                    if h == 0:
                        g.wait_ge(BARC.h, 1)
                        g.wait_ge(LX.h, 96)
                    for kc in range(NCH):
                        LX.inc(g.indirect_dma_start(
                            out=YF[:, HT * kc:HT * (kc + 1)], out_offset=None,
                            in_=ysh[:, :],
                            in_offset=IndirectOffsetOnAxis(
                                YIR[:, 2 * kc + h:2 * kc + h + 1], 0)), 16)
            lx_t = LX.n

            for tt in range(2):
                for grp in range(2):
                    @block.scalar
                    def _(s, tt=tt, grp=grp, lx_t=lx_t):
                        s.wait_ge(LX.h, lx_t)
                        EV.wait(s)
                        if grp == 1 or tt == 1:
                            MMG.wait(s)
                        for kk in range(4):
                            kc = grp * 4 + kk
                            ins = s.activation(
                                SC1c[:, 512 * kk:512 * (kk + 1)],
                                YF[:, HT * kc + 512 * tt:HT * kc + 512 * tt + 512],
                                AF.Square)
                        EA.inc(ins)

                    @block.tensor
                    def _(t, tt=tt, grp=grp):
                        EA.wait(t)
                        if tt + grp == 0:
                            BK[4].wait(t)
                            BK[5].wait(t)
                        for kk in range(4):
                            kc = grp * 4 + kk
                            t.matmul(PSC[4][0:1, 0:512], ONESCc[:, 0:1],
                                     YF[:, HT * kc + 512 * tt:
                                        HT * kc + 512 * tt + 512],
                                     start=(kc == 0), stop=(kc == 7))
                            ins = t.matmul(PSC[5][0:1, 0:512], ONESCc[:, 0:1],
                                           SC1c[:, 512 * kk:512 * (kk + 1)],
                                           start=(kc == 0), stop=(kc == 7))
                        MMG.inc(ins)

                @block.vector
                def _(v, tt=tt):
                    MMG.wait(v)
                    v.tensor_copy(SUMS[0:1, 512 * tt:512 * (tt + 1)],
                                  PSC[4][0:1, 0:512])
                    ins = v.tensor_copy(SSQ[0:1, 512 * tt:512 * (tt + 1)],
                                        PSC[5][0:1, 0:512])
                    BK[4].inc(ins)
                    BK[5].inc(v.engine_nop())

            @block.vector
            def _(v):
                EV.wait(v)
                MMG.wait(v)
                edge(v, v.tensor_scalar_mul(MEAN, SUMS, 1.0 / C))
                edge(v, v.tensor_scalar_mul(VAR, SSQ, 1.0 / C))
                edge(v, v.tensor_mul(RSCR, MEAN, MEAN))
                EV.inc(v.tensor_sub(VAR, VAR, RSCR))

            @block.scalar
            def _(s):
                EV.wait(s)
                s.wait_ge(LW.h, 3)
                edge(s, s.activation(RSCR, VAR, AF.Ln, bias=LNEPS))
                EA.inc(s.activation(RSTD, RSCR, AF.Exp, scale=-0.5))

            @block.vector
            def _(v):
                EA.wait(v)
                EV.inc(v.tensor_mul(MR, MEAN, RSTD))

            for tt in range(2):
                @block.tensor
                def _(t, tt=tt):
                    EV.wait(t)
                    BK[4].wait(t)
                    BK[5].wait(t)
                    t.matmul(PSC[4][:, 0:512], ONES1c[0:1, :],
                             RSTD[0:1, 512 * tt:512 * (tt + 1)],
                             start=True, stop=True)
                    MMG.inc(t.matmul(PSC[5][:, 0:512], ONES1c[0:1, :],
                                     MR[0:1, 512 * tt:512 * (tt + 1)],
                                     start=True, stop=True))

                @block.vector
                def _(v, tt=tt):
                    MMG.wait(v)
                    v.wait_ge(LX.h, 80)
                    for kc in range(NCH):
                        sl = slice(HT * kc + 512 * tt, HT * kc + 512 * tt + 512)
                        v.tensor_mul(SC1c[:, 0:512], YF[:, sl], PSC[4][:, 0:512])
                        v.tensor_sub(SC1c[:, 512:1024], SC1c[:, 0:512],
                                     PSC[5][:, 0:512])
                        v.tensor_scalar(SC1c[:, 0:512], SC1c[:, 512:1024],
                                        LNG[:, kc:kc + 1], None, op0=ALU.mult)
                        ins = v.tensor_scalar(YLN[:, sl], SC1c[:, 0:512],
                                              LNG[:, NCH + kc:NCH + kc + 1],
                                              None, op0=ALU.add)
                    BK[4].inc(ins)
                    BK[5].inc(v.engine_nop())
                    EV.inc(v.engine_nop())

            for ds in range(NCH):
                for tt in range(2):
                    gi = 2 * ds + tt
                    b = gi % 2

                    @block.tensor
                    def _(t, ds=ds, tt=tt, b=b, gi=gi):
                        if gi == 0:
                            EV.wait(t)
                            LP[0].wait(t)
                        BK[b].wait(t)
                        for kc in range(NCH):
                            ins = t.matmul(
                                PSC[b][:, 0:512],
                                WSC[:, C * kc + 128 * ds:C * kc + 128 * (ds + 1)],
                                YLN[:, HT * kc + 512 * tt:HT * kc + 512 * tt + 512],
                                start=(kc == 0), stop=(kc == 7))
                        MMG.inc(ins)

                    @block.vector
                    def _(v, ds=ds, tt=tt, b=b):
                        MMG.wait(v)
                        if ds == 0 and tt == 0:
                            ST.wait(v)
                        ins = v.tensor_copy(
                            S1[:, HT * ds + 512 * tt:HT * ds + 512 * (tt + 1)],
                            PSC[b][:, 0:512])
                        BK[b].inc(ins)
                        if ds == NCH - 1 and tt == 1:
                            EV.inc(v.engine_nop())

            @block.sync
            def _(sy, h=h, t0=t0):
                EV.wait(sy)
                for kc in range(NCH):
                    ST.inc(sy.dma_start(out=s1_out[kc, :, t0:t0 + HT],
                                        in_=S1[:, HT * kc:HT * (kc + 1)]), 16)

        if prezero is None:
            @block.gpsimd
            def _(g):
                g.wait_ge(LW.h, 1)
                for r in range(32):
                    GD.inc(g.dma_start(
                        out=co_out[512 * r:512 * (r + 1), :],
                        in_=AP(ZB[:, :].tensor, ZB[:, :].offset,
                               [[1024, 128], [0, 4], [1, 1024]])), 16)
        gd_zero = GD.n

        for it in range(38):
            @block.vector
            def _(v, it=it):
                if it == 0:
                    v.wait_ge(LW.h, 3)
                    LX.wait(v)
                edge(v, v.tensor_scalar(CMP[:, :], SCL[:, :], TAU[:, 2:3],
                                        None, op0=ALU.is_ge))
                EV.inc(v.reduce_sum(TAU[:, 4:5], CMP[:, :], axis=AX.X))

            @block.tensor
            def _(t, it=it):
                EV.wait(t)
                BK[2].wait(t)
                MMG.inc(t.matmul(PSC[2][0:1, 0:1], TAU[:, 4:5], ONESCc[:, 0:1],
                                 start=True, stop=True))

            @block.vector
            def _(v, it=it):
                MMG.wait(v)
                ins = v.tensor_copy(TAU[0:1, 5:6], PSC[2][0:1, 0:1])
                BK[2].inc(ins)
                edge(v, v.engine_nop())

            @block.tensor
            def _(t, it=it):
                VE.wait(t)
                BK[3].wait(t)
                MMG.inc(t.matmul(PSC[3][:, 0:1], ONES1c[0:1, :], TAU[0:1, 5:6],
                                 start=True, stop=True))

            @block.vector
            def _(v, it=it):
                MMG.wait(v)
                ins = v.tensor_scalar(TAU[:, 3:4], PSC[3][:, 0:1], 2048.0, None,
                                      op0=ALU.is_ge)
                BK[3].inc(ins)
                edge(v, v.engine_nop())
                edge(v, v.tensor_sub(TAU[:, 6:7], TAU[:, 2:3], TAU[:, 0:1]))
                edge(v, v.tensor_mul(TAU[:, 6:7], TAU[:, 6:7], TAU[:, 3:4]))
                edge(v, v.tensor_add(TAU[:, 0:1], TAU[:, 0:1], TAU[:, 6:7]))
                edge(v, v.tensor_sub(TAU[:, 6:7], TAU[:, 1:2], TAU[:, 2:3]))
                edge(v, v.tensor_mul(TAU[:, 6:7], TAU[:, 6:7], TAU[:, 3:4]))
                edge(v, v.tensor_add(TAU[:, 1:2], TAU[:, 2:3], TAU[:, 6:7]))
                edge(v, v.tensor_add(TAU[:, 2:3], TAU[:, 0:1], TAU[:, 1:2]))
                edge(v, v.tensor_scalar_mul(TAU[:, 2:3], TAU[:, 2:3], 0.5))
                EV.inc(v.engine_nop())

        @block.vector
        def _(v):
            EV.wait(v)
            edge(v, v.tensor_scalar(MSKC[:, :], SCL[:, :], TAU[:, 0:1], None,
                                    op0=ALU.is_ge))
            edge(v, v.tensor_tensor_scan(RNK[:, :], MSKC[:, :], MSKC[:, :],
                                         0.0, ALU.add, ALU.bypass))
            edge(v, v.tensor_sub(RNK[:, :], RNK[:, :], MSKC[:, :]))
            EV.inc(v.reduce_sum(TAU[:, 7:8], MSKC[:, :], axis=AX.X))

        @block.tensor
        def _(t):
            EV.wait(t)
            t.wait_ge(LX.h, 80)
            BK[2].wait(t)
            MMG.inc(t.matmul(PSC[2][:, 0:1], TRIU[:, :], TAU[:, 7:8],
                             start=True, stop=True))

        @block.vector
        def _(v):
            MMG.wait(v)
            ins = v.tensor_copy(TAU[:, 6:7], PSC[2][:, 0:1])
            BK[2].inc(ins)
            edge(v, v.engine_nop())
            edge(v, v.tensor_scalar(RNK[:, :], RNK[:, :], TAU[:, 6:7], None,
                                    op0=ALU.add))
            edge(v, v.tensor_mul(RNK[:, :], RNK[:, :], MSKC[:, :]))
            edge(v, v.tensor_scalar(CMP[:, :], MSKC[:, :], -300000.0, 300000.0,
                                    op0=ALU.mult, op1=ALU.add))
            edge(v, v.tensor_add(RNK[:, :], RNK[:, :], CMP[:, :]))
            edge(v, v.tensor_copy(RIDXI[:, :], RNK[:, :]))
            edge(v, v.tensor_copy(
                AP(VALS[:, :].tensor, VALS[:, :].offset, [[256, 128], [2, 128]]),
                TOKV[:, :]))
            EV.inc(v.tensor_copy(
                AP(VALS[:, :].tensor, VALS[:, :].offset + 1, [[256, 128], [2, 128]]),
                SCL[:, :]))

        @block.gpsimd
        def _(g):
            EV.wait(g)
            if prezero is None:
                g.wait_ge(GD.h, gd_zero)
            else:
                g.wait_ge(prezero[0].h, prezero[1])
            rbc = g.alloc_register("rbc")
            g.reg_mov(rbc, 2047)
            for f in range(128):
                GD.inc(g.indirect_dma_start(
                    out=idx_d[:, :],
                    out_offset=IndirectOffsetOnAxis(RIDXI[:, f:f + 1], 0),
                    in_=VALS[:, 2 * f:2 * f + 2], in_offset=None,
                    bounds_check=rbc, oob_is_err=False), 16)
            g.wait_ge(GD.h, GD.n)
            GD.inc(g.dma_start(
                out=TOKF[:, :],
                in_=AP(idx_d.tensor, idx_d.offset, [[32, 128], [1, 32]])), 16)

        @block.vector
        def _(v):
            GD.wait(v)
            edge(v, v.tensor_copy(
                TOKI[:, :],
                AP(TOKF[:, :].tensor, TOKF[:, :].offset, [[32, 128], [2, 16]])))
            EV.inc(v.tensor_copy(
                GCOL[:, :],
                AP(TOKF[:, :].tensor, TOKF[:, :].offset + 1, [[32, 128], [2, 16]])))

        ev_g0 = EV.n
        for g16 in range(16):
            @block.gpsimd
            def _(g, g16=g16):
                if g16 == 0:
                    EV.wait(g)
                elif g16 >= 2:
                    g.wait_ge(EV.h, ev_g0 + g16 - 1)
                LP[1 + g16 % 2].inc(g.indirect_dma_start(
                    out=XGR[:, 1024 * (g16 % 2):1024 * (g16 % 2) + 1024],
                    out_offset=None,
                    in_=xn1f[:, :],
                    in_offset=IndirectOffsetOnAxis(TOKI[:, g16:g16 + 1], 0)), 16)

            @block.vector
            def _(v, g16=g16):
                LP[1 + g16 % 2].wait(v)
                if g16 >= 2:
                    MMG.wait(v, MMG.n - 1)
                EV.inc(v.tensor_copy(XGF[:, 1024 * (g16 % 2):
                                         1024 * (g16 % 2) + 1024],
                                     XGR[:, 1024 * (g16 % 2):
                                         1024 * (g16 % 2) + 1024]))

            @block.tensor
            def _(t, g16=g16):
                t.wait_ge(EV.h, ev_g0 + g16 + 1)
                BK[2].wait(t)
                BK[3].wait(t)
                sl0 = 1024 * (g16 % 2)
                for kc in range(NCH):
                    b = 2 + kc // 4
                    ins = t.transpose(
                        PSC[b][:, 128 * (kc % 4):128 * (kc % 4) + 128],
                        XGF[:, sl0 + 128 * kc:sl0 + 128 * (kc + 1)],
                        IDFc[:, :])
                MMG.inc(ins)

            @block.vector
            def _(v, g16=g16):
                MMG.wait(v)
                for kc in range(NCH):
                    b = 2 + kc // 4
                    ins = v.tensor_copy(
                        XG[:, P * kc + 128 * g16:P * kc + 128 * (g16 + 1)],
                        PSC[b][:, 128 * (kc % 4):128 * (kc % 4) + 128])
                BK[2].inc(ins)
                BK[3].inc(v.engine_nop())

        w1_free = {}
        w2_free = {}
        for ps_ in range(4):
            tok0 = 512 * ps_

            for ds32 in range(32):
                wslot = 1024 * (ds32 % 4)

                @block.gpsimd
                def _(g, ps_=ps_, ds32=ds32, wslot=wslot):
                    g.wait_ge(MMG.h, w1_free.get((ps_, ds32), MMG.n - 1))
                    LQ[ds32 % 4].inc(
                        g.dma_start(out=ws_dst(wslot, NCH, 128),
                                    in_=slab_ap(w1e, 0, C, 128 * ds32, 128, 4096)),
                        16)

                @block.tensor
                def _(t, ps_=ps_, ds32=ds32, wslot=wslot):
                    LQ[ds32 % 4].wait(t)
                    if ps_ == 0 and ds32 == 0:
                        BK[2].wait(t)
                        BK[3].wait(t)
                    b = ds32 % 2
                    BK[b].wait(t)
                    for kc in range(NCH):
                        ins = t.matmul(
                            PSC[b][:, 0:512],
                            WSC[:, wslot + 128 * kc:wslot + 128 * (kc + 1)],
                            XG[:, P * kc + tok0:P * kc + tok0 + 512],
                            start=(kc == 0), stop=(kc == 7))
                    MMG.inc(ins)
                nxt = (ps_, ds32 + 4) if ds32 + 4 < 32 else (ps_ + 1, ds32 % 4)
                w1_free[nxt] = MMG.n

                @block.scalar
                def _(s, ds32=ds32):
                    MMG.wait(s)
                    s.wait_ge(LX.h, 80)
                    b = ds32 % 2
                    ins = s.activation(H1[:, 512 * ds32:512 * (ds32 + 1)],
                                       PSC[b][:, 0:512], AF.Relu,
                                       bias=B1C[:, ds32:ds32 + 1])
                    BK[b].inc(ins)
                    if ds32 == 31:
                        EA.inc(s.nop())

            @block.tensor
            def _(t, ps_=ps_):
                EA.wait(t)
                t.wait_ge(LX.h, 80)
                for b in range(8):
                    BK[b].wait(t)
                for tt4 in range(4):
                    for nhh in range(2):
                        b = 2 * tt4 + nhh
                        ins = t.matmul(PSC[b][:, 0:512],
                                       ONES1c[0:1, :],
                                       B2R[0:1, 512 * nhh:512 * (nhh + 1)],
                                       start=True, stop=False)
                MMG.inc(ins)

            for kc32 in range(32):
                wslot = 4096 + 1024 * (kc32 % 4)

                @block.gpsimd
                def _(g, ps_=ps_, kc32=kc32, wslot=wslot):
                    g.wait_ge(MMG.h, w2_free.get((ps_, kc32), MMG.n - 1))
                    LQ[4 + kc32 % 4].inc(
                        g.dma_start(out=ws_dst(wslot, 1, C),
                                    in_=slab_ap(w2e, 128 * kc32, 128, 0, C, C)),
                        16)

                @block.tensor
                def _(t, ps_=ps_, kc32=kc32, wslot=wslot):
                    LQ[4 + kc32 % 4].wait(t)
                    for tt4 in range(4):
                        for nhh in range(2):
                            b = 2 * tt4 + nhh
                            ins = t.matmul(
                                PSC[b][:, 0:512],
                                H1[:, 512 * kc32 + 128 * tt4:
                                   512 * kc32 + 128 * (tt4 + 1)],
                                WSC[:, wslot + 512 * nhh:wslot + 512 * (nhh + 1)],
                                start=False, stop=(kc32 == 31))
                    MMG.inc(ins)
                nxt = (ps_, kc32 + 4) if kc32 + 4 < 32 else (ps_ + 1, kc32 % 4)
                w2_free[nxt] = MMG.n

            for tt4 in range(4):
                @block.vector
                def _(v, ps_=ps_, tt4=tt4):
                    MMG.wait(v)
                    if ps_ > 0 and tt4 == 0:
                        GD.wait(v)
                    for nhh in range(2):
                        b = 2 * tt4 + nhh
                        ins = v.tensor_scalar(
                            CONTRIB[:, 1024 * tt4 + 512 * nhh:
                                    1024 * tt4 + 512 * (nhh + 1)],
                            PSC[b][:, 0:512],
                            GCOL[:, 4 * ps_ + tt4:4 * ps_ + tt4 + 1],
                            None, op0=ALU.mult)
                        BK[b].inc(ins)
                    EV.inc(v.engine_nop())

                @block.gpsimd
                def _(g, ps_=ps_, tt4=tt4):
                    EV.wait(g)
                    GD.inc(g.indirect_dma_start(
                        out=co_out[:, :],
                        out_offset=IndirectOffsetOnAxis(
                            TOKI[:, 4 * ps_ + tt4:4 * ps_ + tt4 + 1], 0),
                        in_=CONTRIB[:, 1024 * tt4:1024 * (tt4 + 1)],
                        in_offset=None), 16)

        @block.gpsimd
        def _(g):
            GD.wait(g)

        @block.sync
        def _(sy):
            ST.wait(sy)

    return {"ST": ST, "GD": GD}



def build_launch_BC():
    nc = bass.Bass("TRN2", target_bir_lowering=False)
    drB = declare_B_drams(nc)
    drC = declare_C_drams(nc)
    yidxw = nc.dram_tensor("yidxw", [128, 16], I32, kind="ExternalInput").ap()
    yidxr = nc.dram_tensor("yidxr", [128, 16], I32, kind="ExternalInput").ap()
    fidx = nc.dram_tensor("fidx", [128, 1], I32, kind="ExternalInput").ap()
    sidx = nc.dram_tensor("sidx", [128, 1], I32, kind="ExternalInput").ap()
    fval = nc.dram_tensor("fval", [128, 4], I32, kind="ExternalInput").ap()
    rid = nc.dram_tensor("rid", [1, 4], I32, kind="ExternalInput").ap()
    ysh = nc.dram_tensor("ysh", [NCH * 128 * 4, 1024], F32,
                         addr_space="Shared").ap()
    flgsh = nc.dram_tensor("flgsh", [256, 4], I32, addr_space="Shared").ap()
    zcon = nc.dram_tensor("zcon", [512, C], BF16).ap()

    alP = Alloc(nc)
    JN = alP.sem("JN")
    BARC = alP.sem("BARC")
    FW = alP.sem("FWb")
    PSEM = alP.sem("PSb")
    YIW = alP.sb("YIW", [128, 16], I32)
    FID = alP.sb("FID", [128, 1], I32)
    SID = alP.sb("SID", [128, 1], I32)
    FVAL = alP.sb("FVAL", [128, 4], I32)
    RV = alP.sb("RV", [1, 4], I32)
    FLG = alP.sb("FLG", [128, 4], I32)
    GDZ = alP.sem("GDZ")
    ZBP = alP.sb("ZBP", [128, 1024], F32)

    with nc.Block() as block:
        @block.gpsimd
        def _(g):
            GDZ.inc(g.memset(ZBP[:, :], 0.0))
            g.wait_ge(GDZ.h, 1)
            GDZ.inc(g.dma_start(
                out=zcon[:, :],
                in_=AP(ZBP[:, :].tensor, ZBP[:, :].offset,
                       [[1024, 128], [0, 4], [1, 1024]])), 16)
            FW.inc(g.dma_start(out=YIW[:, :], in_=yidxw[:, :]), 16)
            FW.inc(g.dma_start(out=FID[:, :], in_=fidx[:, :]), 16)
            FW.inc(g.dma_start(out=SID[:, :], in_=sidx[:, :]), 16)
            FW.inc(g.dma_start(out=FVAL[:, :], in_=fval[:, :]), 16)
            FW.inc(g.dma_start(out=RV[0:1, :], in_=rid[0:1, :]), 16)
            FW.wait(g)

        @block.scalar
        def _(s):
            s.wait_ge(GDZ.h, 17)
            co_out_bc = drC["co_out"]
            for r in range(32):
                GDZ.inc(s.dma_start(out=co_out_bc[512 * r:512 * (r + 1), :],
                                    in_=zcon[:, :]), 16)

        alB = Alloc(nc)
        SB = emit_B_part(nc, block, alB, alP, drB, ("shared", ysh, YIW))

        @block.tensor
        def _(t):
            JN.inc(t.nop())

        @block.vector
        def _(v):
            JN.inc(v.engine_nop())

        @block.scalar
        def _(s):
            JN.inc(s.nop())

        @block.sync
        def _(sy):
            JN.inc(sy.nop())

        alB.close()

        @block.gpsimd
        def _(g):
            SB["ST"].wait(g)
            FW.inc(g.indirect_dma_start(
                out=flgsh[:, :],
                out_offset=IndirectOffsetOnAxis(FID[:, 0:1], 0),
                in_=FVAL[:, :], in_offset=None), 16)
            FW.wait(g)
            with (
                g.register("cnt") as cnt,
                g.register("go") as go,
                g.register("val") as val,
                g.register("rr") as rr,
            ):
                g.reg_mov(cnt, 0)
                g.reg_mov(go, 1)
                g.reg_load(rr, RV[0:1, 0:1])
                with g.While(go):
                    g.reg_add(cnt, cnt, 16)
                    g.indirect_dma_start(
                        out=FLG[:, :], out_offset=None,
                        in_=flgsh[:, :],
                        in_offset=IndirectOffsetOnAxis(SID[:, 0:1], 0)
                        ).then_inc(PSEM.h, 16)
                    g.wait_ge(PSEM.h, cnt)
                    g.reg_load(val, FLG[0:1, 0:1])
                    g.reg_alu(go, val, rr, ALU.not_equal)
            BARC.inc(g.nop())

        alC = Alloc(nc)
        emit_C_part(nc, block, alC, alP, drC,
                    ("shared", ysh, BARC, JN, yidxr),
                    prezero=(GDZ, 17 + 32 * 16))
        alC.close()
    alP.close()
    return nc


_BC_RUN_COUNTER = [int(time.time() * 1000.0) & 0x3FFFFFFF]


def prep_BC_inputs(inp, x1_full, xn1_bf, scores, core, runid):
    hg = core % 2
    w = prep_B_inputs(inp, x1_full, core)
    w.update(prep_C_inputs(inp, None, xn1_bf, scores, core))
    p = np.arange(128)
    yidxw = np.zeros((128, 16), np.int32)
    yidxr = np.zeros((128, 16), np.int32)
    for j4 in range(4):
        for tb in range(4):
            yidxw[:, 4 * j4 + tb] = ((4 * hg + j4) * 128 + p) * 4 + tb
    for kc in range(NCH):
        for h in range(2):
            yidxr[:, 2 * kc + h] = (kc * 128 + p) * 4 + 2 * hg + h
    w["yidxw"] = yidxw
    w["yidxr"] = yidxr
    w["fidx"] = (hg * 128 + p).reshape(128, 1).astype(np.int32)
    w["sidx"] = ((1 - hg) * 128 + p).reshape(128, 1).astype(np.int32)
    w["fval"] = np.full((128, 4), runid, np.int32)
    w["rid"] = np.full((1, 4), runid, np.int32)
    return w


def run_launch_BC(inp, x1_full, xn1_bf, scores):
    _BC_RUN_COUNTER[0] += 1
    runid = _BC_RUN_COUNTER[0]
    nc = build_launch_BC()
    in_maps = [prep_BC_inputs(inp, x1_full, xn1_bf, scores, c, runid)
               for c in range(N_CORES)]
    res = run_bass_kernel_spmd(nc, in_maps, core_ids=list(range(N_CORES)))
    s1 = np.empty((16384, C), np.float32)
    co_list = []
    for c in range(N_CORES):
        r = res.results[c]
        s1[P * c:P * (c + 1)] = r["s1_out"].reshape(C, P).T
        co_list.append(r["co_out"])
    return s1, co_list


def prep_C_inputs(inp, y_full, xn1_bf, scores, core):
    e = core
    w = {}
    if y_full is not None:
        w["y_in"] = np.ascontiguousarray(
            y_full[P * e:P * (e + 1)].T.reshape(NCH, 128, P).astype(np.float32))
    lngb = np.zeros((128, 2 * NCH), np.float32)
    lngb[:, 0:NCH] = np.asarray(inp["L1_attn_ln_g"],
                                np.float32).reshape(NCH, 128).T
    lngb[:, NCH:2 * NCH] = np.asarray(inp["L1_attn_ln_b"],
                                      np.float32).reshape(NCH, 128).T
    w["lngb"] = lngb
    w["wo"] = np.ascontiguousarray(np.asarray(inp["L1_attn_Wo"], np.float32))
    w["scrow"] = np.ascontiguousarray(
        scores[:, e].reshape(1, 16384).astype(np.float32))
    w["xn1f"] = np.ascontiguousarray(xn1_bf)
    w["w1e"] = np.ascontiguousarray(np.asarray(inp["L1_ffn_w1"], np.float32)[e])
    w["b1e"] = np.ascontiguousarray(
        np.asarray(inp["L1_ffn_b1"], np.float32)[e].reshape(32, 128).T)
    w["w2e"] = np.ascontiguousarray(np.asarray(inp["L1_ffn_w2"], np.float32)[e])
    w["b2e"] = np.ascontiguousarray(
        np.asarray(inp["L1_ffn_b2"], np.float32)[e].reshape(1, C))
    w["triu"] = np.ascontiguousarray(np.triu(np.ones((128, 128), np.float32), 1))
    return w


def run_launch_C(inp, y_full, xn1_bf, scores):
    nc = build_launch_C()
    in_maps = [prep_C_inputs(inp, y_full, xn1_bf, scores, c)
               for c in range(N_CORES)]
    res = run_bass_kernel_spmd(nc, in_maps, core_ids=list(range(N_CORES)))
    s1 = np.empty((16384, C), np.float32)
    co_list = []
    for c in range(N_CORES):
        r = res.results[c]
        s1[P * c:P * (c + 1)] = r["s1_out"].reshape(C, P).T
        co_list.append(r["co_out"])
    return s1, co_list



def build_launch_D():
    nc = bass.Bass("TRN2", target_bir_lowering=False)

    x1d = nc.dram_tensor("x1d", [P, C], F32, kind="ExternalInput").ap()
    s1d = nc.dram_tensor("s1d", [P, C], F32, kind="ExternalInput").ap()
    cod = nc.dram_tensor("cod", [8, P, C], BF16, kind="ExternalInput").ap()
    outd = nc.dram_tensor("outd", [P, C], F32, kind="ExternalOutput").ap()

    al = Alloc(nc)
    sb, sem = al.sb, al.sem
    CO = [sb(f"CO{sl}", [128, 8 * C], BF16) for sl in range(2)]
    X1T = [sb(f"X1T{sl}", [128, C], F32) for sl in range(2)]
    S1T = [sb(f"S1T{sl}", [128, C], F32) for sl in range(2)]
    ACC = [sb(f"ACC{sl}", [128, C], F32) for sl in range(2)]
    GT = [sb(f"GT{sl}", [128, C], F32) for sl in range(2)]
    OUT = [sb(f"OUT{sl}", [128, C], F32) for sl in range(2)]
    LX = sem("LXd")
    EV = sem("EVd")
    EA = sem("EAd")
    ST = sem("STd")

    with nc.Block() as block:
        for t in range(16):
            r0 = 128 * t
            sl = t % 2

            @block.sync
            def _(sy, t=t, r0=r0, sl=sl):
                sy.wait_ge(EV.h, max(0, 2 * (t - 1)))
                for e in range(8):
                    LX.inc(sy.dma_start(out=CO[sl][:, C * e:C * (e + 1)],
                                        in_=cod[e, r0:r0 + 128, :]), 16)
                LX.inc(sy.dma_start(out=X1T[sl][:, :],
                                    in_=x1d[r0:r0 + 128, :]), 16)
                LX.inc(sy.dma_start(out=S1T[sl][:, :],
                                    in_=s1d[r0:r0 + 128, :]), 16)

            @block.vector
            def _(v, t=t, sl=sl):
                LX.wait(v)
                v.tensor_add(ACC[sl][:, :], CO[sl][:, 0:C], CO[sl][:, C:2 * C])
                for e in range(2, 8):
                    ins = v.tensor_add(ACC[sl][:, :], ACC[sl][:, :],
                                       CO[sl][:, C * e:C * (e + 1)])
                EV.inc(ins)

            @block.scalar
            def _(s, t=t, sl=sl):
                EV.wait(s)
                EA.inc(s.activation(GT[sl][:, :], ACC[sl][:, :], AF.Silu))

            @block.vector
            def _(v, t=t, sl=sl):
                EA.wait(v)
                v.wait_ge(ST.h, 16 * max(0, t - 1))
                v.tensor_mul(OUT[sl][:, :], S1T[sl][:, :], GT[sl][:, :])
                EV.inc(v.tensor_add(OUT[sl][:, :], OUT[sl][:, :],
                                    X1T[sl][:, :]))

            @block.sync
            def _(sy, t=t, r0=r0, sl=sl):
                EV.wait(sy)
                ST.inc(sy.dma_start(out=outd[r0:r0 + 128, :],
                                    in_=OUT[sl][:, :]), 16)

        @block.sync
        def _(sy):
            ST.wait(sy)

    al.close()
    return nc


def run_launch_D(x1_full, s1_full, co_list):
    nc = build_launch_D()
    in_maps = []
    for d in range(N_CORES):
        sl = slice(P * d, P * (d + 1))
        in_maps.append({
            "x1d": np.ascontiguousarray(x1_full[sl]),
            "s1d": np.ascontiguousarray(s1_full[sl]),
            "cod": np.ascontiguousarray(
                np.stack([co[sl] for co in co_list], axis=0)),
        })
    res = run_bass_kernel_spmd(nc, in_maps, core_ids=list(range(N_CORES)))
    out = np.empty((16384, C), np.float32)
    for d in range(N_CORES):
        out[P * d:P * (d + 1)] = res.results[d]["outd"]
    return out


def kernel(**inputs):
    x1, xn1, scores = run_launch_A(inputs)
    s1, co_list = run_launch_BC(inputs, x1, xn1, scores)
    out = run_launch_D(x1, s1, co_list)
    return out.reshape(4, 4096, C)



# version 10
# speedup vs baseline: 1.2931x; 1.0478x over previous
import os
import time
import numpy as np
import ml_dtypes
import concourse.bass as bass
import concourse.mybir as mybir
from concourse.ap import AP
from concourse.bass_utils import run_bass_kernel_spmd
from concourse.bass_core_rust import IndirectOffsetOnAxis

F32 = mybir.dt.float32
BF16 = mybir.dt.bfloat16
I32 = mybir.dt.int32
U32 = mybir.dt.uint32
AF = mybir.ActivationFunctionType
ALU = mybir.AluOpType
AX = mybir.AxisListType

BF = ml_dtypes.bfloat16

N_CORES = 8
P = 2048
C = 1024
NCH = C // 128
HT = 1024
LNEPS = 1e-5
APH = int(os.environ.get("APH", "99"))
SUB6 = int(os.environ.get("SUB6", "9"))


def rep_mid(ap2d, n):
    b = ap2d.ap
    return AP(ap2d.tensor, ap2d.offset, [list(b[0]), [0, n], list(b[1])])


def bcast_inner(ap2d, n):
    b = ap2d.ap
    return AP(ap2d.tensor, ap2d.offset, [list(b[0]), list(b[1]), [0, n]])


class Ctr:
    def __init__(self, h):
        self.h = h
        self.n = 0

    def inc(self, inst, k=1):
        inst.then_inc(self.h, k)
        self.n += k
        return inst

    def wait(self, eng, n=None):
        eng.wait_ge(self.h, self.n if n is None else n)


class Alloc:
    def __init__(self, nc):
        self.nc = nc
        self.ctx = []

    def sb(self, name, shape, dt):
        t = self.nc.sbuf_tensor(name, shape, dt)
        self.ctx.append(t)
        return t.__enter__()

    def ps(self, name, shape, dt):
        t = self.nc.psum_tensor(name, shape, dt)
        self.ctx.append(t)
        return t.__enter__()

    def sem(self, name):
        t = self.nc.semaphore(name)
        self.ctx.append(t)
        return Ctr(t.__enter__())

    def close(self):
        for t in reversed(self.ctx):
            t.__exit__(None, None, None)



def build_launch_A():
    nc = bass.Bass("TRN2", target_bir_lowering=False)

    x_in = nc.dram_tensor("x_in", [P, C], F32, kind="ExternalInput").ap()
    wqkv = nc.dram_tensor("wqkv", [C, 3 * C], F32, kind="ExternalInput").ap()
    wproj = nc.dram_tensor("wproj", [C, C], F32, kind="ExternalInput").ap()
    bproj = nc.dram_tensor("bproj", [1, C], F32, kind="ExternalInput").ap()
    r0w = nc.dram_tensor("r0w", [C, 16], F32, kind="ExternalInput").ap()
    wgs = nc.dram_tensor("wgs", [C, 1024], F32, kind="ExternalInput").ap()
    wus = nc.dram_tensor("wus", [C, 1024], F32, kind="ExternalInput").ap()
    wds = nc.dram_tensor("wds", [1024, C], F32, kind="ExternalInput").ap()
    wgr = nc.dram_tensor("wgr", [C, 4096], F32, kind="ExternalInput").ap()
    wur = nc.dram_tensor("wur", [C, 4096], F32, kind="ExternalInput").ap()
    wdr = nc.dram_tensor("wdr", [4096, C], F32, kind="ExternalInput").ap()
    l1rw = nc.dram_tensor("l1rw", [C, 8], F32, kind="ExternalInput").ap()
    l1rb = nc.dram_tensor("l1rb", [1, 8], F32, kind="ExternalInput").ap()
    trilm = nc.dram_tensor("trilm", [128, 128], F32, kind="ExternalInput").ap()

    x1_out = nc.dram_tensor("x1_out", [NCH, 128, P], F32, kind="ExternalOutput").ap()
    xn1_out = nc.dram_tensor("xn1_out", [NCH, 128, P], BF16,
                             kind="ExternalOutput").ap()
    sc_out = nc.dram_tensor("sc_out", [P, 8], F32, kind="ExternalOutput").ap()

    comb_dram = nc.dram_tensor("comb_dram", [16, HT], F32).ap()

    al = Alloc(nc)
    sb, ps, sem = al.sb, al.ps, al.sem

    XF = sb("XF", [128, NCH * HT], F32)
    XN = sb("XN", [128, NCH * HT], BF16)
    S0 = sb("S0", [128, NCH * HT], F32)
    G0 = sb("G0", [128, NCH * HT], F32)
    QKT = sb("QKT", [128, 16 * 512], BF16)
    AFM = sb("AFM", [128, NCH * 512], BF16)
    VT = sb("VT", [128, 4 * 1024], BF16)
    WS = sb("WS", [128, 8192], BF16)
    SC1 = sb("SC1", [128, 2048], F32)
    SC2 = sb("SC2", [128, 1024], F32)
    SCB = sb("SCB", [128, 1024], BF16)
    SCB2 = sb("SCB2", [128, 512], BF16)
    SCM = sb("SCM", [128, 1024], F32)
    ROWS = sb("ROWS", [1, 3 * 1024], F32)
    ROW0 = sb("ROW0", [1, 1024], F32)
    ONES1 = sb("ONES1", [1, 128], F32)
    ONESC = sb("ONESC", [128, 1], F32)
    IDF = sb("IDF", [128, 128], F32)
    TRIL = sb("TRIL", [128, 128], F32)
    BPROJ = sb("BPROJ", [128, NCH], F32)
    SMAX = sb("SMAX", [128, 16], F32)
    CT = sb("CT", [16, 1024], F32)
    CMB = sb("CMB", [128, 1024], BF16)
    RT = sb("RT", [128, 8 * 16], F32)
    RS = sb("RS", [128, 64], F32)
    ZR16 = sb("ZR16", [128, 16], F32)
    CONSTS = sb("CONSTS", [1, 4], F32)

    PS_ = [ps(f"PSU{i}", [128, 512], F32) for i in range(8)]

    LW = sem("LW")
    LX = sem("LX")
    MM = sem("MM")
    EV = sem("EV")
    EA = sem("EA")
    ST = sem("ST")
    MS = sem("MS"); VA = sem("VA"); VB = sem("VB")
    MT = sem("MT"); VC = sem("VC"); MO = sem("MO"); VD = sem("VD")
    EB = sem("EB"); VM = sem("VM"); EX = sem("EX")
    VP = sem("VP"); EC = sem("EC")
    VE = sem("VE")
    LP = [sem(f"LP{i}") for i in range(7)]

    SUMS = MEAN = MR = ROWS[0:1, 0 * HT:1 * HT]
    SSQ = VAR = ROWS[0:1, 1 * HT:2 * HT]
    RSCR = RSTD = ROWS[0:1, 2 * HT:3 * HT]

    def slab_ap(dram, r0, nr, c0, ncols, row_len):
        assert nr % 128 == 0
        t = dram.tensor
        off = dram.offset + r0 * row_len + c0
        return AP(t, off, [[row_len, 128], [128 * row_len, nr // 128], [1, ncols]])

    def ws_dst(col0, nkc, ncols):
        return AP(WS[:, :].tensor, WS[:, :].offset + col0,
                  [[8192, 128], [ncols, nkc], [1, ncols]])

    with nc.Block() as block:

        @block.sync
        def _(sy):
            LX.inc(sy.dma_start(out=TRIL[:, :], in_=trilm[:, :]), 16)
            LX.inc(sy.dma_start(out=ROW0[0:1, :], in_=bproj[0:1, :]), 16)

        nc.const_aps.register(F32, LNEPS, CONSTS[0:1, 0:1])

        @block.gpsimd
        def _(g):
            g.memset(ONES1[:, :], 1.0)
            g.memset(ONESC[:, :], 1.0)
            g.memset(ZR16[:, :], 0.0)
            g.memset(CONSTS[0:1, 0:1], LNEPS)
            LW.inc(g.memset(IDF[:, :], 1.0))
            g.wait_ge(LW.h, 1)
            LW.inc(g.affine_select(IDF[:, :], IDF[:, :], pattern=[[1, 128]],
                                   compare_op=ALU.is_equal, fill=0.0,
                                   channel_multiplier=-1))

        @block.tensor
        def _(t):
            t.wait_ge(LW.h, 2)
            t.wait_ge(LX.h, 32)
            for k in range(NCH):
                ins = t.transpose(PS_[6][:, k:k + 1],
                                  ROW0[0:1, 128 * k:128 * (k + 1)], IDF[0:1, 0:1])
            MM.inc(ins)

        @block.vector
        def _(v):
            MM.wait(v)
            EV.inc(v.tensor_copy(BPROJ[:, :], PS_[6][:, 0:NCH]))

        def edge(eng, inst):
            VE.inc(inst)
            VE.wait(eng)

        def tensor_barrier(t):
            EV.wait(t)
            EA.wait(t)
            VA.wait(t)
            EB.wait(t)
            VM.wait(t)
            EX.wait(t)
            VP.wait(t)
            EC.wait(t)
            VD.wait(t)

        def emit_half(h):
            t0 = h * HT

            ev0, mm0, lx0 = EV.n, MM.n, LX.n
            for j in range(8):
                slot = j % 2

                @block.sync
                def _(sy, j=j, slot=slot):
                    if j < 2:
                        EA.wait(sy)
                        EV.wait(sy)
                        VP.wait(sy)
                        MT.wait(sy)
                        EX.wait(sy)
                    else:
                        sy.wait_ge(EV.h, ev0 + j - 1)
                    LP[j % 2].inc(sy.dma_start(
                        out=SC1[:, 1024 * slot:1024 * (slot + 1)],
                        in_=x_in[t0 + 128 * j: t0 + 128 * (j + 1), :]), 16)

                @block.tensor
                def _(t, j=j, slot=slot):
                    LP[j % 2].wait(t)
                    t.wait_ge(LW.h, 2)
                    if j == 0:
                        tensor_barrier(t)
                    if j >= 2:
                        t.wait_ge(EV.h, ev0 + j - 1)
                    pb = (j % 2) * 2
                    for k in range(NCH):
                        ins = t.transpose(
                            PS_[pb + k // 4][:, 128 * (k % 4):128 * (k % 4 + 1)],
                            SC1[:, 1024 * slot + 128 * k:1024 * slot + 128 * (k + 1)],
                            IDF[:, :])
                    MM.inc(ins)

                @block.vector
                def _(v, j=j):
                    v.wait_ge(MM.h, mm0 + j + 1)
                    pb = (j % 2) * 2
                    for k in range(NCH):
                        ins = v.tensor_copy(
                            XF[:, HT * k + 128 * j: HT * k + 128 * (j + 1)],
                            PS_[pb + k // 4][:, 128 * (k % 4):128 * (k % 4 + 1)])
                    EV.inc(ins)

            def emit_ln_stats(src, dst_xn, out_dram=None, scores=False):
                for tt in range(2):
                    for grp in range(2):
                        @block.scalar
                        def _(s, tt=tt, grp=grp):
                            EV.wait(s)
                            if grp == 1 or tt == 1:
                                MM.wait(s)
                            for kk in range(4):
                                k = grp * 4 + kk
                                ins = s.activation(
                                    SC1[:, 512 * kk:512 * (kk + 1)],
                                    src[:, HT * k + 512 * tt:HT * k + 512 * tt + 512],
                                    AF.Square)
                            EA.inc(ins)

                        @block.tensor
                        def _(t, tt=tt, grp=grp):
                            EA.wait(t)
                            if tt + grp == 0:
                                EV.wait(t)
                            for kk in range(4):
                                k = grp * 4 + kk
                                t.matmul(PS_[6][0:1, 0:512], ONESC[:, 0:1],
                                         src[:, HT * k + 512 * tt:
                                             HT * k + 512 * tt + 512],
                                         start=(k == 0), stop=(k == 7))
                                ins = t.matmul(PS_[7][0:1, 0:512], ONESC[:, 0:1],
                                               SC1[:, 512 * kk:512 * (kk + 1)],
                                               start=(k == 0), stop=(k == 7))
                            MM.inc(ins)

                    @block.vector
                    def _(v, tt=tt):
                        MM.wait(v)
                        v.tensor_copy(SUMS[0:1, 512 * tt:512 * (tt + 1)],
                                      PS_[6][0:1, 0:512])
                        EV.inc(v.tensor_copy(SSQ[0:1, 512 * tt:512 * (tt + 1)],
                                             PS_[7][0:1, 0:512]))

                @block.vector
                def _(v):
                    EV.wait(v)
                    edge(v, v.tensor_scalar_mul(MEAN, SUMS, 1.0 / C))
                    edge(v, v.tensor_scalar_mul(VAR, SSQ, 1.0 / C))
                    edge(v, v.tensor_mul(RSCR, MEAN, MEAN))
                    EV.inc(v.tensor_sub(VAR, VAR, RSCR))

                @block.scalar
                def _(s):
                    EV.wait(s)
                    s.wait_ge(LW.h, 2)
                    edge(s, s.activation(RSCR, VAR, AF.Ln, bias=LNEPS))
                    EA.inc(s.activation(RSTD, RSCR, AF.Exp, scale=-0.5))

                @block.vector
                def _(v):
                    EA.wait(v)
                    EV.inc(v.tensor_mul(MR, MEAN, RSTD))

                for tt in range(2):
                    @block.tensor
                    def _(t, tt=tt):
                        EV.wait(t)
                        t.matmul(PS_[6][:, 0:512], ONES1[0:1, :],
                                 RSTD[0:1, 512 * tt:512 * (tt + 1)],
                                 start=True, stop=True)
                        MM.inc(t.matmul(PS_[7][:, 0:512], ONES1[0:1, :],
                                        MR[0:1, 512 * tt:512 * (tt + 1)],
                                        start=True, stop=True))

                    @block.vector
                    def _(v, tt=tt):
                        MM.wait(v)
                        for k in range(NCH):
                            sl = slice(HT * k + 512 * tt, HT * k + 512 * tt + 512)
                            v.tensor_mul(SC2[:, 0:512], src[:, sl], PS_[6][:, 0:512])
                            ins = v.tensor_sub(dst_xn[:, sl], SC2[:, 0:512],
                                               PS_[7][:, 0:512])
                        EV.inc(ins)

                if out_dram is not None:
                    @block.sync
                    def _(sy):
                        EV.wait(sy)
                        for k in range(NCH):
                            ST.inc(sy.dma_start(
                                out=out_dram[k, :, t0:t0 + HT],
                                in_=dst_xn[:, HT * k:HT * (k + 1)]), 16)

            def emit_debug_dump():
                @block.sync
                def _(sy):
                    EV.wait(sy)
                    EA.wait(sy)
                    VD.wait(sy)
                    for k in range(NCH):
                        ST.inc(sy.dma_start(out=x1_out[k, :, t0:t0 + HT],
                                            in_=XF[:, HT * k:HT * (k + 1)]), 16)
                    for k in range(NCH):
                        ST.inc(sy.dma_start(out=xn1_out[k, :, t0:t0 + HT],
                                            in_=XN[:, HT * k:HT * (k + 1)]), 16)

            if APH <= 1:
                emit_debug_dump()
                return
            emit_ln_stats(XF, XN)
            if APH <= 2:
                emit_debug_dump()
                return

            def wload(g, ctr, dram_block, ws_col0, nkc, ncols):
                ins = g.dma_start(out=ws_dst(ws_col0, nkc, ncols),
                                  in_=dram_block)
                ctr.inc(ins, 16)
                return ins

            attn_i = {"i": 0}
            for tt in range(2 if APH >= 4 else 0):
                for nh in range(2):
                    @block.gpsimd
                    def _(g, tt=tt, nh=nh):
                        MM.wait(g)
                        wload(g, LP[4],
                              slab_ap(wqkv, 0, C, 2048 + 512 * nh, 512, 3 * C),
                              2048, NCH, 512)

                    for jb in range(4):
                        j = 4 * tt + jb

                        @block.tensor
                        def _(t, nh=nh, j=j, jb=jb):
                            LP[4].wait(t)
                            if nh == 0 and jb == 0:
                                tensor_barrier(t)
                            if nh * 4 + jb >= 2:
                                EV.wait(t, EV.n - 1)
                            bank = jb % 2
                            for kc in range(NCH):
                                ins = t.matmul(
                                    PS_[bank][:, 0:512],
                                    XN[:, HT * kc + 128 * j:HT * kc + 128 * (j + 1)],
                                    WS[:, 2048 + 512 * kc:2048 + 512 * (kc + 1)],
                                    start=(kc == 0), stop=(kc == NCH - 1))
                            MM.inc(ins)

                        @block.vector
                        def _(v, nh=nh, jb=jb):
                            MM.wait(v)
                            bank = jb % 2
                            EV.inc(v.tensor_copy(
                                VT[:, 1024 * jb + 512 * nh:1024 * jb + 512 * (nh + 1)],
                                PS_[bank][:, 0:512]))

                for s16 in range(16 if APH >= 5 else 0):
                    wb = 1024 * (s16 % 2)

                    @block.gpsimd
                    def _(g, s16=s16, wb=wb, tt=tt):
                        MM.wait(g, MM.n - 1)
                        wload(g, LP[2 + s16 % 2],
                              slab_ap(wqkv, 0, C, 128 * s16, 128, 3 * C),
                              wb, NCH, 128)

                    @block.tensor
                    def _(t, s16=s16, wb=wb, tt=tt):
                        LP[2 + s16 % 2].wait(t)
                        if s16 == 0:
                            tensor_barrier(t)
                        if s16 >= 2:
                            EV.wait(t, EV.n - 1)
                        bank = s16 % 2
                        for kc in range(NCH):
                            ins = t.matmul(
                                PS_[bank][:, 0:512],
                                WS[:, wb + 128 * kc:wb + 128 * (kc + 1)],
                                XN[:, HT * kc + 512 * tt:HT * kc + 512 * tt + 512],
                                start=(kc == 0), stop=(kc == NCH - 1))
                        MM.inc(ins)

                    @block.vector
                    def _(v, s16=s16):
                        MM.wait(v)
                        bank = s16 % 2
                        EV.inc(v.tensor_copy(QKT[:, 512 * s16:512 * (s16 + 1)],
                                             PS_[bank][:, 0:512]))

                if APH in (4, 5) and tt == 1:
                    @block.sync
                    def _(sy):
                        EV.wait(sy)
                        if APH == 4:
                            for k in range(NCH):
                                ST.inc(sy.dma_start(
                                    out=xn1_out[k, :, t0:t0 + 512],
                                    in_=VT[:, 512 * k:512 * (k + 1)]), 16)
                        else:
                            for k in range(NCH):
                                ST.inc(sy.dma_start(
                                    out=xn1_out[k, :, t0:t0 + 1024],
                                    in_=QKT[:, 1024 * k:1024 * (k + 1)]), 16)

                @block.gpsimd
                def _(g, tt=tt):
                    MM.wait(g)
                    wload(g, LP[5], slab_ap(wproj, 0, C, 0, C, C), 0, NCH, C)
                pw0 = LP[5].n

                if tt == 0:
                    attn_i["base"] = dict(
                        ms=MS.n, va=VA.n, eb=EB.n, vm=VM.n, ex=EX.n,
                        vp=VP.n, mt=MT.n, ec=EC.n, mo=MO.n, vd=VD.n)

                for bj in range(4 if APH >= 6 else 0):
                    for hg in range(4):
                        i = attn_i["i"]
                        attn_i["i"] += 1
                        b0 = attn_i["base"]
                        ptb = PS_[4 + i % 2]
                        so = 512 * (i % 2)
                        sm = 8 * (i % 2)

                        @block.tensor
                        def _(t, bj=bj, hg=hg, i=i):
                            if i % 16 == 0:
                                EV.wait(t)
                            if i >= 1:
                                t.wait_ge(VA.h, b0["va"] + i)
                                t.wait_ge(EB.h, b0["eb"] + i)
                            for u in range(4):
                                hh = 4 * hg + u
                                qsl = QKT[64 * (hh % 2):64 * (hh % 2) + 64,
                                          512 * (hh // 2) + 128 * bj:
                                          512 * (hh // 2) + 128 * (bj + 1)]
                                ksl = QKT[64 * (hh % 2):64 * (hh % 2) + 64,
                                          512 * (8 + hh // 2) + 128 * bj:
                                          512 * (8 + hh // 2) + 128 * (bj + 1)]
                                ins = t.matmul(PS_[u][:, 0:128],
                                               qsl, ksl, start=True, stop=True,
                                               tile_position=(64 * (hh % 2), 0))
                            MS.inc(ins)

                        @block.vector
                        def _(v, i=i, so=so):
                            v.wait_ge(MS.h, b0["ms"] + i + 1)
                            if i >= 2:
                                v.wait_ge(VM.h, b0["vm"] + i - 1)
                            v.tensor_copy(SCM[:, so:so + 128], PS_[0][:, 0:128])
                            VA.inc(v.tensor_copy(SCM[:, so + 128:so + 256],
                                                 PS_[1][:, 0:128]))

                        @block.scalar
                        def _(s, i=i, so=so):
                            s.wait_ge(MS.h, b0["ms"] + i + 1)
                            if i >= 2:
                                s.wait_ge(VM.h, b0["vm"] + i - 1)
                            s.copy(SCM[:, so + 256:so + 384], PS_[2][:, 0:128])
                            EB.inc(s.copy(SCM[:, so + 384:so + 512],
                                          PS_[3][:, 0:128]))

                        @block.vector
                        def _(v, i=i, so=so, sm=sm):
                            v.wait_ge(VA.h, b0["va"] + i + 1)
                            v.wait_ge(EB.h, b0["eb"] + i + 1)
                            if i >= 2:
                                v.wait_ge(EX.h, b0["ex"] + i - 1)
                            v.scalar_tensor_tensor(
                                SC2[:, so:so + 512].rearrange("p (g f) -> p g f",
                                                              f=128),
                                SCM[:, so:so + 512].rearrange("p (g f) -> p g f",
                                                              f=128),
                                0.125, rep_mid(TRIL[:, :], 4),
                                ALU.mult, ALU.add)
                            ins = v.reduce_max(
                                SMAX[:, sm:sm + 4],
                                SC2[:, so:so + 512].rearrange("p (g f) -> p g f",
                                                              f=128),
                                axis=AX.X, negate=True)
                            VM.inc(ins)

                        @block.scalar
                        def _(s, i=i, so=so, sm=sm):
                            s.wait_ge(VM.h, b0["vm"] + i + 1)
                            if i >= 2:
                                s.wait_ge(MT.h, b0["mt"] + i - 1)
                            for u in range(4):
                                ins = s.activation(
                                    SC1[:, so + 128 * u:so + 128 * (u + 1)],
                                    SC2[:, so + 128 * u:so + 128 * (u + 1)],
                                    AF.Exp, bias=SMAX[:, sm + u:sm + u + 1],
                                    accum_out=SMAX[:, sm + 4 + u:sm + 5 + u])
                            EX.inc(ins)

                        @block.vector
                        def _(v, i=i, so=so, sm=sm):
                            v.wait_ge(EX.h, b0["ex"] + i + 1)
                            edge(v, v.tensor_scalar_add(SMAX[:, sm + 4:sm + 8],
                                                        SMAX[:, sm + 4:sm + 8],
                                                        1e-6))
                            edge(v, v.reciprocal(SMAX[:, sm + 4:sm + 8],
                                                 SMAX[:, sm + 4:sm + 8]))
                            ins = v.tensor_mul(
                                SC1[:, so:so + 512].rearrange("p (g f) -> p g f",
                                                              f=128),
                                SC1[:, so:so + 512].rearrange("p (g f) -> p g f",
                                                              f=128),
                                bcast_inner(SMAX[:, sm + 4:sm + 8], 128))
                            VP.inc(ins)

                        @block.tensor
                        def _(t, i=i, ptb=ptb, so=so):
                            t.wait_ge(VP.h, b0["vp"] + i + 1)
                            if i >= 2:
                                t.wait_ge(EC.h, b0["ec"] + i - 1)
                            for u in range(4):
                                ins = t.transpose(ptb[:, 128 * u:128 * (u + 1)],
                                                  SC1[:, so + 128 * u:
                                                      so + 128 * (u + 1)],
                                                  IDF[:, :])
                            MT.inc(ins)

                        @block.scalar
                        def _(s, i=i, ptb=ptb):
                            s.wait_ge(MT.h, b0["mt"] + i + 1)
                            if i >= 2:
                                s.wait_ge(MO.h, b0["mo"] + 2 * i - 2)
                            bo = 512 * (i % 2)
                            ins = s.copy(SCB[:, bo:bo + 512], ptb[:, 0:512])
                            EC.inc(ins)

                        for pair in range(2):
                            @block.tensor
                            def _(t, bj=bj, hg=hg, i=i, pair=pair):
                                if pair == 0:
                                    t.wait_ge(EC.h, b0["ec"] + i + 1)
                                    if i >= 1:
                                        t.wait_ge(VD.h, b0["vd"] + 2 * i)
                                else:
                                    t.wait_ge(VD.h, b0["vd"] + 2 * i + 1)
                                bo = 512 * (i % 2)
                                for w in range(2):
                                    u = 2 * pair + w
                                    hh = 4 * hg + u
                                    vsl = VT[:, 1024 * bj + 64 * hh:
                                             1024 * bj + 64 * (hh + 1)]
                                    ins = t.matmul(
                                        PS_[6 + w][64 * w:64 * w + 64, 0:128],
                                        vsl,
                                        SCB[:, bo + 128 * u:bo + 128 * (u + 1)],
                                        start=True, stop=True,
                                        tile_position=(0, 64 * w))
                                MO.inc(ins)

                            @block.vector
                            def _(v, bj=bj, hg=hg, i=i, pair=pair):
                                v.wait_ge(MO.h, b0["mo"] + 2 * i + pair + 1)
                                ck = 2 * hg + pair
                                col = 512 * ck + 128 * bj
                                v.tensor_copy(AFM[0:64, col:col + 128],
                                              PS_[6][0:64, 0:128])
                                VD.inc(v.tensor_copy(AFM[64:128, col:col + 128],
                                                     PS_[7][64:128, 0:128]))

                for ds in range(NCH if APH >= 7 else 0):
                    @block.tensor
                    def _(t, ds=ds, tt=tt):
                        t.wait_ge(LP[5].h, pw0)
                        if ds == 0:
                            VD.wait(t)
                            VA.wait(t)
                            EB.wait(t)
                        if ds >= 2:
                            EV.wait(t, EV.n - 1)
                        bank = ds % 2
                        for kc in range(NCH):
                            ins = t.matmul(
                                PS_[bank][:, 0:512],
                                WS[:, 1024 * kc + 128 * ds:
                                   1024 * kc + 128 * (ds + 1)],
                                AFM[:, 512 * kc:512 * (kc + 1)],
                                start=(kc == 0), stop=(kc == NCH - 1))
                        MM.inc(ins)

                    @block.vector
                    def _(v, ds=ds, tt=tt):
                        MM.wait(v)
                        bank = ds % 2
                        EV.inc(v.tensor_scalar_add(
                            S0[:, HT * ds + 512 * tt:HT * ds + 512 * (tt + 1)],
                            PS_[bank][:, 0:512], BPROJ[:, ds:ds + 1]))

            if APH == 6:
                @block.sync
                def _(sy):
                    VD.wait(sy)
                    EC.wait(sy)
                    ST.inc(sy.dma_start(out=x1_out[0, :, 0:512],
                                        in_=SCM[:, 512:1024]), 16)
                    ST.inc(sy.dma_start(out=x1_out[1, :, 0:512],
                                        in_=SC2[:, 512:1024]), 16)
                    ST.inc(sy.dma_start(out=x1_out[2, :, 0:512],
                                        in_=SC1[:, 512:1024]), 16)
                    ST.inc(sy.dma_start(out=x1_out[3, :, 0:16],
                                        in_=SMAX[:, 0:16]), 16)
                    ST.inc(sy.dma_start(out=xn1_out[0, :, 0:512],
                                        in_=SCB[:, 512:1024]), 16)

            if APH in (6, 7):
                @block.sync
                def _(sy):
                    EV.wait(sy)
                    EA.wait(sy)
                    VD.wait(sy)
                    for k in range(NCH):
                        ST.inc(sy.dma_start(out=x1_out[k, :, t0:t0 + HT],
                                            in_=S0[:, HT * k:HT * (k + 1)]), 16)
                    for k in range(NCH):
                        ST.inc(sy.dma_start(out=xn1_out[k, :, t0:t0 + 512],
                                            in_=AFM[:, 512 * k:512 * (k + 1)]), 16)
            if APH <= 7:
                return
            @block.gpsimd
            def _(g):
                MM.wait(g)
                wload(g, LP[5], slab_ap(r0w, 0, C, 0, 16, 16), 8064, NCH, 16)
            rw0 = LP[5].n

            for j in range(8):
                @block.tensor
                def _(t, j=j):
                    t.wait_ge(LP[5].h, rw0)
                    if j == 0:
                        EV.wait(t)
                    if j >= 2:
                        EV.wait(t, EV.n - 1)
                    bank = 6 + j % 2
                    for kc in range(NCH):
                        ins = t.matmul(
                            PS_[bank][:, 0:16],
                            XN[:, HT * kc + 128 * j:HT * kc + 128 * (j + 1)],
                            WS[:, 8064 + 16 * kc:8064 + 16 * (kc + 1)],
                            start=(kc == 0), stop=(kc == NCH - 1))
                    MM.inc(ins)

                @block.vector
                def _(v, j=j):
                    MM.wait(v)
                    bank = 6 + j % 2
                    EV.inc(v.tensor_copy(RT[:, 16 * j:16 * (j + 1)],
                                         PS_[bank][:, 0:16]))

            R3 = RT[:, :].rearrange("p (j e) -> p j e", e=16)

            @block.scalar
            def _(s):
                EV.wait(s)
                EA.inc(s.activation(RT[:, :], RT[:, :], AF.Exp))

            @block.vector
            def _(v):
                EA.wait(v)
                edge(v, v.reduce_sum(RS[:, 0:8], R3, axis=AX.X))
                edge(v, v.reciprocal(RS[:, 0:8], RS[:, 0:8]))
                edge(v, v.tensor_mul(R3, R3, bcast_inner(RS[:, 0:8], 16)))
                EV.inc(v.reduce_max(RS[:, 8:16], R3, axis=AX.X))

            for j in range(8):
                @block.vector
                def _(v, j=j):
                    EV.wait(v)
                    sl = slice(16 * j, 16 * (j + 1))
                    r = RT[:, sl]
                    m1c = RS[:, 8 + j:9 + j]
                    edge(v, v.tensor_scalar(SC2[:, 0:16], r, m1c, None,
                                            op0=ALU.is_ge))
                    edge(v, v.tensor_tensor_scan(SC1[:, 0:16], SC2[:, 0:16],
                                                 ZR16[:, 0:16], 0.0,
                                                 ALU.add, ALU.add))
                    edge(v, v.tensor_scalar(SC1[:, 16:32], SC1[:, 0:16], 1.0,
                                            None, op0=ALU.is_le))
                    edge(v, v.tensor_mul(SC2[:, 0:16], SC2[:, 0:16],
                                         SC1[:, 16:32]))
                    edge(v, v.scalar_tensor_tensor(SC1[:, 32:48], SC2[:, 0:16],
                                                   -1e30, r, ALU.mult, ALU.add))
                    edge(v, v.reduce_max(RS[:, 24 + j:25 + j], SC1[:, 32:48],
                                         axis=AX.X))
                    m2c = RS[:, 24 + j:25 + j]
                    edge(v, v.tensor_scalar(SC2[:, 16:32], SC1[:, 32:48], m2c,
                                            None, op0=ALU.is_ge))
                    edge(v, v.tensor_tensor_scan(SC1[:, 0:16], SC2[:, 16:32],
                                                 ZR16[:, 0:16], 0.0,
                                                 ALU.add, ALU.add))
                    edge(v, v.tensor_scalar(SC1[:, 16:32], SC1[:, 0:16], 1.0,
                                            None, op0=ALU.is_le))
                    edge(v, v.tensor_mul(SC2[:, 16:32], SC2[:, 16:32],
                                         SC1[:, 16:32]))
                    edge(v, v.tensor_add(RS[:, 32 + j:33 + j], m1c, m2c))
                    edge(v, v.tensor_scalar_add(RS[:, 32 + j:33 + j],
                                                RS[:, 32 + j:33 + j], 1e-9))
                    edge(v, v.reciprocal(RS[:, 32 + j:33 + j],
                                         RS[:, 32 + j:33 + j]))
                    edge(v, v.tensor_mul(RS[:, 40 + j:41 + j], m1c,
                                         RS[:, 32 + j:33 + j]))
                    edge(v, v.tensor_mul(RS[:, 48 + j:49 + j], m2c,
                                         RS[:, 32 + j:33 + j]))
                    edge(v, v.tensor_scalar(SC2[:, 32:48], SC2[:, 0:16],
                                            RS[:, 40 + j:41 + j], None,
                                            op0=ALU.mult))
                    edge(v, v.tensor_scalar(SC2[:, 16:32], SC2[:, 16:32],
                                            RS[:, 48 + j:49 + j], None,
                                            op0=ALU.mult))
                    edge(v, v.tensor_add(SC2[:, 32:48], SC2[:, 32:48],
                                         SC2[:, 16:32]))
                    EV.inc(v.tensor_copy(RT[:, sl], SC2[:, 32:48]))

                @block.tensor
                def _(t, j=j):
                    EV.wait(t)
                    MM.inc(t.transpose(PS_[6 + (j % 2)][0:16, 0:128],
                                       RT[:, 16 * j:16 * (j + 1)], IDF[:, :]))

                @block.vector
                def _(v, j=j):
                    MM.wait(v)
                    EV.inc(v.tensor_copy(CT[0:16, 128 * j:128 * (j + 1)],
                                         PS_[6 + (j % 2)][0:16, 0:128]))

            @block.gpsimd
            def _(g):
                EV.wait(g)
                LP[4].inc(g.dma_start(out=comb_dram[:, :], in_=CT[:, :]), 16)

            moe_prev = {"gu": None, "all": None}

            def moe_block(wg_d, wu_d, wd_d, mid, e_base_col, comb_col, first):
                nms = mid // 128
                wu0 = 8 * mid
                wd0 = 16 * mid

                gu_prev = moe_prev["gu"] if moe_prev["gu"] is not None else MM.n
                all_prev = moe_prev["all"] if moe_prev["all"] is not None else MM.n

                @block.gpsimd
                def _(g):
                    g.wait_ge(MM.h, gu_prev)
                    wload(g, LP[5], wg_d, 0, NCH, mid)
                    wload(g, LP[5], wu_d, wu0, NCH, mid)
                lwgu = LP[5].n

                @block.gpsimd
                def _(g):
                    g.wait_ge(MM.h, all_prev)
                    wload(g, LP[6], wd_d, wd0, nms, C)
                lw0 = LP[6].n

                for ms in range(nms):
                    for tt in range(2):
                        @block.tensor
                        def _(t, ms=ms, tt=tt):
                            t.wait_ge(LP[5].h, lwgu)
                            if ms + tt == 0:
                                EV.wait(t)
                                EA.wait(t)
                            gi = ms * 2 + tt
                            if gi >= 2:
                                EV.wait(t, EV.n - 1)
                            bg = gi % 2
                            bu = 2 + gi % 2
                            for kc in range(NCH):
                                t.matmul(PS_[bg][:, 0:512],
                                         WS[:, mid * kc + 128 * ms:
                                            mid * kc + 128 * (ms + 1)],
                                         XN[:, HT * kc + 512 * tt:
                                            HT * kc + 512 * tt + 512],
                                         start=(kc == 0), stop=(kc == NCH - 1))
                                ins = t.matmul(PS_[bu][:, 0:512],
                                               WS[:, wu0 + mid * kc + 128 * ms:
                                                  wu0 + mid * kc + 128 * (ms + 1)],
                                               XN[:, HT * kc + 512 * tt:
                                                  HT * kc + 512 * tt + 512],
                                               start=(kc == 0), stop=(kc == NCH - 1))
                            MM.inc(ins)

                        @block.scalar
                        def _(s, ms=ms, tt=tt):
                            MM.wait(s)
                            gi = ms * 2 + tt
                            bg = gi % 2
                            EA.inc(s.activation(SCB2[:, 0:512], PS_[bg][:, 0:512],
                                                AF.Silu))

                        @block.vector
                        def _(v, ms=ms, tt=tt):
                            EA.wait(v)
                            gi = ms * 2 + tt
                            bu = 2 + gi % 2
                            dst = AFM[:, 512 * (2 * ms + tt):512 * (2 * ms + tt + 1)]
                            if comb_col is not None:
                                v.tensor_tensor(SCB[:, 0:512], PS_[bu][:, 0:512],
                                                CMB[:, 512 * tt:512 * (tt + 1)],
                                                ALU.mult)
                                ins = v.tensor_mul(dst, SCB2[:, 0:512],
                                                   SCB[:, 0:512])
                            else:
                                v.tensor_copy(SCB[:, 0:512], PS_[bu][:, 0:512])
                                ins = v.tensor_mul(dst, SCB2[:, 0:512],
                                                   SCB[:, 0:512])
                            EV.inc(ins)

                moe_prev["gu"] = MM.n

                for ds in range(NCH):
                    for tt in range(2):
                        @block.tensor
                        def _(t, ds=ds, tt=tt):
                            if ds + tt == 0:
                                t.wait_ge(LP[6].h, lw0)
                                EV.wait(t)
                            gi = ds * 2 + tt
                            if gi >= 2:
                                EV.wait(t, EV.n - 1)
                            bank = 4 + gi % 2
                            for kg in range(nms):
                                ins = t.matmul(
                                    PS_[bank][:, 0:512],
                                    WS[:, wd0 + C * kg + 128 * ds:
                                       wd0 + C * kg + 128 * (ds + 1)],
                                    AFM[:, 512 * (2 * kg + tt):
                                        512 * (2 * kg + tt + 1)],
                                    start=(kg == 0), stop=(kg == nms - 1))
                            MM.inc(ins)

                        @block.vector
                        def _(v, ds=ds, tt=tt):
                            MM.wait(v)
                            gi = ds * 2 + tt
                            bank = 4 + gi % 2
                            dst = G0[:, HT * ds + 512 * tt:HT * ds + 512 * (tt + 1)]
                            if first:
                                ins = v.tensor_copy(dst, PS_[bank][:, 0:512])
                            else:
                                ins = v.tensor_add(dst, dst, PS_[bank][:, 0:512])
                            EV.inc(ins)

                moe_prev["all"] = MM.n

            if APH == 8:
                @block.sync
                def _(sy):
                    EV.wait(sy)
                    EA.wait(sy)
                    LP[4].wait(sy)
                    ST.inc(sy.dma_start(out=x1_out[0, 0:16, t0:t0 + HT],
                                        in_=CT[:, :]), 16)
            if APH <= 8:
                return
            for sh in range(4):
                moe_block(slab_ap(wgs, 0, C, 256 * sh, 256, 1024),
                          slab_ap(wus, 0, C, 256 * sh, 256, 1024),
                          slab_ap(wds, 256 * sh, 256, 0, C, C),
                          256, None, None, first=(sh == 0))

            if APH <= 9:
                return
            for e in range(16):
                @block.gpsimd
                def _(g, e=e):
                    EV.wait(g, EV.n)
                    LP[4].inc(g.dma_start(out=ROW0[0:1, 0:HT],
                                          in_=comb_dram[e:e + 1, :]), 16)
                cb0 = LP[4].n

                for tt in range(2):
                    @block.tensor
                    def _(t, tt=tt):
                        t.wait_ge(LP[4].h, cb0)
                        if tt == 0:
                            EV.wait(t)
                        MM.inc(t.matmul(PS_[6 + tt][:, 0:512], ONES1[0:1, :],
                                        ROW0[0:1, 512 * tt:512 * (tt + 1)],
                                        start=True, stop=True))

                    @block.vector
                    def _(v, tt=tt):
                        MM.wait(v)
                        EV.inc(v.tensor_copy(CMB[:, 512 * tt:512 * (tt + 1)],
                                             PS_[6 + tt][:, 0:512]))

                moe_block(slab_ap(wgr, 0, C, 256 * e, 256, 4096),
                          slab_ap(wur, 0, C, 256 * e, 256, 4096),
                          slab_ap(wdr, 256 * e, 256, 0, C, C),
                          256, None, 0, first=False)

            if APH in (9, 10):
                @block.sync
                def _(sy):
                    EV.wait(sy)
                    EA.wait(sy)
                    for k in range(NCH):
                        ST.inc(sy.dma_start(out=x1_out[k, :, t0:t0 + HT],
                                            in_=G0[:, HT * k:HT * (k + 1)]), 16)
            if APH <= 10:
                return
            for k in range(NCH):
                for tt in range(2):
                    @block.scalar
                    def _(s, k=k, tt=tt):
                        EV.wait(s)
                        sl = slice(HT * k + 512 * tt, HT * k + 512 * tt + 512)
                        EA.inc(s.activation(SC1[:, 512 * (tt % 2):
                                                512 * (tt % 2) + 512],
                                            G0[:, sl], AF.Silu))

                    @block.vector
                    def _(v, k=k, tt=tt):
                        EA.wait(v)
                        sl = slice(HT * k + 512 * tt, HT * k + 512 * tt + 512)
                        v.tensor_mul(SC2[:, 0:512],
                                     S0[:, sl], SC1[:, 512 * (tt % 2):
                                                    512 * (tt % 2) + 512])
                        EV.inc(v.tensor_add(XF[:, sl], XF[:, sl], SC2[:, 0:512]))

            @block.sync
            def _(sy):
                EV.wait(sy)
                for k in range(NCH):
                    ST.inc(sy.dma_start(out=x1_out[k, :, t0:t0 + HT],
                                        in_=XF[:, HT * k:HT * (k + 1)]), 16)

            if APH <= 11:
                return
            emit_ln_stats(XF, XN, out_dram=xn1_out)

            @block.gpsimd
            def _(g):
                MM.wait(g)
                wload(g, LP[5], slab_ap(l1rw, 0, C, 0, 8, 8), 8128, NCH, 8)
                LP[4].inc(g.dma_start(out=ROW0[0:1, 0:8], in_=l1rb[0:1, :]), 16)
            lb0 = LP[5].n
            lb1 = LP[4].n

            @block.tensor
            def _(t):
                t.wait_ge(LP[5].h, lb0)
                t.wait_ge(LP[4].h, lb1)
                EV.wait(t)
                MM.inc(t.matmul(PS_[5][:, 0:8], ONES1[0:1, :], ROW0[0:1, 0:8],
                                start=True, stop=True))

            @block.vector
            def _(v):
                MM.wait(v)
                EV.inc(v.tensor_copy(RS[:, 56:64], PS_[5][:, 0:8]))

            for j in range(8):
                @block.tensor
                def _(t, j=j):
                    if j >= 2:
                        EV.wait(t, EV.n - 1)
                    bank = 6 + j % 2
                    for kc in range(NCH):
                        ins = t.matmul(
                            PS_[bank][:, 0:8],
                            XN[:, HT * kc + 128 * j:HT * kc + 128 * (j + 1)],
                            WS[:, 8128 + 8 * kc:8128 + 8 * (kc + 1)],
                            start=(kc == 0), stop=(kc == NCH - 1))
                    MM.inc(ins)

                @block.vector
                def _(v, j=j):
                    MM.wait(v)
                    bank = 6 + j % 2
                    EV.inc(v.tensor_add(RT[:, 16 * j:16 * j + 8],
                                        PS_[bank][:, 0:8], RS[:, 56:64]))

            RL = AP(RT[:, :].tensor, RT[:, :].offset,
                    [list(RT[:, :].ap[0]), [16, 8], [1, 8]])

            @block.scalar
            def _(s):
                EV.wait(s)
                EA.inc(s.activation(RT[:, :], RT[:, :], AF.Exp))

            @block.vector
            def _(v):
                EA.wait(v)
                edge(v, v.reduce_sum(RS[:, 0:8], RL, axis=AX.X))
                edge(v, v.reciprocal(RS[:, 0:8], RS[:, 0:8]))
                EV.inc(v.tensor_mul(RL, RL, bcast_inner(RS[:, 0:8], 8)))

            @block.sync
            def _(sy):
                EV.wait(sy)
                for j in range(8):
                    ST.inc(sy.dma_start(
                        out=sc_out[t0 + 128 * j:t0 + 128 * (j + 1), :],
                        in_=RT[:, 16 * j:16 * j + 8]), 16)

        emit_half(0)
        emit_half(1)

        @block.sync
        def _(sy):
            ST.wait(sy)

    al.close()
    return nc



_A_CACHE = {}


def prep_A_weights(inp):
    w = {}
    w["wqkv"] = inp["L0_attn_W_qkv"].astype(np.float32)
    w["wproj"] = inp["L0_attn_W_proj"].astype(np.float32)
    w["bproj"] = inp["L0_attn_b_proj"].astype(np.float32).reshape(1, C)
    w["r0w"] = inp["L0_ffn_router_w"].astype(np.float32)
    w["wgs"] = np.concatenate([inp["L0_ffn_wg_s"][i] for i in range(2)],
                              axis=1).astype(np.float32)
    w["wus"] = np.concatenate([inp["L0_ffn_wu_s"][i] for i in range(2)],
                              axis=1).astype(np.float32)
    w["wds"] = np.concatenate([inp["L0_ffn_wd_s"][i] for i in range(2)],
                              axis=0).astype(np.float32)
    w["wgr"] = np.concatenate([inp["L0_ffn_wg_r"][i] for i in range(16)],
                              axis=1).astype(np.float32)
    w["wur"] = np.concatenate([inp["L0_ffn_wu_r"][i] for i in range(16)],
                              axis=1).astype(np.float32)
    w["wdr"] = np.concatenate([inp["L0_ffn_wd_r"][i] for i in range(16)],
                              axis=0).astype(np.float32)
    w["l1rw"] = inp["L1_ffn_router_w"].astype(np.float32)
    w["l1rb"] = inp["L1_ffn_router_b"].astype(np.float32).reshape(1, 8)
    tril = np.where(np.tril(np.ones((128, 128), bool)), 0.0, -1e30)
    w["trilm"] = tril.astype(np.float32)
    return {k: np.ascontiguousarray(v) for k, v in w.items()}


def run_launch_A(inp):
    x = np.asarray(inp["L0_attn_x"], np.float32).reshape(16384, C)
    w = prep_A_weights(inp)
    nc = build_launch_A()
    in_maps = []
    for c in range(N_CORES):
        m = dict(w)
        m["x_in"] = np.ascontiguousarray(x[P * c:P * (c + 1)])
        in_maps.append(m)
    res = run_bass_kernel_spmd(nc, in_maps, core_ids=list(range(N_CORES)))
    x1 = np.empty((16384, C), np.float32)
    xn1 = np.empty((16384, C), BF)
    scores = np.empty((16384, 8), np.float32)
    for c in range(N_CORES):
        r = res.results[c]
        x1[P * c:P * (c + 1)] = r["x1_out"].reshape(C, P).T
        xn1[P * c:P * (c + 1)] = r["xn1_out"].reshape(C, P).T
        scores[P * c:P * (c + 1)] = r["sc_out"]
    return x1, xn1, scores





TB = 512
Q = 256


def declare_B_drams(nc):
    dr = {}
    for nm, shape in (("x1b", [NCH, 128, 4096]), ("tmcd", [128, 6 * NCH]),
                      ("maw1", [C, 128]), ("maw2", [128, C]),
                      ("dcw1", [C, 64]), ("dcw2", [64, 512]),
                      ("faaT", [64, 8]), ("wr_", [C, 512]), ("wk_", [C, 512]),
                      ("wv_", [C, 512]), ("hindm", [128, 32]),
                      ("strictT", [2, 128, 256])):
        dr[nm] = nc.dram_tensor(nm, shape, F32, kind="ExternalInput").ap()
    dr["wrows_d"] = nc.dram_tensor("wrows_d", [16, 520], F32).ap()
    return dr


def build_launch_B():
    nc = bass.Bass("TRN2", target_bir_lowering=False)
    dr = declare_B_drams(nc)
    y_out = nc.dram_tensor("y_out", [4, 128, 4096], F32,
                           kind="ExternalOutput").ap()
    alP = Alloc(nc)
    alB = Alloc(nc)
    with nc.Block() as block:
        emit_B_part(nc, block, alB, alP, dr, ("dram", y_out))
    alB.close()
    alP.close()
    return nc


def emit_B_part(nc, block, alB, alP, dr, y_sink):
    x1b = dr["x1b"]; tmcd = dr["tmcd"]; maw1 = dr["maw1"]; maw2 = dr["maw2"]
    dcw1 = dr["dcw1"]; dcw2 = dr["dcw2"]; faaT = dr["faaT"]
    wr_ = dr["wr_"]; wk_ = dr["wk_"]; wv_ = dr["wv_"]
    hindm = dr["hindm"]; strictT = dr["strictT"]; wrows_d = dr["wrows_d"]
    sb, ps = alB.sb, alB.ps
    sem = alP.sem
    CONSTB = alP.sb("CONSTB", [1, 4], F32)

    XT = sb("XT", [128, NCH * TB], F32)
    XX = sb("XX", [128, NCH * TB], F32)
    XXB = sb("XXB", [128, NCH * TB], BF16)
    XNB = sb("XNB", [128, NCH * TB], BF16)
    MIX = sb("MIX", [128, NCH * TB], BF16)
    MF = sb("MF", [128, NCH * TB], BF16)
    M1 = sb("M1", [128, TB], BF16)
    RF = sb("RF", [128, 4 * TB], BF16)
    KF = sb("KF", [128, 4 * TB], BF16)
    VF = sb("VF", [128, 4 * TB], BF16)
    KTM = sb("KTM", [128, 4 * TB], BF16)
    VTM = sb("VTM", [128, 4 * TB], BF16)
    KW = sb("KW", [128, 4 * TB], BF16)
    WD = sb("WD", [128, 4 * TB], F32)
    WH = sb("WH", [8, TB], F32)
    T1B = sb("T1B", [64, TB], BF16)
    WCC = sb("WCC", [8, 256], F32)
    WCUM = sb("WCUM", [8, 256], F32)
    WINT = sb("WINT", [8, 256], F32)
    WROWS = sb("WROWS", [8, 520], F32)
    WRB = sb("WRB", [1, 2 * 520], F32)
    WEXPA = sb("WEXPA", [8, 256], F32)
    WEXPB = sb("WEXPB", [8, 256], F32)
    IND128 = sb("IND128", [8, 8 * 128], F32)
    IND64 = sb("IND64", [8, 8 * 64], F32)
    WIT = sb("WIT", [128, 4 * 8], F32)
    WCT = sb("WCT", [128, 4 * 8], F32)
    SCT = sb("SCT", [128, 2 * 256], BF16)
    MSK = sb("MSK", [128, 2 * 256], F32)
    MSK8 = sb("MSK8", [128, 8 * 512], F32)
    RSW = sb("RSW", [64, 256], BF16)
    KV = sb("KV", [64, 512], F32)
    STATE = sb("STATE", [64, 512], F32)
    STB = sb("STB", [64, 512], BF16)
    YC = sb("YC", [128, 4 * 256], F32)
    DGR = sb("DGR", [1, 3 * 256], F32)
    WSROW = sb("WSROW", [1, 768], F32)
    WSR = sb("WSR", [128, NCH * 512], BF16)
    WSK = sb("WSK", [128, NCH * 512], BF16)
    WSV = sb("WSV", [128, NCH * 512], BF16)
    WSA = sb("WSA", [128, NCH * 128], BF16)
    WSA2 = sb("WSA2", [128, C], BF16)
    WSD1 = sb("WSD1", [128, NCH * 64], BF16)
    WSD2 = sb("WSD2", [64, 512], BF16)
    SRW = sb("SRW", [1, 3 * TB], F32)
    SC1B = sb("SC1B", [128, 2048], F32)
    SC2B = sb("SC2B", [64, 512], F32)
    FAA = sb("FAA", [64, 8], F32)
    HIND = sb("HIND", [128, 32], F32)
    STT = sb("STT", [128, 2 * 256], F32)
    PREV = sb("PREV", [128, NCH], F32)
    ONES1b = sb("ONES1b", [1, 128], F32)
    ONESCb = sb("ONESCb", [128, 1], F32)
    IDFb = sb("IDFb", [128, 128], F32)
    IDB = sb("IDB", [128, 128], BF16)
    ZR256 = sb("ZR256", [8, 256], F32)
    TMC = sb("TMC", [128, 6 * NCH], F32)

    PSF = [ps(f"PBF{i}", [128, 512], F32) for i in range(6)]
    PSB = [ps(f"PBB{i}", [128, 512], BF16) for i in range(2)]

    LX = sem("LX")
    LW = sem("LW")
    EV = sem("EV")
    EA = sem("EA")
    MMG = sem("MMG")
    ST = sem("ST")
    VE = sem("VE")
    BK = [sem(f"BKF{i}") for i in range(6)] + [sem(f"BKB{i}") for i in range(2)]
    LPB = [sem(f"LPB{i}") for i in range(5)]
    WRC = sem("WRC")

    SUMS = MEAN = MR = SRW[0:1, 0 * TB:1 * TB]
    SSQ = VAR = SRW[0:1, 1 * TB:2 * TB]
    RSCR = RSTD = SRW[0:1, 2 * TB:3 * TB]

    def slab_ap(dram, r0, nr, c0, ncols, row_len):
        t = dram.tensor
        off = dram.offset + r0 * row_len + c0
        return AP(t, off, [[row_len, 128], [128 * row_len, max(nr // 128, 1)],
                           [1, ncols]])

    def sb_dst(tile, col0, nkc, ncols, width):
        full = tile[:, :]
        return AP(full.tensor, full.offset + col0,
                  [[width, 128], [ncols, nkc], [1, ncols]])

    if True:

        nc.const_aps.register(F32, LNEPS, CONSTB[0:1, 0:1])

        def edge(eng, inst):
            VE.inc(inst)
            VE.wait(eng)

        @block.gpsimd
        def _(g):
            g.memset(ONES1b[:, :], 1.0)
            g.memset(ONESCb[:, :], 1.0)
            g.memset(ZR256[:, :], 0.0)
            g.memset(PREV[:, :], 0.0)
            g.memset(STATE[:, :], 0.0)
            g.memset(STB[:, :], 0.0)
            g.memset(WROWS[:, :], 0.0)
            g.memset(CONSTB[0:1, 0:1], LNEPS)
            g.memset(IDB[:, :], 1.0)
            LW.inc(g.memset(IDFb[:, :], 1.0))
            g.wait_ge(LW.h, 1)
            LW.inc(g.affine_select(IDFb[:, :], IDFb[:, :], pattern=[[1, 128]],
                                   compare_op=ALU.is_equal, fill=0.0,
                                   channel_multiplier=-1))
            g.wait_ge(LW.h, 2)
            LW.inc(g.affine_select(IDB[:, :], IDB[:, :], pattern=[[1, 128]],
                                   compare_op=ALU.is_equal, fill=0.0,
                                   channel_multiplier=-1))
            g.wait_ge(LW.h, 3)
            LW.inc(g.memset(IND128[:, :], 1.0))
            g.wait_ge(LW.h, 4)
            LW.inc(g.affine_select(IND128[:, :], IND128[:, :],
                                   pattern=[[1, 1024]], compare_op=ALU.is_ge,
                                   fill=0.0, base=0, channel_multiplier=-128))
            g.wait_ge(LW.h, 5)
            LW.inc(g.affine_select(IND128[:, :], IND128[:, :],
                                   pattern=[[-1, 1024]], compare_op=ALU.is_ge,
                                   fill=0.0, base=127, channel_multiplier=128))
            g.wait_ge(LW.h, 6)
            LW.inc(g.memset(IND64[:, :], 1.0))
            g.wait_ge(LW.h, 7)
            LW.inc(g.affine_select(IND64[:, :], IND64[:, :],
                                   pattern=[[1, 512]], compare_op=ALU.is_ge,
                                   fill=0.0, base=0, channel_multiplier=-64))
            g.wait_ge(LW.h, 8)
            LW.inc(g.affine_select(IND64[:, :], IND64[:, :],
                                   pattern=[[-1, 512]], compare_op=ALU.is_ge,
                                   fill=0.0, base=63, channel_multiplier=64))
            LPB[0].inc(g.dma_start(out=sb_dst(WSR, 0, NCH, 512, NCH * 512),
                                   in_=slab_ap(wr_, 0, C, 0, 512, 512)), 16)
            LPB[0].inc(g.dma_start(out=sb_dst(WSK, 0, NCH, 512, NCH * 512),
                                   in_=slab_ap(wk_, 0, C, 0, 512, 512)), 16)
            LPB[0].inc(g.dma_start(out=sb_dst(WSV, 0, NCH, 512, NCH * 512),
                                   in_=slab_ap(wv_, 0, C, 0, 512, 512)), 16)
            LPB[0].inc(g.dma_start(out=sb_dst(WSA, 0, NCH, 128, NCH * 128),
                                   in_=slab_ap(maw1, 0, C, 0, 128, 128)), 16)
            LPB[0].inc(g.dma_start(out=WSA2[:, :], in_=maw2[:, :]), 16)
            LPB[0].inc(g.dma_start(out=sb_dst(WSD1, 0, NCH, 64, NCH * 64),
                                   in_=slab_ap(dcw1, 0, C, 0, 64, 64)), 16)
            LPB[0].inc(g.dma_start(out=WSD2[:, :], in_=dcw2[:, :]), 16)

        @block.sync
        def _(sy):
            LX.inc(sy.dma_start(out=HIND[:, :], in_=hindm[:, :]), 16)
            LX.inc(sy.dma_start(out=FAA[:, :], in_=faaT[:, :]), 16)
            LX.inc(sy.dma_start(out=STT[:, 0:256], in_=strictT[0, :, :]), 16)
            LX.inc(sy.dma_start(out=STT[:, 256:512], in_=strictT[1, :, :]), 16)
            LX.inc(sy.dma_start(out=TMC[:, :], in_=tmcd[:, :]), 16)

        wrows_rd = {"n": None}
        for it in range(8):
            t0 = it * TB

            @block.sync
            def _(sy, it=it, t0=t0):
                EV.wait(sy)
                EA.wait(sy)
                for kc in range(NCH):
                    LX.inc(sy.dma_start(out=XT[:, TB * kc:TB * (kc + 1)],
                                        in_=x1b[kc, :, t0:t0 + TB]), 16)
            lx_t = LX.n

            for grp in range(2):
                @block.scalar
                def _(s, grp=grp):
                    s.wait_ge(LX.h, lx_t)
                    EV.wait(s)
                    if grp == 1:
                        MMG.wait(s)
                    for kk in range(4):
                        kc = grp * 4 + kk
                        ins = s.activation(SC1B[:, 512 * kk:512 * (kk + 1)],
                                           XT[:, TB * kc:TB * (kc + 1)],
                                           AF.Square)
                    EA.inc(ins)

                @block.tensor
                def _(t, grp=grp):
                    EA.wait(t)
                    if grp == 0:
                        t.wait_ge(LW.h, 9)
                        LPB[0].wait(t)
                        BK[4].wait(t)
                        BK[5].wait(t)
                    for kk in range(4):
                        kc = grp * 4 + kk
                        t.matmul(PSF[4][0:1, 0:512], ONESCb[:, 0:1],
                                 XT[:, TB * kc:TB * (kc + 1)],
                                 start=(kc == 0), stop=(kc == 7))
                        ins = t.matmul(PSF[5][0:1, 0:512], ONESCb[:, 0:1],
                                       SC1B[:, 512 * kk:512 * (kk + 1)],
                                       start=(kc == 0), stop=(kc == 7))
                    MMG.inc(ins)

            @block.vector
            def _(v):
                MMG.wait(v)
                v.tensor_copy(SUMS, PSF[4][0:1, 0:512])
                ins = v.tensor_copy(SSQ, PSF[5][0:1, 0:512])
                BK[4].inc(ins)
                BK[5].inc(v.engine_nop())
                edge(v, v.tensor_scalar_mul(MEAN, SUMS, 1.0 / C))
                edge(v, v.tensor_scalar_mul(VAR, SSQ, 1.0 / C))
                edge(v, v.tensor_mul(RSCR, MEAN, MEAN))
                EV.inc(v.tensor_sub(VAR, VAR, RSCR))

            @block.scalar
            def _(s):
                EV.wait(s)
                edge(s, s.activation(RSCR, VAR, AF.Ln, bias=LNEPS))
                EA.inc(s.activation(RSTD, RSCR, AF.Exp, scale=-0.5))

            @block.vector
            def _(v):
                EA.wait(v)
                EV.inc(v.tensor_mul(MR, MEAN, RSTD))

            @block.tensor
            def _(t):
                EV.wait(t)
                BK[4].wait(t)
                BK[5].wait(t)
                t.matmul(PSF[4][:, 0:512], ONES1b[0:1, :], RSTD,
                         start=True, stop=True)
                MMG.inc(t.matmul(PSF[5][:, 0:512], ONES1b[0:1, :], MR,
                                 start=True, stop=True))

            @block.vector
            def _(v):
                MMG.wait(v)
                for kc in range(NCH):
                    sl = slice(TB * kc, TB * (kc + 1))
                    v.tensor_mul(SC1B[:, 0:512], XT[:, sl], PSF[4][:, 0:512])
                    ins = v.tensor_sub(XT[:, sl], SC1B[:, 0:512], PSF[5][:, 0:512])
                BK[4].inc(ins)
                BK[5].inc(v.engine_nop())
                for kc in range(NCH):
                    sl0 = TB * kc
                    v.tensor_sub(XX[:, sl0 + 1:sl0 + TB], XT[:, sl0:sl0 + TB - 1],
                                 XT[:, sl0 + 1:sl0 + TB])
                    ins = v.tensor_sub(XX[:, sl0:sl0 + 1], PREV[:, kc:kc + 1],
                                       XT[:, sl0:sl0 + 1])
                    VE.inc(ins)
                for kc in range(NCH):
                    v.tensor_copy(PREV[:, kc:kc + 1],
                                  XT[:, TB * kc + TB - 1:TB * kc + TB])
                VE.wait(v)
                v.tensor_copy(XXB[:, :], XX[:, :])
                EV.inc(v.tensor_copy(XNB[:, :], XT[:, :]))

            @block.vector
            def _(v):
                EV.wait(v)
                for kc in range(NCH):
                    sl = slice(TB * kc, TB * (kc + 1))
                    ins = v.scalar_tensor_tensor(MIX[:, sl], XX[:, sl],
                                                 TMC[:, kc:kc + 1], XT[:, sl],
                                                 ALU.mult, ALU.add)
                EV.inc(ins)

            @block.tensor
            def _(t):
                EV.wait(t)
                LPB[0].wait(t)
                BK[0].wait(t)
                for kc in range(NCH):
                    ins = t.matmul(PSF[0][:, 0:512],
                                   WSA[:, 128 * kc:128 * (kc + 1)],
                                   MIX[:, TB * kc:TB * (kc + 1)],
                                   start=(kc == 0), stop=(kc == 7))
                MMG.inc(ins)

            @block.scalar
            def _(s):
                MMG.wait(s)
                ins = s.activation(M1[:, :], PSF[0][:, 0:512], AF.Tanh)
                EA.inc(ins)
                BK[0].inc(s.nop())

            for f, (dst, tmblk) in enumerate(
                    [(None, 1), (KF, 2), (VF, 3), (RF, 4)]):
                for sl8 in range(NCH):
                    b = sl8 % 2

                    @block.tensor
                    def _(t, f=f, sl8=sl8, b=b):
                        if f == 0 and sl8 == 0:
                            EA.wait(t)
                        BK[b].wait(t)
                        ins = t.matmul(
                            PSF[b][:, 0:512],
                            WSA2[32 * f:32 * (f + 1), 128 * sl8:128 * (sl8 + 1)],
                            M1[32 * f:32 * (f + 1), :],
                            start=True, stop=True,
                            tile_position=(32 * f, 0))
                        MMG.inc(ins)

                    @block.vector
                    def _(v, f=f, sl8=sl8, b=b, tmblk=tmblk):
                        MMG.wait(v)
                        ins = v.tensor_scalar(
                            MF[:, TB * sl8:TB * (sl8 + 1)], PSF[b][:, 0:512],
                            TMC[:, NCH * tmblk + sl8:NCH * tmblk + sl8 + 1],
                            None, op0=ALU.add)
                        BK[b].inc(ins)
                        if sl8 == NCH - 1:
                            EV.inc(v.engine_nop())

                @block.vector
                def _(v, f=f):
                    EV.wait(v)
                    for kc in range(NCH):
                        sl = slice(TB * kc, TB * (kc + 1))
                        if f == 0:
                            v.tensor_mul(SC1B[:, 0:512], XX[:, sl], MF[:, sl])
                            ins = v.tensor_add(MIX[:, sl], SC1B[:, 0:512],
                                               XT[:, sl])
                        else:
                            v.tensor_mul(SC1B[:, 0:512], XXB[:, sl], MF[:, sl])
                            ins = v.tensor_add(MIX[:, sl], SC1B[:, 0:512],
                                               XNB[:, sl])
                    EV.inc(ins)

                if f == 0:
                    @block.tensor
                    def _(t):
                        EV.wait(t)
                        BK[2].wait(t)
                        for kc in range(NCH):
                            ins = t.matmul(PSF[2][0:64, 0:512],
                                           WSD1[:, 64 * kc:64 * (kc + 1)],
                                           MIX[:, TB * kc:TB * (kc + 1)],
                                           start=(kc == 0), stop=(kc == 7))
                        MMG.inc(ins)

                    @block.scalar
                    def _(s):
                        MMG.wait(s)
                        ins = s.activation(T1B[:, :], PSF[2][0:64, 0:512], AF.Tanh)
                        EA.inc(ins)
                        BK[2].inc(s.nop())

                    for s4 in range(4):
                        b = s4 % 2

                        @block.tensor
                        def _(t, s4=s4, b=b):
                            if s4 == 0:
                                EA.wait(t)
                            BK[b].wait(t)
                            ins = t.matmul(PSF[b][:, 0:512],
                                           WSD2[:, 128 * s4:128 * (s4 + 1)],
                                           T1B[:, :], start=True, stop=True)
                            MMG.inc(ins)

                        @block.vector
                        def _(v, s4=s4, b=b):
                            MMG.wait(v)
                            ins = v.tensor_scalar(
                                WD[:, TB * s4:TB * (s4 + 1)], PSF[b][:, 0:512],
                                TMC[:, NCH * 5 + s4:NCH * 5 + s4 + 1],
                                None, op0=ALU.add)
                            BK[b].inc(ins)
                            if s4 == 3:
                                EV.inc(v.engine_nop())

                    @block.tensor
                    def _(t):
                        EV.wait(t)
                        BK[3].wait(t)
                        for s4 in range(4):
                            ins = t.matmul(PSF[3][0:8, 0:512],
                                           HIND[:, 8 * s4:8 * (s4 + 1)],
                                           WD[:, TB * s4:TB * (s4 + 1)],
                                           start=(s4 == 0), stop=(s4 == 3))
                        MMG.inc(ins)

                    @block.vector
                    def _(v):
                        MMG.wait(v)
                        ins = v.tensor_copy(WH[:, :], PSF[3][0:8, 0:512])
                        BK[3].inc(ins)
                        EV.inc(v.engine_nop())
                else:
                    for s4 in range(4):
                        b = s4 % 2
                        wsl = [None, WSK, WSV, WSR][f]

                        @block.tensor
                        def _(t, f=f, s4=s4, b=b, wsl=wsl):
                            if s4 == 0:
                                EV.wait(t)
                            BK[b].wait(t)
                            for kc in range(NCH):
                                ins = t.matmul(
                                    PSF[b][:, 0:512],
                                    wsl[:, 512 * kc + 128 * s4:
                                        512 * kc + 128 * (s4 + 1)],
                                    MIX[:, TB * kc:TB * (kc + 1)],
                                    start=(kc == 0), stop=(kc == 7))
                            MMG.inc(ins)

                        @block.vector
                        def _(v, f=f, s4=s4, b=b, dst=dst):
                            MMG.wait(v)
                            ins = v.tensor_copy(dst[:, TB * s4:TB * (s4 + 1)],
                                                PSF[b][:, 0:512])
                            BK[b].inc(ins)
                            if s4 == 3:
                                EV.inc(v.engine_nop())

            for (srcf, dstf) in ((KF, KTM), (VF, VTM)):
                for s4 in range(4):
                    bb = s4 % 2

                    @block.tensor
                    def _(t, srcf=srcf, s4=s4, bb=bb):
                        if srcf is KF and s4 == 0:
                            EV.wait(t)
                        BK[6 + bb].wait(t)
                        for qt in range(4):
                            ins = t.transpose(
                                PSB[bb][:, 128 * qt:128 * (qt + 1)],
                                srcf[:, TB * s4 + 128 * qt:TB * s4 + 128 * (qt + 1)],
                                IDB[:, :])
                        MMG.inc(ins)

                    @block.vector
                    def _(v, dstf=dstf, s4=s4, bb=bb):
                        MMG.wait(v)
                        dst = AP(dstf[:, :].tensor,
                                 dstf[:, :].offset + 128 * s4,
                                 [[4 * TB, 128], [512, 4], [1, 128]])
                        src = PSB[bb][:, 0:512].rearrange("p (g f) -> p g f", f=128)
                        ins = v.tensor_copy(dst, src)
                        BK[6 + bb].inc(ins)
                        if dstf is VTM and s4 == 3:
                            EV.inc(v.engine_nop())

            for cc in range(2):
                c_glob = 2 * it + cc

                @block.scalar
                def _(s, cc=cc):
                    EV.wait(s)
                    EA.inc(s.activation(WCC[:, :], WH[0:8, 256 * cc:256 * (cc + 1)],
                                        AF.Exp))

                @block.vector
                def _(v, cc=cc):
                    EA.wait(v)
                    if wrows_rd["n"] is not None:
                        v.wait_ge(MMG.h, wrows_rd["n"])
                    edge(v, v.tensor_scalar_mul(WCC[:, :], WCC[:, :], -1.0))
                    edge(v, v.tensor_tensor_scan(WCUM[:, :], WCC[:, :],
                                                 ZR256[:, :], 0.0,
                                                 ALU.add, ALU.add))
                    edge(v, v.tensor_sub(WROWS[0:8, 0:256], WCC[:, :],
                                         WCUM[:, :]))
                    edge(v, v.tensor_copy(WROWS[0:8, 256:512], WCC[:, :]))
                    EV.inc(v.tensor_copy(WROWS[0:8, 512:513],
                                         WCUM[:, 255:256]))
                wexp_ev = EV.n

                @block.scalar
                def _(s, cc=cc):
                    EV.wait(s)
                    s.activation(WEXPA[:, :], WROWS[0:8, 0:256], AF.Exp,
                                 scale=-1.0)
                    EA.inc(s.activation(WEXPB[:, :], WROWS[0:8, 256:512],
                                        AF.Exp))
                wexp_ea = EA.n

                @block.scalar
                def _(s, cc=cc):
                    EV.wait(s)
                    EA.inc(s.activation(WINT[:, :], WCUM[:, :], AF.Exp,
                                        scale=-1.0,
                                        bias=WCUM[:, 255:256]))

                @block.tensor
                def _(t, cc=cc):
                    EA.wait(t)
                    BK[2].wait(t)
                    for qt in range(2):
                        t.transpose(PSF[2][0:128, 16 * qt:16 * qt + 8],
                                    WINT[0:8, 128 * qt:128 * (qt + 1)],
                                    IDFb[0:8, 0:8])
                        ins = t.transpose(PSF[2][0:128, 16 * qt + 8:16 * qt + 16],
                                          WCUM[0:8, 128 * qt:128 * (qt + 1)],
                                          IDFb[0:8, 0:8])
                    ins = t.transpose(PSF[2][0:1, 40:48],
                                      WCUM[0:8, 255:256], IDFb[0:8, 0:8])
                    MMG.inc(ins)

                @block.vector
                def _(v, cc=cc):
                    MMG.wait(v)
                    for qt in range(2):
                        v.tensor_copy(WIT[:, 8 * (2 * cc + qt):8 * (2 * cc + qt) + 8],
                                      PSF[2][0:128, 16 * qt:16 * qt + 8])
                        v.tensor_copy(
                            WCT[:, 8 * (2 * cc + qt):8 * (2 * cc + qt) + 8],
                            PSF[2][0:128, 16 * qt + 8:16 * qt + 16])
                    ins = v.tensor_copy(WSROW[0:1, 0:8], PSF[2][0:1, 40:48])
                    BK[2].inc(ins)
                    EV.inc(v.engine_nop())

                @block.vector
                def _(v, cc=cc):
                    EV.wait(v)
                    for qt in range(2):
                        qcol = 512 * (2 * cc + qt)
                        ins = v.tensor_mul(
                            KW[:, qcol:qcol + 512].rearrange("p (h n) -> p h n", n=64),
                            KTM[:, qcol:qcol + 512].rearrange("p (h n) -> p h n", n=64),
                            bcast_inner(WIT[:, 8 * (2 * cc + qt):
                                            8 * (2 * cc + qt) + 8], 64))
                    EV.inc(ins)

                for h2 in range(4):
                    @block.tensor
                    def _(t, cc=cc, h2=h2, wexp_ev=wexp_ev):
                        if h2 == 0:
                            t.wait_ge(EV.h, wexp_ev)
                        BK[0].wait(t)
                        BK[2].wait(t)
                        t.matmul(PSF[0][:, 0:256],
                                 IND128[0:8, 256 * h2:256 * h2 + 128],
                                 WROWS[0:8, 0:256], start=True, stop=True)
                        ins = t.matmul(PSF[2][:, 0:256],
                                       IND128[0:8, 256 * h2 + 128:
                                              256 * (h2 + 1)],
                                       WROWS[0:8, 0:256], start=True, stop=True)
                        MMG.inc(ins)

                    @block.vector
                    def _(v, cc=cc, h2=h2):
                        MMG.wait(v)
                        for w in range(2):
                            h = 2 * h2 + w
                            pb = PSF[0] if w == 0 else PSF[2]
                            for kt in range(2):
                                wcol = 8 * (2 * cc + kt) + h
                                ins = v.tensor_scalar(
                                    MSK8[:, 512 * h + 256 * kt:
                                         512 * h + 256 * (kt + 1)],
                                    pb[:, 0:256],
                                    WCT[:, wcol:wcol + 1], None, op0=ALU.add)
                            if w == 0:
                                BK[0].inc(ins)
                            else:
                                BK[2].inc(ins)
                        if h2 == 3:
                            EV.inc(v.engine_nop())

                @block.scalar
                def _(s, cc=cc):
                    EV.wait(s)
                    s.activation(MSK8[:, :], MSK8[:, :], AF.Relu)
                    EA.inc(s.activation(MSK8[:, :], MSK8[:, :], AF.Exp,
                                        scale=-1.0))

                @block.vector
                def _(v, cc=cc):
                    EA.wait(v)
                    ins = v.tensor_mul(
                        MSK8[:, :].rearrange("p (h f) -> p h f", f=512),
                        MSK8[:, :].rearrange("p (h f) -> p h f", f=512),
                        rep_mid(STT[:, 0:512], 8))
                    EV.inc(ins)

                for h in range(8):
                    hp = h % 2
                    hc = h // 2

                    @block.tensor
                    def _(t, h=h, cc=cc):
                        if h == 0:
                            EV.wait(t)
                        BK[4].wait(t)
                        for qt in range(2):
                            qcol = 512 * (2 * cc + qt)
                            ins = t.matmul(PSF[4][0:64, 0:64],
                                           KW[:, qcol + 64 * h:qcol + 64 * (h + 1)],
                                           VTM[:, qcol + 64 * h:qcol + 64 * (h + 1)],
                                           start=(qt == 0), stop=(qt == 1))
                        MMG.inc(ins)

                    @block.vector
                    def _(v, h=h):
                        MMG.wait(v)
                        ins = v.tensor_copy(KV[:, 64 * h:64 * (h + 1)],
                                            PSF[4][0:64, 0:64])
                        BK[4].inc(ins)

                    @block.tensor
                    def _(t, h=h, wexp_ea=wexp_ea):
                        t.wait_ge(EA.h, wexp_ea)
                        BK[4].wait(t)
                        ins = t.matmul(PSF[4][0:64, 0:256],
                                       IND64[0:8, 64 * h:64 * (h + 1)],
                                       WEXPB[0:8, 0:256],
                                       start=True, stop=True)
                        MMG.inc(ins)

                    for kt in range(2):
                        @block.tensor
                        def _(t, h=h, cc=cc, kt=kt, hp=hp, hc=hc):
                            if kt == 0:
                                EV.wait(t)
                            BK[1 + kt].wait(t)
                            ins = t.matmul(
                                PSF[1 + kt][:, 0:256],
                                KF[64 * hp:64 * hp + 64,
                                   TB * hc + 256 * cc + 128 * kt:
                                   TB * hc + 256 * cc + 128 * (kt + 1)],
                                RF[64 * hp:64 * hp + 64,
                                   TB * hc + 256 * cc:TB * hc + 256 * (cc + 1)],
                                start=True, stop=True,
                                tile_position=(64 * hp, 0))
                            MMG.inc(ins)

                        @block.vector
                        def _(v, h=h, kt=kt):
                            MMG.wait(v)
                            ins = v.tensor_mul(SCT[:, 256 * kt:256 * (kt + 1)],
                                               PSF[1 + kt][:, 0:256],
                                               MSK8[:, 512 * h + 256 * kt:
                                                    512 * h + 256 * (kt + 1)])
                            BK[1 + kt].inc(ins)
                            if kt == 1:
                                EV.inc(v.engine_nop())

                    @block.tensor
                    def _(t, h=h, wexp_ea=wexp_ea):
                        t.wait_ge(EA.h, wexp_ea)
                        BK[0].wait(t)
                        ins = t.matmul(PSF[0][0:64, 0:256],
                                       IND64[0:8, 64 * h:64 * (h + 1)],
                                       WEXPA[0:8, 0:256],
                                       start=True, stop=True)
                        MMG.inc(ins)

                    @block.vector
                    def _(v, h=h, cc=cc, hp=hp, hc=hc):
                        MMG.wait(v)
                        ins = v.tensor_mul(
                            RSW[:, :],
                            RF[64 * hp:64 * hp + 64,
                               TB * hc + 256 * cc:TB * hc + 256 * (cc + 1)],
                            PSF[0][0:64, 0:256])
                        BK[0].inc(ins)
                        EV.inc(v.engine_nop())

                    @block.tensor
                    def _(t, h=h, cc=cc, hp=hp, hc=hc):
                        EV.wait(t)
                        BK[5].wait(t)
                        for kt in range(2):
                            qcol = 512 * (2 * cc) + 0
                            t.matmul(
                                PSF[5][64 * hp:64 * hp + 64, 0:256],
                                VTM[:, 512 * (2 * cc + kt) + 64 * h:
                                    512 * (2 * cc + kt) + 64 * (h + 1)],
                                SCT[:, 256 * kt:256 * (kt + 1)],
                                start=(kt == 0), stop=False,
                                tile_position=(0, 64 * hp))
                        ins = t.matmul(
                            PSF[5][64 * hp:64 * hp + 64, 0:256],
                            STB[:, 64 * h:64 * (h + 1)],
                            RSW[:, :], start=False, stop=True,
                            tile_position=(0, 64 * hp))
                        MMG.inc(ins)

                    @block.vector
                    def _(v, h=h, cc=cc, hp=hp, hc=hc):
                        sl = slice(TB * hc + 256 * cc, TB * hc + 256 * (cc + 1))
                        v.tensor_mul(SC2B[:, 0:256], RF[64 * hp:64 * hp + 64, sl],
                                     KF[64 * hp:64 * hp + 64, sl])
                        EV.inc(v.tensor_scalar(SC2B[:, 0:256], SC2B[:, 0:256],
                                               FAA[:, h:h + 1], None,
                                               op0=ALU.mult))

                    @block.tensor
                    def _(t, h=h):
                        EV.wait(t)
                        BK[3].wait(t)
                        ins = t.matmul(PSF[3][0:1, 0:256], ONESCb[0:64, 0:1],
                                       SC2B[:, 0:256], start=True, stop=True)
                        MMG.inc(ins)

                    @block.vector
                    def _(v, h=h):
                        MMG.wait(v)
                        ins = v.tensor_copy(DGR[0:1, 512:768],
                                            PSF[3][0:1, 0:256])
                        BK[3].inc(ins)

                    @block.tensor
                    def _(t, h=h):
                        BK[3].wait(t)
                        BK[0].wait(t)
                        ins = t.matmul(PSF[0][0:64, 0:256], ONES1b[0:1, 0:64],
                                       DGR[0:1, 512:768], start=True, stop=True)
                        MMG.inc(ins)

                    @block.vector
                    def _(v, h=h, cc=cc, hp=hp, hc=hc):
                        MMG.wait(v)
                        sl = slice(TB * hc + 256 * cc, TB * hc + 256 * (cc + 1))
                        ins = v.tensor_mul(SC2B[:, 256:512],
                                           VF[64 * hp:64 * hp + 64, sl],
                                           PSF[0][0:64, 0:256])
                        BK[0].inc(ins)
                        ins = v.tensor_mul(SC2B[:, 256:512], SC2B[:, 256:512],
                                           PSF[4][0:64, 0:256])
                        BK[4].inc(ins)
                        ins = v.tensor_add(
                            YC[64 * hp:64 * hp + 64, 256 * hc:256 * (hc + 1)],
                            PSF[5][64 * hp:64 * hp + 64, 0:256],
                            SC2B[:, 256:512])
                        BK[5].inc(ins)
                        EV.inc(v.engine_nop())

                wrows_rd["n"] = MMG.n

                @block.scalar
                def _(s, cc=cc):
                    EV.wait(s)
                    EA.inc(s.activation(WSROW[0:1, 8:16], WSROW[0:1, 0:8], AF.Exp))

                @block.vector
                def _(v, cc=cc):
                    EA.wait(v)
                    ins = v.tensor_copy(
                        AP(WSROW[0:1, :].tensor, WSROW[0:1, :].offset + 16,
                           [[768, 1], [64, 8], [1, 64]]),
                        bcast_inner(WSROW[0:1, 8:16], 64))
                    EV.inc(ins)

                @block.tensor
                def _(t, cc=cc):
                    EV.wait(t)
                    BK[0].wait(t)
                    ins = t.matmul(PSF[0][0:64, 0:512], ONES1b[0:1, 0:64],
                                   WSROW[0:1, 16:528], start=True, stop=True)
                    MMG.inc(ins)

                @block.vector
                def _(v, cc=cc):
                    MMG.wait(v)
                    ins = v.tensor_mul(STATE[:, :], STATE[:, :], PSF[0][0:64, 0:512])
                    BK[0].inc(ins)
                    v.tensor_add(STATE[:, :], STATE[:, :], KV[:, :])
                    EV.inc(v.tensor_copy(STB[:, :], STATE[:, :]))

                if y_sink[0] == "dram":
                    y_out = y_sink[1]

                    @block.sync
                    def _(sy, cc=cc, c_glob=c_glob):
                        EV.wait(sy)
                        for j4 in range(4):
                            ST.inc(sy.dma_start(
                                out=y_out[j4, :, 256 * c_glob:256 * (c_glob + 1)],
                                in_=YC[:, 256 * j4:256 * (j4 + 1)]), 16)
                else:
                    ysh, YIW = y_sink[1], y_sink[2]

                    @block.gpsimd
                    def _(g, cc=cc, c_glob=c_glob):
                        EV.wait(g)
                        for j4 in range(4):
                            col = 4 * j4 + c_glob // 4
                            ST.inc(g.indirect_dma_start(
                                out=ysh[:, :],
                                out_offset=IndirectOffsetOnAxis(
                                    YIW[:, col:col + 1], 0),
                                in_=YC[:, 256 * j4:256 * (j4 + 1)],
                                in_offset=None,
                                element_offset=256 * (c_glob % 4)), 16)

        @block.sync
        def _(sy):
            ST.wait(sy)

    return {"ST": ST, "EV": EV}




def prep_B_inputs(inp, x1_full, core):
    b, hg = core // 2, core % 2
    w = {}
    w["x1b"] = np.ascontiguousarray(
        x1_full[4096 * b:4096 * (b + 1)].T.reshape(NCH, 128, 4096).astype(np.float32))
    cols = []
    for key in ["L1_attn_time_maa_x", "L1_attn_time_maa_w", "L1_attn_time_maa_k",
                "L1_attn_time_maa_v", "L1_attn_time_maa_r"]:
        cols.append(np.asarray(inp[key], np.float32).reshape(C).reshape(NCH, 128).T)
    td = np.asarray(inp["L1_attn_time_decay"], np.float32).reshape(C)
    tds = td[512 * hg:512 * (hg + 1)].reshape(4, 128).T
    tmcd = np.zeros((128, 6 * NCH), np.float32)
    for i, c_ in enumerate(cols):
        tmcd[:, NCH * i:NCH * (i + 1)] = c_
    tmcd[:, NCH * 5:NCH * 5 + 4] = tds
    w["tmcd"] = np.ascontiguousarray(tmcd)
    w["maw1"] = np.ascontiguousarray(np.asarray(inp["L1_attn_maa_w1"], np.float32))
    w["maw2"] = np.ascontiguousarray(
        np.asarray(inp["L1_attn_maa_w2"], np.float32).reshape(128, C))
    w["dcw1"] = np.ascontiguousarray(np.asarray(inp["L1_attn_decay_w1"], np.float32))
    w["dcw2"] = np.ascontiguousarray(
        np.asarray(inp["L1_attn_decay_w2"], np.float32)[:, 512 * hg:512 * (hg + 1)])
    faa = np.asarray(inp["L1_attn_time_faaaa"], np.float32).reshape(16, 64)
    w["faaT"] = np.ascontiguousarray(faa[8 * hg:8 * (hg + 1)].T)
    for nm, key in (("wr_", "L1_attn_Wr"), ("wk_", "L1_attn_Wk"),
                    ("wv_", "L1_attn_Wv")):
        w[nm] = np.ascontiguousarray(
            np.asarray(inp[key], np.float32)[:, 512 * hg:512 * (hg + 1)])
    hindm = np.zeros((128, 32), np.float32)
    for s4 in range(4):
        for p in range(128):
            hindm[p, 8 * s4 + 2 * s4 + p // 64] = 1.0 / 64
    w["hindm"] = hindm
    st = np.zeros((2, 128, 256), np.float32)
    for kt in range(2):
        for p in range(128):
            st[kt, p, 128 * kt + p + 1:] = 1.0
    w["strictT"] = st
    return w


def run_launch_B(inp, x1_full):
    nc = build_launch_B()
    in_maps = [prep_B_inputs(inp, x1_full, c) for c in range(N_CORES)]
    res = run_bass_kernel_spmd(nc, in_maps, core_ids=list(range(N_CORES)))
    y = np.empty((16384, C), np.float32)
    for core in range(N_CORES):
        b, hg = core // 2, core % 2
        yo = res.results[core]["y_out"]
        y[4096 * b:4096 * (b + 1), 512 * hg:512 * (hg + 1)] = \
            yo.reshape(512, 4096).T
    return y





def declare_C_drams(nc):
    dr = {}
    dr["lngb"] = nc.dram_tensor("lngb", [128, 2 * NCH], F32,
                                kind="ExternalInput").ap()
    dr["wo"] = nc.dram_tensor("wo", [C, C], F32, kind="ExternalInput").ap()
    dr["scrow"] = nc.dram_tensor("scrow", [1, 16384], F32,
                                 kind="ExternalInput").ap()
    dr["xn1f"] = nc.dram_tensor("xn1f", [16384, C], BF16,
                                kind="ExternalInput").ap()
    dr["w1e"] = nc.dram_tensor("w1e", [C, 4096], F32, kind="ExternalInput").ap()
    dr["b1e"] = nc.dram_tensor("b1e", [128, 32], F32, kind="ExternalInput").ap()
    dr["w2e"] = nc.dram_tensor("w2e", [4096, C], F32, kind="ExternalInput").ap()
    dr["b2e"] = nc.dram_tensor("b2e", [1, C], F32, kind="ExternalInput").ap()
    dr["triu"] = nc.dram_tensor("triu", [128, 128], F32,
                                kind="ExternalInput").ap()
    dr["s1_out"] = nc.dram_tensor("s1_out", [NCH, 128, P], F32,
                                  kind="ExternalOutput").ap()
    dr["co_out"] = nc.dram_tensor("co_out", [16384, C], BF16,
                                  kind="ExternalOutput").ap()
    dr["idx_d"] = nc.dram_tensor("idx_d", [2049, 2], F32).ap()
    return dr


def build_launch_C():
    nc = bass.Bass("TRN2", target_bir_lowering=False)
    dr = declare_C_drams(nc)
    dr["y_in"] = nc.dram_tensor("y_in", [NCH, 128, P], F32,
                                kind="ExternalInput").ap()
    alP = Alloc(nc)
    alC = Alloc(nc)
    with nc.Block() as block:
        emit_C_part(nc, block, alC, alP, dr, ("dram", dr["y_in"]))
    alC.close()
    alP.close()
    return nc


def emit_C_part(nc, block, alC, alP, dr, y_src, prezero=None):
    fused = y_src[0] != "dram"
    lngb = dr["lngb"]; wo = dr["wo"]; scrow = dr["scrow"]; xn1f = dr["xn1f"]
    w1e = dr["w1e"]; b1e = dr["b1e"]; w2e = dr["w2e"]; b2e = dr["b2e"]
    triu = dr["triu"]; s1_out = dr["s1_out"]; co_out = dr["co_out"]
    idx_d = dr["idx_d"]
    if fused:
        _, ysh, BARC, JN, yidxr = y_src
    else:
        y_in = y_src[1]
    sb, ps = alC.sb, alC.ps
    _tag = "c" if fused else ""

    def sem(n):
        return alP.sem(n + _tag)

    if not fused:
        CONSTc = alP.sb("CONSTc", [1, 4], F32)
    if fused:
        YIR = alP.sb("YIR", [128, 16], I32)

    YF = sb("YF", [128, NCH * HT], F32)
    S1 = sb("S1", [128, NCH * HT], F32)
    WSC = sb("WSC", [128, 8192], BF16)
    XG = sb("XG", [128, NCH * P], BF16)
    YLN = XG[:, 0:NCH * HT]
    XGR = sb("XGR", [128, 2048], BF16)
    XGF = sb("XGF", [128, 2048], F32)
    H1 = sb("H1", [128, 32 * 512], BF16)
    CONTRIB = sb("CONTRIB", [128, 4 * 1024], BF16)
    SRWc = sb("SRWc", [1, 3 * HT], F32)
    SCL = sb("SCL", [128, 128], F32)
    TAU = sb("TAU", [128, 8], F32)
    CMP = sb("CMP", [128, 128], F32)
    MSKC = sb("MSKC", [128, 128], F32)
    RNK = sb("RNK", [128, 128], F32)
    VALS = sb("VALS", [128, 256], F32)
    RIDXI = sb("RIDXI", [128, 128], I32)
    TOKI = sb("TOKI", [128, 16], I32)
    TOKF = sb("TOKF", [128, 32], F32)
    GCOL = sb("GCOL", [128, 16], F32)
    TOKV = sb("TOKV", [128, 128], I32)
    TRIU = sb("TRIU", [128, 128], F32)
    B1C = sb("B1C", [128, 32], F32)
    B2R = sb("B2R", [1, C], F32)
    ZB = sb("ZB", [128, 1024], F32)
    SC1c = sb("SC1c", [128, 2048], F32)
    ONES1c = sb("ONES1c", [1, 128], F32)
    ONESCc = sb("ONESCc", [128, 1], F32)
    LNG = sb("LNG", [128, 2 * NCH], F32)
    IDFc = sb("IDFc", [128, 128], F32)

    PSC = [ps(f"PCC{i}", [128, 512], F32) for i in range(8)]

    LX = sem("LX")
    LW = sem("LW")
    EV = sem("EV")
    EA = sem("EA")
    MMG = sem("MMG")
    ST = sem("ST")
    VE = sem("VE")
    GD = sem("GD")
    BK = [sem(f"BKC{i}") for i in range(8)]
    LP = [sem(f"LPC{i}") for i in range(4)]
    LQ = [sem(f"LQC{i}") for i in range(8)]

    SUMS = MEAN = MR = SRWc[0:1, 0 * HT:1 * HT]
    SSQ = VAR = SRWc[0:1, 1 * HT:2 * HT]
    RSCR = RSTD = SRWc[0:1, 2 * HT:3 * HT]

    def slab_ap(dram, r0, nr, c0, ncols, row_len):
        t = dram.tensor
        off = dram.offset + r0 * row_len + c0
        return AP(t, off, [[row_len, 128], [128 * row_len, max(nr // 128, 1)],
                           [1, ncols]])

    def ws_dst(col0, nkc, ncols):
        full = WSC[:, :]
        return AP(full.tensor, full.offset + col0,
                  [[8192, 128], [ncols, nkc], [1, ncols]])

    if True:

        if not fused:
            nc.const_aps.register(F32, LNEPS, CONSTc[0:1, 0:1])

        def edge(eng, inst):
            VE.inc(inst)
            VE.wait(eng)

        if fused:
            @block.tensor
            def _(t):
                t.wait_ge(JN.h, 4)

            @block.vector
            def _(v):
                v.wait_ge(JN.h, 4)

            @block.scalar
            def _(s):
                s.wait_ge(JN.h, 4)

            @block.sync
            def _(sy):
                sy.wait_ge(JN.h, 4)

        @block.gpsimd
        def _(g):
            if fused:
                g.wait_ge(JN.h, 4)
            g.memset(ONES1c[:, :], 1.0)
            g.memset(ONESCc[:, :], 1.0)
            g.memset(ZB[:, :], 0.0)
            if not fused:
                g.memset(CONSTc[0:1, 0:1], LNEPS)
            g.memset(TAU[:, 0:1], 0.0)
            g.memset(TAU[:, 1:2], 1.01)
            g.memset(TAU[:, 2:3], 0.505)
            LW.inc(g.iota(TOKV[:, :], pattern=[[1, 128]], base=0,
                          channel_multiplier=128))
            g.wait_ge(LW.h, 1)
            LW.inc(g.memset(IDFc[:, :], 1.0))
            g.wait_ge(LW.h, 2)
            LW.inc(g.affine_select(IDFc[:, :], IDFc[:, :], pattern=[[1, 128]],
                                   compare_op=ALU.is_equal, fill=0.0,
                                   channel_multiplier=-1))
            LP[0].inc(g.dma_start(out=ws_dst(0, NCH, C),
                                  in_=slab_ap(wo, 0, C, 0, C, C)), 16)

        @block.sync
        def _(sy):
            LX.inc(sy.dma_start(out=TRIU[:, :], in_=triu[:, :]), 16)
            LX.inc(sy.dma_start(out=LNG[:, :], in_=lngb[:, :]), 16)
            LX.inc(sy.dma_start(out=B1C[:, :], in_=b1e[:, :]), 16)
            LX.inc(sy.dma_start(out=B2R[0:1, :], in_=b2e[0:1, :]), 16)
            LX.inc(sy.dma_start(
                out=AP(SCL[:, :].tensor, SCL[:, :].offset,
                       [[128, 128], [1, 128]]),
                in_=scrow[0:1, :]), 16)
            if fused:
                LX.inc(sy.dma_start(out=YIR[:, :], in_=yidxr[:, :]), 16)

        for h in range(2):
            t0 = h * HT

            if not fused:
                @block.sync
                def _(sy, h=h, t0=t0):
                    EV.wait(sy)
                    EA.wait(sy)
                    for kc in range(NCH):
                        LX.inc(sy.dma_start(out=YF[:, HT * kc:HT * (kc + 1)],
                                            in_=y_in[kc, :, t0:t0 + HT]), 16)
            else:
                @block.gpsimd
                def _(g, h=h):
                    EV.wait(g)
                    EA.wait(g)
                    if h == 0:
                        g.wait_ge(BARC.h, 1)
                        g.wait_ge(LX.h, 96)
                    for kc in range(NCH):
                        LX.inc(g.indirect_dma_start(
                            out=YF[:, HT * kc:HT * (kc + 1)], out_offset=None,
                            in_=ysh[:, :],
                            in_offset=IndirectOffsetOnAxis(
                                YIR[:, 2 * kc + h:2 * kc + h + 1], 0)), 16)
            lx_t = LX.n

            for tt in range(2):
                for grp in range(2):
                    @block.scalar
                    def _(s, tt=tt, grp=grp, lx_t=lx_t):
                        s.wait_ge(LX.h, lx_t)
                        EV.wait(s)
                        if grp == 1 or tt == 1:
                            MMG.wait(s)
                        for kk in range(4):
                            kc = grp * 4 + kk
                            ins = s.activation(
                                SC1c[:, 512 * kk:512 * (kk + 1)],
                                YF[:, HT * kc + 512 * tt:HT * kc + 512 * tt + 512],
                                AF.Square)
                        EA.inc(ins)

                    @block.tensor
                    def _(t, tt=tt, grp=grp):
                        EA.wait(t)
                        if tt + grp == 0:
                            BK[4].wait(t)
                            BK[5].wait(t)
                        for kk in range(4):
                            kc = grp * 4 + kk
                            t.matmul(PSC[4][0:1, 0:512], ONESCc[:, 0:1],
                                     YF[:, HT * kc + 512 * tt:
                                        HT * kc + 512 * tt + 512],
                                     start=(kc == 0), stop=(kc == 7))
                            ins = t.matmul(PSC[5][0:1, 0:512], ONESCc[:, 0:1],
                                           SC1c[:, 512 * kk:512 * (kk + 1)],
                                           start=(kc == 0), stop=(kc == 7))
                        MMG.inc(ins)

                @block.vector
                def _(v, tt=tt):
                    MMG.wait(v)
                    v.tensor_copy(SUMS[0:1, 512 * tt:512 * (tt + 1)],
                                  PSC[4][0:1, 0:512])
                    ins = v.tensor_copy(SSQ[0:1, 512 * tt:512 * (tt + 1)],
                                        PSC[5][0:1, 0:512])
                    BK[4].inc(ins)
                    BK[5].inc(v.engine_nop())

            @block.vector
            def _(v):
                EV.wait(v)
                MMG.wait(v)
                edge(v, v.tensor_scalar_mul(MEAN, SUMS, 1.0 / C))
                edge(v, v.tensor_scalar_mul(VAR, SSQ, 1.0 / C))
                edge(v, v.tensor_mul(RSCR, MEAN, MEAN))
                EV.inc(v.tensor_sub(VAR, VAR, RSCR))

            @block.scalar
            def _(s):
                EV.wait(s)
                s.wait_ge(LW.h, 3)
                edge(s, s.activation(RSCR, VAR, AF.Ln, bias=LNEPS))
                EA.inc(s.activation(RSTD, RSCR, AF.Exp, scale=-0.5))

            @block.vector
            def _(v):
                EA.wait(v)
                EV.inc(v.tensor_mul(MR, MEAN, RSTD))

            for tt in range(2):
                @block.tensor
                def _(t, tt=tt):
                    EV.wait(t)
                    BK[4].wait(t)
                    BK[5].wait(t)
                    t.matmul(PSC[4][:, 0:512], ONES1c[0:1, :],
                             RSTD[0:1, 512 * tt:512 * (tt + 1)],
                             start=True, stop=True)
                    MMG.inc(t.matmul(PSC[5][:, 0:512], ONES1c[0:1, :],
                                     MR[0:1, 512 * tt:512 * (tt + 1)],
                                     start=True, stop=True))

                @block.vector
                def _(v, tt=tt):
                    MMG.wait(v)
                    v.wait_ge(LX.h, 80)
                    for kc in range(NCH):
                        sl = slice(HT * kc + 512 * tt, HT * kc + 512 * tt + 512)
                        v.tensor_mul(SC1c[:, 0:512], YF[:, sl], PSC[4][:, 0:512])
                        v.tensor_sub(SC1c[:, 512:1024], SC1c[:, 0:512],
                                     PSC[5][:, 0:512])
                        v.tensor_scalar(SC1c[:, 0:512], SC1c[:, 512:1024],
                                        LNG[:, kc:kc + 1], None, op0=ALU.mult)
                        ins = v.tensor_scalar(YLN[:, sl], SC1c[:, 0:512],
                                              LNG[:, NCH + kc:NCH + kc + 1],
                                              None, op0=ALU.add)
                    BK[4].inc(ins)
                    BK[5].inc(v.engine_nop())
                    EV.inc(v.engine_nop())

            for ds in range(NCH):
                for tt in range(2):
                    gi = 2 * ds + tt
                    b = gi % 2

                    @block.tensor
                    def _(t, ds=ds, tt=tt, b=b, gi=gi):
                        if gi == 0:
                            EV.wait(t)
                            LP[0].wait(t)
                        BK[b].wait(t)
                        for kc in range(NCH):
                            ins = t.matmul(
                                PSC[b][:, 0:512],
                                WSC[:, C * kc + 128 * ds:C * kc + 128 * (ds + 1)],
                                YLN[:, HT * kc + 512 * tt:HT * kc + 512 * tt + 512],
                                start=(kc == 0), stop=(kc == 7))
                        MMG.inc(ins)

                    @block.vector
                    def _(v, ds=ds, tt=tt, b=b):
                        MMG.wait(v)
                        if ds == 0 and tt == 0:
                            ST.wait(v)
                        ins = v.tensor_copy(
                            S1[:, HT * ds + 512 * tt:HT * ds + 512 * (tt + 1)],
                            PSC[b][:, 0:512])
                        BK[b].inc(ins)
                        if ds == NCH - 1 and tt == 1:
                            EV.inc(v.engine_nop())

            @block.sync
            def _(sy, h=h, t0=t0):
                EV.wait(sy)
                for kc in range(NCH):
                    ST.inc(sy.dma_start(out=s1_out[kc, :, t0:t0 + HT],
                                        in_=S1[:, HT * kc:HT * (kc + 1)]), 16)

        if prezero is None:
            @block.gpsimd
            def _(g):
                g.wait_ge(LW.h, 1)
                for r in range(32):
                    GD.inc(g.dma_start(
                        out=co_out[512 * r:512 * (r + 1), :],
                        in_=AP(ZB[:, :].tensor, ZB[:, :].offset,
                               [[1024, 128], [0, 4], [1, 1024]])), 16)
        gd_zero = GD.n

        for it in range(38):
            @block.vector
            def _(v, it=it):
                if it == 0:
                    v.wait_ge(LW.h, 3)
                    LX.wait(v)
                edge(v, v.tensor_scalar(CMP[:, :], SCL[:, :], TAU[:, 2:3],
                                        None, op0=ALU.is_ge))
                EV.inc(v.reduce_sum(TAU[:, 4:5], CMP[:, :], axis=AX.X))

            @block.tensor
            def _(t, it=it):
                EV.wait(t)
                BK[2].wait(t)
                MMG.inc(t.matmul(PSC[2][0:1, 0:1], TAU[:, 4:5], ONESCc[:, 0:1],
                                 start=True, stop=True))

            @block.vector
            def _(v, it=it):
                MMG.wait(v)
                ins = v.tensor_copy(TAU[0:1, 5:6], PSC[2][0:1, 0:1])
                BK[2].inc(ins)
                edge(v, v.engine_nop())

            @block.tensor
            def _(t, it=it):
                VE.wait(t)
                BK[3].wait(t)
                MMG.inc(t.matmul(PSC[3][:, 0:1], ONES1c[0:1, :], TAU[0:1, 5:6],
                                 start=True, stop=True))

            @block.vector
            def _(v, it=it):
                MMG.wait(v)
                ins = v.tensor_scalar(TAU[:, 3:4], PSC[3][:, 0:1], 2048.0, None,
                                      op0=ALU.is_ge)
                BK[3].inc(ins)
                edge(v, v.engine_nop())
                edge(v, v.tensor_sub(TAU[:, 6:7], TAU[:, 2:3], TAU[:, 0:1]))
                edge(v, v.tensor_mul(TAU[:, 6:7], TAU[:, 6:7], TAU[:, 3:4]))
                edge(v, v.tensor_add(TAU[:, 0:1], TAU[:, 0:1], TAU[:, 6:7]))
                edge(v, v.tensor_sub(TAU[:, 6:7], TAU[:, 1:2], TAU[:, 2:3]))
                edge(v, v.tensor_mul(TAU[:, 6:7], TAU[:, 6:7], TAU[:, 3:4]))
                edge(v, v.tensor_add(TAU[:, 1:2], TAU[:, 2:3], TAU[:, 6:7]))
                edge(v, v.tensor_add(TAU[:, 2:3], TAU[:, 0:1], TAU[:, 1:2]))
                edge(v, v.tensor_scalar_mul(TAU[:, 2:3], TAU[:, 2:3], 0.5))
                EV.inc(v.engine_nop())

        @block.vector
        def _(v):
            EV.wait(v)
            edge(v, v.tensor_scalar(MSKC[:, :], SCL[:, :], TAU[:, 0:1], None,
                                    op0=ALU.is_ge))
            edge(v, v.tensor_tensor_scan(RNK[:, :], MSKC[:, :], MSKC[:, :],
                                         0.0, ALU.add, ALU.bypass))
            edge(v, v.tensor_sub(RNK[:, :], RNK[:, :], MSKC[:, :]))
            EV.inc(v.reduce_sum(TAU[:, 7:8], MSKC[:, :], axis=AX.X))

        @block.tensor
        def _(t):
            EV.wait(t)
            t.wait_ge(LX.h, 80)
            BK[2].wait(t)
            MMG.inc(t.matmul(PSC[2][:, 0:1], TRIU[:, :], TAU[:, 7:8],
                             start=True, stop=True))

        @block.vector
        def _(v):
            MMG.wait(v)
            ins = v.tensor_copy(TAU[:, 6:7], PSC[2][:, 0:1])
            BK[2].inc(ins)
            edge(v, v.engine_nop())
            edge(v, v.tensor_scalar(RNK[:, :], RNK[:, :], TAU[:, 6:7], None,
                                    op0=ALU.add))
            edge(v, v.tensor_mul(RNK[:, :], RNK[:, :], MSKC[:, :]))
            edge(v, v.tensor_scalar(CMP[:, :], MSKC[:, :], -300000.0, 300000.0,
                                    op0=ALU.mult, op1=ALU.add))
            edge(v, v.tensor_add(RNK[:, :], RNK[:, :], CMP[:, :]))
            edge(v, v.tensor_copy(RIDXI[:, :], RNK[:, :]))
            edge(v, v.tensor_copy(
                AP(VALS[:, :].tensor, VALS[:, :].offset, [[256, 128], [2, 128]]),
                TOKV[:, :]))
            EV.inc(v.tensor_copy(
                AP(VALS[:, :].tensor, VALS[:, :].offset + 1, [[256, 128], [2, 128]]),
                SCL[:, :]))

        @block.gpsimd
        def _(g):
            EV.wait(g)
            if prezero is None:
                g.wait_ge(GD.h, gd_zero)
            else:
                g.wait_ge(prezero[0].h, prezero[1])
            rbc = g.alloc_register("rbc")
            g.reg_mov(rbc, 2047)
            for f in range(128):
                GD.inc(g.indirect_dma_start(
                    out=idx_d[:, :],
                    out_offset=IndirectOffsetOnAxis(RIDXI[:, f:f + 1], 0),
                    in_=VALS[:, 2 * f:2 * f + 2], in_offset=None,
                    bounds_check=rbc, oob_is_err=False), 16)
            g.wait_ge(GD.h, GD.n)
            GD.inc(g.dma_start(
                out=TOKF[:, :],
                in_=AP(idx_d.tensor, idx_d.offset, [[32, 128], [1, 32]])), 16)

        @block.vector
        def _(v):
            GD.wait(v)
            edge(v, v.tensor_copy(
                TOKI[:, :],
                AP(TOKF[:, :].tensor, TOKF[:, :].offset, [[32, 128], [2, 16]])))
            EV.inc(v.tensor_copy(
                GCOL[:, :],
                AP(TOKF[:, :].tensor, TOKF[:, :].offset + 1, [[32, 128], [2, 16]])))

        ev_g0 = EV.n
        for g16 in range(16):
            @block.gpsimd
            def _(g, g16=g16):
                if g16 == 0:
                    EV.wait(g)
                elif g16 >= 2:
                    g.wait_ge(EV.h, ev_g0 + g16 - 1)
                LP[1 + g16 % 2].inc(g.indirect_dma_start(
                    out=XGR[:, 1024 * (g16 % 2):1024 * (g16 % 2) + 1024],
                    out_offset=None,
                    in_=xn1f[:, :],
                    in_offset=IndirectOffsetOnAxis(TOKI[:, g16:g16 + 1], 0)), 16)

            @block.vector
            def _(v, g16=g16):
                LP[1 + g16 % 2].wait(v)
                if g16 >= 2:
                    MMG.wait(v, MMG.n - 1)
                EV.inc(v.tensor_copy(XGF[:, 1024 * (g16 % 2):
                                         1024 * (g16 % 2) + 1024],
                                     XGR[:, 1024 * (g16 % 2):
                                         1024 * (g16 % 2) + 1024]))

            @block.tensor
            def _(t, g16=g16):
                t.wait_ge(EV.h, ev_g0 + g16 + 1)
                BK[2].wait(t)
                BK[3].wait(t)
                sl0 = 1024 * (g16 % 2)
                for kc in range(NCH):
                    b = 2 + kc // 4
                    ins = t.transpose(
                        PSC[b][:, 128 * (kc % 4):128 * (kc % 4) + 128],
                        XGF[:, sl0 + 128 * kc:sl0 + 128 * (kc + 1)],
                        IDFc[:, :])
                MMG.inc(ins)

            @block.vector
            def _(v, g16=g16):
                MMG.wait(v)
                for kc in range(NCH):
                    b = 2 + kc // 4
                    ins = v.tensor_copy(
                        XG[:, P * kc + 128 * g16:P * kc + 128 * (g16 + 1)],
                        PSC[b][:, 128 * (kc % 4):128 * (kc % 4) + 128])
                BK[2].inc(ins)
                BK[3].inc(v.engine_nop())

        w1_free = {}
        w2_free = {}
        for ps_ in range(4):
            tok0 = 512 * ps_

            for ds32 in range(32):
                wslot = 1024 * (ds32 % 4)

                @block.gpsimd
                def _(g, ps_=ps_, ds32=ds32, wslot=wslot):
                    g.wait_ge(MMG.h, w1_free.get((ps_, ds32), MMG.n - 1))
                    LQ[ds32 % 4].inc(
                        g.dma_start(out=ws_dst(wslot, NCH, 128),
                                    in_=slab_ap(w1e, 0, C, 128 * ds32, 128, 4096)),
                        16)

                @block.tensor
                def _(t, ps_=ps_, ds32=ds32, wslot=wslot):
                    LQ[ds32 % 4].wait(t)
                    if ps_ == 0 and ds32 == 0:
                        BK[2].wait(t)
                        BK[3].wait(t)
                    b = ds32 % 2
                    BK[b].wait(t)
                    for kc in range(NCH):
                        ins = t.matmul(
                            PSC[b][:, 0:512],
                            WSC[:, wslot + 128 * kc:wslot + 128 * (kc + 1)],
                            XG[:, P * kc + tok0:P * kc + tok0 + 512],
                            start=(kc == 0), stop=(kc == 7))
                    MMG.inc(ins)
                nxt = (ps_, ds32 + 4) if ds32 + 4 < 32 else (ps_ + 1, ds32 % 4)
                w1_free[nxt] = MMG.n

                @block.scalar
                def _(s, ds32=ds32):
                    MMG.wait(s)
                    s.wait_ge(LX.h, 80)
                    b = ds32 % 2
                    ins = s.activation(H1[:, 512 * ds32:512 * (ds32 + 1)],
                                       PSC[b][:, 0:512], AF.Relu,
                                       bias=B1C[:, ds32:ds32 + 1])
                    BK[b].inc(ins)
                    if ds32 == 31:
                        EA.inc(s.nop())

            @block.tensor
            def _(t, ps_=ps_):
                EA.wait(t)
                t.wait_ge(LX.h, 80)
                for b in range(8):
                    BK[b].wait(t)
                for tt4 in range(4):
                    for nhh in range(2):
                        b = 2 * tt4 + nhh
                        ins = t.matmul(PSC[b][:, 0:512],
                                       ONES1c[0:1, :],
                                       B2R[0:1, 512 * nhh:512 * (nhh + 1)],
                                       start=True, stop=False)
                MMG.inc(ins)

            for kc32 in range(32):
                wslot = 4096 + 1024 * (kc32 % 4)

                @block.gpsimd
                def _(g, ps_=ps_, kc32=kc32, wslot=wslot):
                    g.wait_ge(MMG.h, w2_free.get((ps_, kc32), MMG.n - 1))
                    LQ[4 + kc32 % 4].inc(
                        g.dma_start(out=ws_dst(wslot, 1, C),
                                    in_=slab_ap(w2e, 128 * kc32, 128, 0, C, C)),
                        16)

                @block.tensor
                def _(t, ps_=ps_, kc32=kc32, wslot=wslot):
                    LQ[4 + kc32 % 4].wait(t)
                    for tt4 in range(4):
                        for nhh in range(2):
                            b = 2 * tt4 + nhh
                            ins = t.matmul(
                                PSC[b][:, 0:512],
                                H1[:, 512 * kc32 + 128 * tt4:
                                   512 * kc32 + 128 * (tt4 + 1)],
                                WSC[:, wslot + 512 * nhh:wslot + 512 * (nhh + 1)],
                                start=False, stop=(kc32 == 31))
                    MMG.inc(ins)
                nxt = (ps_, kc32 + 4) if kc32 + 4 < 32 else (ps_ + 1, kc32 % 4)
                w2_free[nxt] = MMG.n

            for tt4 in range(4):
                @block.vector
                def _(v, ps_=ps_, tt4=tt4):
                    MMG.wait(v)
                    if ps_ > 0 and tt4 == 0:
                        GD.wait(v)
                    for nhh in range(2):
                        b = 2 * tt4 + nhh
                        ins = v.tensor_scalar(
                            CONTRIB[:, 1024 * tt4 + 512 * nhh:
                                    1024 * tt4 + 512 * (nhh + 1)],
                            PSC[b][:, 0:512],
                            GCOL[:, 4 * ps_ + tt4:4 * ps_ + tt4 + 1],
                            None, op0=ALU.mult)
                        BK[b].inc(ins)
                    EV.inc(v.engine_nop())

                @block.gpsimd
                def _(g, ps_=ps_, tt4=tt4):
                    EV.wait(g)
                    GD.inc(g.indirect_dma_start(
                        out=co_out[:, :],
                        out_offset=IndirectOffsetOnAxis(
                            TOKI[:, 4 * ps_ + tt4:4 * ps_ + tt4 + 1], 0),
                        in_=CONTRIB[:, 1024 * tt4:1024 * (tt4 + 1)],
                        in_offset=None), 16)

        @block.gpsimd
        def _(g):
            GD.wait(g)

        @block.sync
        def _(sy):
            ST.wait(sy)

    return {"ST": ST, "GD": GD}



def build_launch_BC():
    nc = bass.Bass("TRN2", target_bir_lowering=False)
    drB = declare_B_drams(nc)
    drC = declare_C_drams(nc)
    yidxw = nc.dram_tensor("yidxw", [128, 16], I32, kind="ExternalInput").ap()
    yidxr = nc.dram_tensor("yidxr", [128, 16], I32, kind="ExternalInput").ap()
    fidx = nc.dram_tensor("fidx", [128, 1], I32, kind="ExternalInput").ap()
    sidx = nc.dram_tensor("sidx", [128, 1], I32, kind="ExternalInput").ap()
    fval = nc.dram_tensor("fval", [128, 4], I32, kind="ExternalInput").ap()
    rid = nc.dram_tensor("rid", [1, 4], I32, kind="ExternalInput").ap()
    ysh = nc.dram_tensor("ysh", [NCH * 128 * 4, 1024], F32,
                         addr_space="Shared").ap()
    flgsh = nc.dram_tensor("flgsh", [256, 4], I32, addr_space="Shared").ap()
    zcon = nc.dram_tensor("zcon", [512, C], BF16).ap()

    alP = Alloc(nc)
    JN = alP.sem("JN")
    BARC = alP.sem("BARC")
    FW = alP.sem("FWb")
    PSEM = alP.sem("PSb")
    YIW = alP.sb("YIW", [128, 16], I32)
    FID = alP.sb("FID", [128, 1], I32)
    SID = alP.sb("SID", [128, 1], I32)
    FVAL = alP.sb("FVAL", [128, 4], I32)
    RV = alP.sb("RV", [1, 4], I32)
    FLG = alP.sb("FLG", [128, 4], I32)
    GDZ = alP.sem("GDZ")
    ZBP = alP.sb("ZBP", [128, 1024], F32)

    with nc.Block() as block:
        @block.gpsimd
        def _(g):
            GDZ.inc(g.memset(ZBP[:, :], 0.0))
            g.wait_ge(GDZ.h, 1)
            GDZ.inc(g.dma_start(
                out=zcon[:, :],
                in_=AP(ZBP[:, :].tensor, ZBP[:, :].offset,
                       [[1024, 128], [0, 4], [1, 1024]])), 16)
            FW.inc(g.dma_start(out=YIW[:, :], in_=yidxw[:, :]), 16)
            FW.inc(g.dma_start(out=FID[:, :], in_=fidx[:, :]), 16)
            FW.inc(g.dma_start(out=SID[:, :], in_=sidx[:, :]), 16)
            FW.inc(g.dma_start(out=FVAL[:, :], in_=fval[:, :]), 16)
            FW.inc(g.dma_start(out=RV[0:1, :], in_=rid[0:1, :]), 16)
            FW.wait(g)

        @block.scalar
        def _(s):
            s.wait_ge(GDZ.h, 17)
            co_out_bc = drC["co_out"]
            for r in range(32):
                GDZ.inc(s.dma_start(out=co_out_bc[512 * r:512 * (r + 1), :],
                                    in_=zcon[:, :]), 16)

        alB = Alloc(nc)
        SB = emit_B_part(nc, block, alB, alP, drB, ("shared", ysh, YIW))

        @block.tensor
        def _(t):
            JN.inc(t.nop())

        @block.vector
        def _(v):
            JN.inc(v.engine_nop())

        @block.scalar
        def _(s):
            JN.inc(s.nop())

        @block.sync
        def _(sy):
            JN.inc(sy.nop())

        alB.close()

        @block.gpsimd
        def _(g):
            SB["ST"].wait(g)
            FW.inc(g.indirect_dma_start(
                out=flgsh[:, :],
                out_offset=IndirectOffsetOnAxis(FID[:, 0:1], 0),
                in_=FVAL[:, :], in_offset=None), 16)
            FW.wait(g)
            with (
                g.register("cnt") as cnt,
                g.register("go") as go,
                g.register("val") as val,
                g.register("rr") as rr,
            ):
                g.reg_mov(cnt, 0)
                g.reg_mov(go, 1)
                g.reg_load(rr, RV[0:1, 0:1])
                with g.While(go):
                    g.reg_add(cnt, cnt, 16)
                    g.indirect_dma_start(
                        out=FLG[:, :], out_offset=None,
                        in_=flgsh[:, :],
                        in_offset=IndirectOffsetOnAxis(SID[:, 0:1], 0)
                        ).then_inc(PSEM.h, 16)
                    g.wait_ge(PSEM.h, cnt)
                    g.reg_load(val, FLG[0:1, 0:1])
                    g.reg_alu(go, val, rr, ALU.not_equal)
            BARC.inc(g.nop())

        alC = Alloc(nc)
        emit_C_part(nc, block, alC, alP, drC,
                    ("shared", ysh, BARC, JN, yidxr),
                    prezero=(GDZ, 17 + 32 * 16))
        alC.close()
    alP.close()
    return nc


_BC_RUN_COUNTER = [int(time.time() * 1000.0) & 0x3FFFFFFF]


def prep_BC_inputs(inp, x1_full, xn1_bf, scores, core, runid):
    hg = core % 2
    w = prep_B_inputs(inp, x1_full, core)
    w.update(prep_C_inputs(inp, None, xn1_bf, scores, core))
    p = np.arange(128)
    yidxw = np.zeros((128, 16), np.int32)
    yidxr = np.zeros((128, 16), np.int32)
    for j4 in range(4):
        for tb in range(4):
            yidxw[:, 4 * j4 + tb] = ((4 * hg + j4) * 128 + p) * 4 + tb
    for kc in range(NCH):
        for h in range(2):
            yidxr[:, 2 * kc + h] = (kc * 128 + p) * 4 + 2 * hg + h
    w["yidxw"] = yidxw
    w["yidxr"] = yidxr
    w["fidx"] = (hg * 128 + p).reshape(128, 1).astype(np.int32)
    w["sidx"] = ((1 - hg) * 128 + p).reshape(128, 1).astype(np.int32)
    w["fval"] = np.full((128, 4), runid, np.int32)
    w["rid"] = np.full((1, 4), runid, np.int32)
    return w


def run_launch_BC(inp, x1_full, xn1_bf, scores):
    _BC_RUN_COUNTER[0] += 1
    runid = _BC_RUN_COUNTER[0]
    nc = build_launch_BC()
    in_maps = [prep_BC_inputs(inp, x1_full, xn1_bf, scores, c, runid)
               for c in range(N_CORES)]
    res = run_bass_kernel_spmd(nc, in_maps, core_ids=list(range(N_CORES)))
    s1 = np.empty((16384, C), np.float32)
    co_list = []
    for c in range(N_CORES):
        r = res.results[c]
        s1[P * c:P * (c + 1)] = r["s1_out"].reshape(C, P).T
        co_list.append(r["co_out"])
    return s1, co_list


def prep_C_inputs(inp, y_full, xn1_bf, scores, core):
    e = core
    w = {}
    if y_full is not None:
        w["y_in"] = np.ascontiguousarray(
            y_full[P * e:P * (e + 1)].T.reshape(NCH, 128, P).astype(np.float32))
    lngb = np.zeros((128, 2 * NCH), np.float32)
    lngb[:, 0:NCH] = np.asarray(inp["L1_attn_ln_g"],
                                np.float32).reshape(NCH, 128).T
    lngb[:, NCH:2 * NCH] = np.asarray(inp["L1_attn_ln_b"],
                                      np.float32).reshape(NCH, 128).T
    w["lngb"] = lngb
    w["wo"] = np.ascontiguousarray(np.asarray(inp["L1_attn_Wo"], np.float32))
    w["scrow"] = np.ascontiguousarray(
        scores[:, e].reshape(1, 16384).astype(np.float32))
    w["xn1f"] = np.ascontiguousarray(xn1_bf)
    w["w1e"] = np.ascontiguousarray(np.asarray(inp["L1_ffn_w1"], np.float32)[e])
    w["b1e"] = np.ascontiguousarray(
        np.asarray(inp["L1_ffn_b1"], np.float32)[e].reshape(32, 128).T)
    w["w2e"] = np.ascontiguousarray(np.asarray(inp["L1_ffn_w2"], np.float32)[e])
    w["b2e"] = np.ascontiguousarray(
        np.asarray(inp["L1_ffn_b2"], np.float32)[e].reshape(1, C))
    w["triu"] = np.ascontiguousarray(np.triu(np.ones((128, 128), np.float32), 1))
    return w


def run_launch_C(inp, y_full, xn1_bf, scores):
    nc = build_launch_C()
    in_maps = [prep_C_inputs(inp, y_full, xn1_bf, scores, c)
               for c in range(N_CORES)]
    res = run_bass_kernel_spmd(nc, in_maps, core_ids=list(range(N_CORES)))
    s1 = np.empty((16384, C), np.float32)
    co_list = []
    for c in range(N_CORES):
        r = res.results[c]
        s1[P * c:P * (c + 1)] = r["s1_out"].reshape(C, P).T
        co_list.append(r["co_out"])
    return s1, co_list



def build_launch_D():
    nc = bass.Bass("TRN2", target_bir_lowering=False)

    x1d = nc.dram_tensor("x1d", [P, C], F32, kind="ExternalInput").ap()
    s1d = nc.dram_tensor("s1d", [P, C], F32, kind="ExternalInput").ap()
    cod = nc.dram_tensor("cod", [8, P, C], BF16, kind="ExternalInput").ap()
    outd = nc.dram_tensor("outd", [P, C], F32, kind="ExternalOutput").ap()

    al = Alloc(nc)
    sb, sem = al.sb, al.sem
    CO = [sb(f"CO{sl}", [128, 8 * C], BF16) for sl in range(2)]
    X1T = [sb(f"X1T{sl}", [128, C], F32) for sl in range(2)]
    S1T = [sb(f"S1T{sl}", [128, C], F32) for sl in range(2)]
    ACC = [sb(f"ACC{sl}", [128, C], F32) for sl in range(2)]
    GT = [sb(f"GT{sl}", [128, C], F32) for sl in range(2)]
    OUT = [sb(f"OUT{sl}", [128, C], F32) for sl in range(2)]
    LX = sem("LXd")
    EV = sem("EVd")
    EA = sem("EAd")
    ST = sem("STd")

    with nc.Block() as block:
        for t in range(16):
            r0 = 128 * t
            sl = t % 2

            @block.sync
            def _(sy, t=t, r0=r0, sl=sl):
                sy.wait_ge(EV.h, max(0, 2 * (t - 1)))
                for e in range(8):
                    LX.inc(sy.dma_start(out=CO[sl][:, C * e:C * (e + 1)],
                                        in_=cod[e, r0:r0 + 128, :]), 16)
                LX.inc(sy.dma_start(out=X1T[sl][:, :],
                                    in_=x1d[r0:r0 + 128, :]), 16)
                LX.inc(sy.dma_start(out=S1T[sl][:, :],
                                    in_=s1d[r0:r0 + 128, :]), 16)

            @block.vector
            def _(v, t=t, sl=sl):
                LX.wait(v)
                v.tensor_add(ACC[sl][:, :], CO[sl][:, 0:C], CO[sl][:, C:2 * C])
                for e in range(2, 8):
                    ins = v.tensor_add(ACC[sl][:, :], ACC[sl][:, :],
                                       CO[sl][:, C * e:C * (e + 1)])
                EV.inc(ins)

            @block.scalar
            def _(s, t=t, sl=sl):
                EV.wait(s)
                EA.inc(s.activation(GT[sl][:, :], ACC[sl][:, :], AF.Silu))

            @block.vector
            def _(v, t=t, sl=sl):
                EA.wait(v)
                v.wait_ge(ST.h, 16 * max(0, t - 1))
                v.tensor_mul(OUT[sl][:, :], S1T[sl][:, :], GT[sl][:, :])
                EV.inc(v.tensor_add(OUT[sl][:, :], OUT[sl][:, :],
                                    X1T[sl][:, :]))

            @block.sync
            def _(sy, t=t, r0=r0, sl=sl):
                EV.wait(sy)
                ST.inc(sy.dma_start(out=outd[r0:r0 + 128, :],
                                    in_=OUT[sl][:, :]), 16)

        @block.sync
        def _(sy):
            ST.wait(sy)

    al.close()
    return nc


def run_launch_D(x1_full, s1_full, co_list):
    nc = build_launch_D()
    in_maps = []
    for d in range(N_CORES):
        sl = slice(P * d, P * (d + 1))
        in_maps.append({
            "x1d": np.ascontiguousarray(x1_full[sl]),
            "s1d": np.ascontiguousarray(s1_full[sl]),
            "cod": np.ascontiguousarray(
                np.stack([co[sl] for co in co_list], axis=0)),
        })
    res = run_bass_kernel_spmd(nc, in_maps, core_ids=list(range(N_CORES)))
    out = np.empty((16384, C), np.float32)
    for d in range(N_CORES):
        out[P * d:P * (d + 1)] = res.results[d]["outd"]
    return out


def kernel(**inputs):
    x1, xn1, scores = run_launch_A(inputs)
    s1, co_list = run_launch_BC(inputs, x1, xn1, scores)
    out = run_launch_D(x1, s1, co_list)
    return out.reshape(4, 4096, C)

